# version 15
# baseline (speedup 1.0000x reference)
import numpy as np
from contextlib import ExitStack
import ml_dtypes

import concourse.bass as bass
import concourse.mybir as mybir
from concourse.bass_utils import run_bass_kernel_spmd

F32 = mybir.dt.float32
BF16 = mybir.dt.bfloat16
AF = mybir.ActivationFunctionType
ALU = mybir.AluOpType
AX = mybir.AxisListType

NEG_BIG = -30000.0


class Cfg:
    def __init__(self, seq=8192, qpc=2048, n_cores=8):
        self.D = 1024
        self.DH = 4096
        self.DC = 8
        self.HC = 32
        self.seq = seq
        self.qpc = qpc
        self.NKT = seq // 1024
        self.NQT = qpc // 128
        self.TO = 512
        self.NFT = qpc // 512
        self.n_cores = n_cores
        self.cps = n_cores // 2
        self.G = 8 // self.cps
        self.slots = [[p for p in range(self.NQT) if p // self.G >= kt]
                      for kt in range(self.NKT)]
        self.NVIS = sum(len(s) for s in self.slots)


CFG_FULL = Cfg(8192, 2048, 8)
CFG_MINI = Cfg(4096, 2048, 4)


def build_program(cfg: Cfg):
    D, DH, DC, HC = cfg.D, cfg.DH, cfg.DC, cfg.HC
    NKT, NQT, TO, NFT = cfg.NKT, cfg.NQT, cfg.TO, cfg.NFT
    QPC, SEQ = cfg.qpc, cfg.seq
    NVIS = cfg.NVIS
    SLOTS = cfg.slots

    nc = bass.Bass("TRN2", target_bir_lowering=False, debug=False)

    dt = {}
    dt["xfm"] = nc.dram_tensor("xfm", [D, SEQ], F32, kind="ExternalInput").ap()
    dt["xq"] = nc.dram_tensor("xq", [D, QPC], F32, kind="ExternalInput").ap()
    dt["xtm"] = nc.dram_tensor("xtm", [QPC, D], F32, kind="ExternalInput").ap()
    dt["wattT"] = nc.dram_tensor("wattT", [D, 2 * D], F32, kind="ExternalInput").ap()
    dt["wattK"] = nc.dram_tensor("wattK", [D, D], F32, kind="ExternalInput").ap()
    dt["xraw"] = nc.dram_tensor("xraw", [SEQ, D], F32, kind="ExternalInput").ap()
    dt["w1tp"] = nc.dram_tensor("w1tp", [HC * 128, DC * 128], F32, kind="ExternalInput").ap()
    dt["w2tp"] = nc.dram_tensor("w2tp", [HC * 128, DC * 128], F32, kind="ExternalInput").ap()
    dt["h1tp"] = nc.dram_tensor("h1tp", [HC * 128, HC * 128], F32, kind="ExternalInput").ap()
    dt["h2tp"] = nc.dram_tensor("h2tp", [HC * 128, HC * 128], F32, kind="ExternalInput").ap()
    dt["w3tp"] = nc.dram_tensor("w3tp", [2 * HC * 128, 512], F32, kind="ExternalInput").ap()
    for s in ("s1", "s2", "s1h", "s2h"):
        dt[s] = nc.dram_tensor(s, [128, HC], F32, kind="ExternalInput").ap()
    dt["s3b"] = nc.dram_tensor("s3b", [128, D], F32, kind="ExternalInput").ap()
    dt["theta"] = nc.dram_tensor("theta", [128, NVIS], F32, kind="ExternalInput").ap()
    dt["uio"] = nc.dram_tensor("uio", [128, 1024], mybir.dt.float16, kind="ExternalInput").ap()
    dt["identb"] = nc.dram_tensor("identb", [128, 128], BF16, kind="ExternalInput").ap()
    dt["out"] = nc.dram_tensor("out", [QPC, D], F32, kind="ExternalOutput").ap()
    dt["w1s"] = nc.dram_tensor("w1s", [HC * 128, DC * 128], BF16, kind="Internal").ap()
    dt["w2s"] = nc.dram_tensor("w2s", [HC * 128, DC * 128], BF16, kind="Internal").ap()
    dt["h1s"] = nc.dram_tensor("h1s", [HC * 128, HC * 128], BF16, kind="Internal").ap()
    dt["h2s"] = nc.dram_tensor("h2s", [HC * 128, HC * 128], BF16, kind="Internal").ap()
    dt["w3s"] = nc.dram_tensor("w3s", [2 * HC * 128, 512], BF16, kind="Internal").ap()

    sem_names = ["ts", "vs", "ss", "gs",
                 "d_const", "g_xk", "g_q", "d_xtm0", "d_xtm1", "d_out",
                 "g_wt", "g_wk", "g_xt", "g_cw1", "g_cw2", "g_cw3", "g_ch1", "g_ch2",
                 "d_w0", "d_w1", "d_w2", "d_w3"]
    es_top = ExitStack()
    SEM = {n: es_top.enter_context(nc.semaphore(n)) for n in sem_names}

    sb = {}

    def sbuf(name, shape, dtp, stack):
        sb[name] = stack.enter_context(nc.sbuf_tensor("sb_" + name, shape, dtp))
        return sb[name]

    sbuf("O", [128, NQT * 1024], F32, es_top)
    sbuf("uio", [128, 1024], mybir.dt.float16, es_top)
    sbuf("theta", [128, NVIS], F32, es_top)
    for s in ("s1", "s2", "s1h", "s2h"):
        sbuf(s, [128, HC], F32, es_top)
    sbuf("s3b", [128, D], F32, es_top)
    sbuf("identb", [128, 128], BF16, es_top)
    sbuf("negm", [128, NQT], F32, es_top)
    sbuf("mpos", [128, NQT], F32, es_top)
    sbuf("lrun", [128, NQT], F32, es_top)
    sbuf("alpha", [128, 4], F32, es_top)
    sbuf("lpart", [128, 4], F32, es_top)
    for s in ("nmp", "rh", "s1v", "s2v", "nmean", "mm2", "tv", "epsv"):
        sbuf(s, [128, 1], F32, es_top)
    sbuf("bst", [128, 12], F32, es_top)
    sbuf("bmv", [128, 2], F32, es_top)
    sbuf("msk", [128, 1024], BF16, es_top)
    sbuf("late8", [128, 2048], F32, es_top)

    streams = {"sync": [], "tensor": [], "vector": [], "scalar": [], "gpsimd": []}
    cnt = {"ts": 0, "vs": 0, "ss": 0, "gs": 0}
    dcnt = {n: 0 for n in sem_names if n not in ("ts", "vs", "ss", "gs")}

    SELF = {"tensor": "ts", "vector": "vs", "scalar": "ss", "gpsimd": "gs"}

    def comp(eng, build_fn, waits=()):
        s = SELF[eng]
        cnt[s] += 1
        my = cnt[s]
        w = tuple((a, b) for (a, b) in waits if b > 0)

        def f(e, build_fn=build_fn, w=w, s=s):
            for sname, v in w:
                e.wait_ge(SEM[sname], v)
            build_fn(e).then_inc(SEM[s], 1)
        streams[eng].append(f)
        return my

    def dma(eng, build_fn, sem_name, waits=()):
        dcnt[sem_name] += 16
        my = dcnt[sem_name]
        w = tuple((a, b) for (a, b) in waits if b > 0)

        def f(e, build_fn=build_fn, w=w, sem_name=sem_name):
            for sname, v in w:
                e.wait_ge(SEM[sname], v)
            build_fn(e).then_inc(SEM[sem_name], 16)
        streams[eng].append(f)
        return my

    def wait(eng, waits):
        w = tuple((a, b) for (a, b) in waits if b > 0)

        def f(e, w=w):
            for sname, v in w:
                e.wait_ge(SEM[sname], v)
        streams[eng].append(f)

    for name in ("uio", "theta", "s1", "s2", "s1h", "s2h", "s3b", "identb"):
        dma("sync", (lambda e, n=name: e.dma_start(out=sb[n][:], in_=dt[n])), "d_const")
    C_CONST = dcnt["d_const"]

    comp("vector", lambda e: e.memset(sb["epsv"][:], 1e-5))

    es_wt = ExitStack()
    sbuf("WTv", [128, DC * 1024], BF16, es_wt)
    es_a = ExitStack()
    sbuf("QP", [128, DC * QPC], BF16, es_a)
    sbuf("xk0", [128, DC * 1024], BF16, es_a)
    sbuf("XT", [128, DC * 1024], BF16, es_a)
    sbuf("P0", [128, 1024], BF16, es_a)
    sbuf("P1", [128, 1024], BF16, es_a)
    sbuf("PT0", [128, 1024], BF16, es_a)
    sbuf("PT1", [128, 1024], BF16, es_a)
    es_q = ExitStack()
    sbuf("Q", [128, DC * 1024], BF16, es_q)
    sbuf("WK", [128, DC * D], BF16, es_q)
    ps = {}
    ps["S0"] = es_a.enter_context(nc.psum_tensor("S0", [128, 1024], F32))
    ps["S1"] = es_a.enter_context(nc.psum_tensor("S1", [128, 1024], F32))
    ps["PV"] = es_a.enter_context(nc.psum_tensor("PV", [128, 1024], F32))
    ps["PT0"] = es_a.enter_context(nc.psum_tensor("PT0p", [128, 1024], BF16))
    ps["PT1"] = es_a.enter_context(nc.psum_tensor("PT1p", [128, 1024], BF16))

    for ic in range(DC):
        dma("gpsimd", (lambda e, ic=ic: e.dma_start(
            out=sb["WTv"][:, ic * 1024:(ic + 1) * 1024],
            in_=dt["wattT"][ic * 128:(ic + 1) * 128, D:2 * D])), "g_wt")
    G_WT = dcnt["g_wt"]
    for dc in range(DC):
        dma("gpsimd", (lambda e, dc=dc: e.dma_start(
            out=sb["WK"][:, dc * D:(dc + 1) * D],
            in_=dt["wattK"][dc * 128:(dc + 1) * 128, :])), "g_wk")
    G_WK = dcnt["g_wk"]
    for ic in range(DC):
        dma("gpsimd", (lambda e, ic=ic: e.dma_start(
            out=sb["Q"][:, ic * 1024:(ic + 1) * 1024],
            in_=dt["xq"][ic * 128:(ic + 1) * 128, 0:1024])), "g_q")
    G_Q = dcnt["g_q"]

    cast_chunks = []

    def cast_rows(src, dst, rows, cols, sem):
        step = max(128, (2 * 1024 * 1024) // (cols * 4))
        n = 0
        for r0 in range(0, rows, step):
            r1 = min(rows, r0 + step)
            cast_chunks.append((src, dst, r0, r1, sem))
            n += 16
        return n

    G_CW1 = cast_rows("w1tp", "w1s", HC * 128, DC * 128, "g_cw1")
    G_CW2 = cast_rows("w2tp", "w2s", HC * 128, DC * 128, "g_cw2")
    G_CW3 = cast_rows("w3tp", "w3s", 2 * HC * 128, 512, "g_cw3")
    G_CH1 = cast_rows("h1tp", "h1s", HC * 128, HC * 128, "g_ch1")
    G_CH2 = cast_rows("h2tp", "h2s", HC * 128, HC * 128, "g_ch2")
    G_GATE = {"w1s": ("g_cw1", G_CW1), "w2s": ("g_cw2", G_CW2),
              "w3s": ("g_cw3", G_CW3),
              "h1s": ("g_ch1", G_CH1), "h2s": ("g_ch2", G_CH2)}
    _cast_i = [0]

    def emit_cast_chunks(n):
        bound = min(len(cast_chunks), _cast_i[0] + n)
        while _cast_i[0] < bound:
            s_, d_, r0, r1, sem = cast_chunks[_cast_i[0]]
            dma("gpsimd", (lambda e, r0=r0, r1=r1, s_=s_, d_=d_: e.dma_start(
                out=dt[d_][r0:r1, :], in_=dt[s_][r0:r1, :])), sem)
            _cast_i[0] += 1

    last_pv_mm = 0
    pair_free = {"S0": [], "S1": [], "PV": []}
    pt_free = {0: 0, 1: 0}
    p_free = {0: 0, 1: 0}
    xk_cast_done = 0
    xt_cast_done = 0
    xk_war = {0: 0, 1: 0}

    vS, vMadd, vNegm, vExp, vAlpha, vPT, vPTe, vPV, vO, vL = ({} for _ in range(10))
    _vctr = [0]
    _vmap = {}

    def vid(kt, qi):
        key = (kt, qi)
        if key not in _vmap:
            _vmap[key] = _vctr[0]
            _vctr[0] += 1
        return _vmap[key]

    def xk_load_cast(kt, first):
        nonlocal xk_cast_done
        buf = "xk%d" % (kt % 2)
        for ic in range(DC):
            wts = []
            if ic == 0 and not first:
                wts.append(("ts", xk_war[kt % 2]))
            if ic == 0 and kt == 1:
                wts.append(("ts", qp_last_mm))
            xk_cast_done = dma("gpsimd", (lambda e, ic=ic, kt=kt, buf=buf: e.dma_start(
                out=sb[buf][:, ic * 1024:(ic + 1) * 1024],
                in_=dt["xfm"][ic * 128:(ic + 1) * 128, kt * 1024:(kt + 1) * 1024])),
                "g_xk", waits=wts)

    def xt_load_cast(kt, first):
        nonlocal xt_cast_done
        for kc in range(DC):
            wts = []
            if kc == 0 and not first:
                wts.append(("ts", last_pv_mm))
            xt_cast_done = dma("gpsimd", (lambda e, kc=kc, kt=kt: e.dma_start(
                out=sb["XT"][:, kc * 1024:(kc + 1) * 1024],
                in_=dt["xraw"][kt * 1024 + kc * 128: kt * 1024 + (kc + 1) * 128, :])),
                "g_xt", waits=wts)

    def emit_qprime():
        pairs = ["S0", "S1"]
        ev = 0
        g_q_half = G_Q
        for half in range(QPC // 1024):
            if half > 0:
                last_mm_prev = cnt["ts"]
                for ic in range(DC):
                    g_q_half = dma("gpsimd", (lambda e, ic=ic, half=half: e.dma_start(
                        out=sb["Q"][:, ic * 1024:(ic + 1) * 1024],
                        in_=dt["xq"][ic * 128:(ic + 1) * 128,
                                     half * 1024:(half + 1) * 1024])),
                        "g_q", waits=[("ts", last_mm_prev)] if ic == 0 else [])
            for j in range(DC * 2):
                ib, nt = j // 2, j % 2
                pr = pairs[j % 2]
                wts = list(pair_free[pr])
                if j == 0:
                    wts += [("g_wk", G_WK), ("g_q", g_q_half)]
                last = 0
                for dc in range(DC):
                    fn = (lambda e, pr=pr, dc=dc, ib=ib, nt=nt: e.matmul(
                        ps[pr][:, 0:512],
                        sb["WK"][:, dc * D + ib * 128: dc * D + ib * 128 + 128],
                        sb["Q"][:, dc * 1024 + nt * 512: dc * 1024 + (nt + 1) * 512],
                        start=(dc == 0), stop=(dc == DC - 1)))
                    last = comp("tensor", fn, wts)
                    wts = ()
                ev = comp("scalar", (lambda e, pr=pr, ib=ib, nt=nt, half=half: e.copy(
                    out=sb["QP"][:, ib * QPC + half * 1024 + nt * 512:
                                 ib * QPC + half * 1024 + (nt + 1) * 512],
                    in_=ps[pr][:, 0:512])), [("ts", last)])
                pair_free[pr] = [("ss", ev)]
        return ev

    def emit_S(kt, qi, first_of_kt):
        v = vid(kt, qi)
        buf = "xk%d" % (kt % 2)
        pr = "S0" if qi % 2 == 0 else "S1"
        wts = list(pair_free[pr])
        if first_of_kt:
            wts.append(("g_xk", xk_cast_done))
        if kt == 0 and qi == SLOTS[0][0]:
            wts += [("ss", QP_DONE), ("d_const", C_CONST)]
        lastmm = 0
        for ns in range(2):
            for ic in range(DC):
                fn = (lambda e, pr=pr, ns=ns, ic=ic, qi=qi, buf=buf: e.matmul(
                    ps[pr][:, ns * 512:(ns + 1) * 512],
                    sb["QP"][:, ic * QPC + qi * 128: ic * QPC + qi * 128 + 128],
                    sb[buf][:, ic * 1024 + ns * 512: ic * 1024 + (ns + 1) * 512],
                    start=(ic == 0), stop=(ic == DC - 1)))
                lastmm = comp("tensor", fn, wts)
                wts = ()
        vS[v] = lastmm
        xk_war[kt % 2] = lastmm
        pair_free[pr] = []

    def emit_mask(kt, qi):
        v = vid(kt, qi)
        pr = "S0" if qi % 2 == 0 else "S1"
        c1 = comp("vector", (lambda e, v=v: e.tensor_scalar(
            out=sb["msk"][:], in0=sb["uio"][:],
            scalar1=sb["theta"][:, v:v + 1], scalar2=NEG_BIG,
            op0=ALU.is_gt, op1=ALU.mult)),
            [("vs", cnt["vs"]), ("d_const", C_CONST if v == 0 else 0)])
        c2 = comp("vector", (lambda e, pr=pr: e.tensor_tensor(
            out=ps[pr][:, :], in0=ps[pr][:, :], in1=sb["msk"][:], op=ALU.add)),
            [("ts", vS[v]), ("vs", c1)])
        vMadd[v] = c2

    def emit_negm(kt, qi):
        v = vid(kt, qi)
        pr = "S0" if qi % 2 == 0 else "S1"
        c1 = comp("vector", (lambda e, pr=pr: e.reduce_max(
            sb["nmp"][:, 0:1], ps[pr][:, :], axis=AX.X, negate=True)),
            [("vs", vMadd[v])])
        if kt == 0:
            c2 = comp("vector", (lambda e, qi=qi: e.tensor_copy(
                out=sb["negm"][:, qi:qi + 1], in_=sb["nmp"][:, 0:1])), [("vs", c1)])
        else:
            c2 = comp("vector", (lambda e, qi=qi: e.tensor_tensor(
                out=sb["negm"][:, qi:qi + 1], in0=sb["negm"][:, qi:qi + 1],
                in1=sb["nmp"][:, 0:1], op=ALU.min)), [("vs", c1)])
        vNegm[v] = c2

    def emit_exp(kt, qi):
        v = vid(kt, qi)
        pr = "S0" if qi % 2 == 0 else "S1"
        par = qi % 2
        sl = v % 4
        wts = [("vs", vNegm[v]), ("vs", vL.get(v - 4, 0))]
        if kt > 0:
            ca = comp("scalar", (lambda e, qi=qi, sl=sl: e.activation(
                out=sb["alpha"][:, sl:sl + 1], in_=sb["mpos"][:, qi:qi + 1],
                func=AF.Exp, bias=sb["negm"][:, qi:qi + 1])), wts)
            vAlpha[v] = ca
            wts = [("ss", ca)]
        else:
            vAlpha[v] = 0
        ce = comp("scalar", (lambda e, pr=pr, par=par, qi=qi, sl=sl: e.activation(
            out=sb["P%d" % par][:], in_=ps[pr][:, :], func=AF.Exp,
            bias=sb["negm"][:, qi:qi + 1], accum_out=sb["lpart"][:, sl:sl + 1])),
            [("ts", p_free[par])] + wts)
        cm = comp("scalar", (lambda e, qi=qi: e.activation(
            out=sb["mpos"][:, qi:qi + 1], in_=sb["negm"][:, qi:qi + 1],
            func=AF.Copy, scale=-1.0)), [("ss", ce)])
        vExp[v] = ce
        pair_free[pr] = [("ss", ce)]

    def emit_PT(kt, qi):
        v = vid(kt, qi)
        par = qi % 2
        wts = [("ss", vExp[v]), ("ss", pt_free[par])]
        last = 0
        for kc in range(DC):
            fn = (lambda e, par=par, kc=kc: e.transpose(
                ps["PT%d" % par][:, kc * 128:(kc + 1) * 128],
                sb["P%d" % par][:, kc * 128:(kc + 1) * 128],
                sb["identb"][:]))
            last = comp("tensor", fn, wts)
            wts = ()
        vPT[v] = last
        p_free[par] = last

    def emit_PTevac(kt, qi):
        v = vid(kt, qi)
        par = qi % 2
        c1 = comp("scalar", (lambda e, par=par: e.copy(
            out=sb["PT%d" % par][:], in_=ps["PT%d" % par][:, :])),
            [("ts", vPT[v])])
        vPTe[v] = c1
        pt_free[par] = c1

    def emit_PV(kt, qi, first_of_kt):
        nonlocal last_pv_mm
        v = vid(kt, qi)
        par = qi % 2
        wts = [("ss", vPTe[v])] + pair_free["PV"]
        if first_of_kt:
            wts.append(("g_xt", xt_cast_done))
        last = 0
        for ns in range(2):
            for kc in range(DC):
                fn = (lambda e, par=par, ns=ns, kc=kc: e.matmul(
                    ps["PV"][:, ns * 512:(ns + 1) * 512],
                    sb["PT%d" % par][:, kc * 128:(kc + 1) * 128],
                    sb["XT"][:, kc * 1024 + ns * 512: kc * 1024 + (ns + 1) * 512],
                    start=(kc == 0), stop=(kc == DC - 1)))
                last = comp("tensor", fn, wts)
                wts = ()
        vPV[v] = last
        last_pv_mm = last
        pair_free["PV"] = []

    def emit_O(kt, qi):
        v = vid(kt, qi)
        sl = v % 4
        if kt == 0:
            c1 = comp("vector", (lambda e, qi=qi: e.tensor_copy(
                out=sb["O"][:, qi * 1024:(qi + 1) * 1024], in_=ps["PV"][:, :])),
                [("ts", vPV[v])])
            cl = comp("vector", (lambda e, qi=qi, sl=sl: e.tensor_copy(
                out=sb["lrun"][:, qi:qi + 1], in_=sb["lpart"][:, sl:sl + 1])),
                [("ss", vExp[v])])
        else:
            c1 = comp("vector", (lambda e, qi=qi, sl=sl: e.scalar_tensor_tensor(
                out=sb["O"][:, qi * 1024:(qi + 1) * 1024],
                in0=sb["O"][:, qi * 1024:(qi + 1) * 1024],
                scalar=sb["alpha"][:, sl:sl + 1], in1=ps["PV"][:, :],
                op0=ALU.mult, op1=ALU.add)),
                [("ts", vPV[v]), ("ss", vAlpha[v])])
            cl = comp("vector", (lambda e, qi=qi, sl=sl: e.scalar_tensor_tensor(
                out=sb["lrun"][:, qi:qi + 1], in0=sb["lrun"][:, qi:qi + 1],
                scalar=sb["alpha"][:, sl:sl + 1], in1=sb["lpart"][:, sl:sl + 1],
                op0=ALU.mult, op1=ALU.add)), [("ss", vExp[v])])
        vO[v] = c1
        vL[v] = cl
        pair_free["PV"] = [("vs", c1)]

    n_chunks = len(cast_chunks)
    xk_load_cast(0, first=True)
    xt_load_cast(0, first=True)
    QP_DONE = emit_qprime()
    qp_last_mm = cnt["ts"]
    es_q.close()
    sbuf("xk1", [128, DC * 1024], BF16, es_a)
    vis_i = 0
    want_prev = 0
    for kt in range(NKT):
        if kt + 1 < NKT:
            xk_load_cast(kt + 1, first=False)
        sl = SLOTS[kt]
        nv = len(sl)
        for k in range(nv + 2):
            if k < nv:
                want = min(n_chunks, int(n_chunks * (vis_i + 1) / max(1, 0.85 * NVIS)))
                emit_cast_chunks(want - want_prev)
                want_prev = want
                vis_i += 1
                emit_S(kt, sl[k], first_of_kt=(k == 0))
                emit_mask(kt, sl[k])
                emit_negm(kt, sl[k])
            if 1 <= k <= nv:
                emit_exp(kt, sl[k - 1])
                emit_PT(kt, sl[k - 1])
            if 2 <= k <= nv + 1:
                emit_PTevac(kt, sl[k - 2])
                emit_PV(kt, sl[k - 2], first_of_kt=(k == 2))
                emit_O(kt, sl[k - 2])
        if kt + 1 < NKT:
            xt_load_cast(kt + 1, first=False)

    emit_cast_chunks(len(cast_chunks))

    INV_D = 1.0 / D
    xtm_slot_free = [0, 0]
    x1_ready_v = {}
    ep_s = 0
    ep_v = 0

    def emit_ep(qi):
        nonlocal ep_s, ep_v
        slot = qi % 2
        ld = dma("gpsimd", (lambda e, qi=qi, slot=slot: e.dma_start(
            out=sb["late8"][:, slot * 1024:(slot + 1) * 1024],
            in_=dt["xtm"][qi * 128:(qi + 1) * 128, :])),
            "d_xtm%d" % slot, waits=[("vs", xtm_slot_free[slot])])
        lastkt = qi // cfg.G
        par = 0
        cr0 = comp("vector", (lambda e, qi=qi: e.tensor_scalar_mul(
            out=sb["rh"][:, 0:1], in0=sb["lrun"][:, qi:qi + 1], scalar1=2.0)),
            [("vs", vL[vid(lastkt, qi)]), ("ss", ep_s), ("vs", cnt["vs"])])
        cr = comp("vector", (lambda e: e.reciprocal(
            out=sb["rh"][:, 0:1], in_=sb["rh"][:, 0:1])), [("vs", cr0)])
        cnorm = comp("vector", (lambda e, qi=qi, par=par: e.tensor_scalar_mul(
            out=sb["Ob%d" % par][:], in0=sb["O"][:, qi * 1024:(qi + 1) * 1024],
            scalar1=sb["rh"][:, 0:1])),
            [("vs", cr), ("vs", vO[vid(lastkt, qi)]), ("ts", ep_state["ob_free"][par])])
        wts = [("vs", cnorm), ("ss", ep_state["pte_evac"])]
        last = 0
        for ic in range(DC):
            fn = (lambda e, par=par, ic=ic: e.transpose(
                ps["PTE"][:, ic * 128:(ic + 1) * 128],
                sb["Ob%d" % par][:, ic * 128:(ic + 1) * 128],
                sb["identb"][:]))
            last = comp("tensor", fn, wts)
            wts = ()
        ep_state["ob_free"][par] = last
        pev = comp("scalar", (lambda e, par=par: e.copy(
            out=sb["Ofm%d" % par][:], in_=ps["PTE"][:, :])),
            [("ts", last), ("ts", ep_state["ofm_readers"][par])])
        ep_state["pte_evac"] = pev
        pr = ["F0", "F1", "F2"][rot["i"] % 3]
        rot["i"] += 1
        wts = [("ss", pev)] + pfb[pr + "a"] + pfb[pr + "b"]
        if qi == 0:
            wts.append(("g_wt", G_WT))
        lastp = 0
        for ns in range(2):
            for ic in range(DC):
                fn = (lambda e, pr=pr, par=par, ns=ns, ic=ic: e.matmul(
                    ps[pr][:, ns * 512:(ns + 1) * 512],
                    sb["Ofm%d" % par][:, ic * 128:(ic + 1) * 128],
                    sb["WTv"][:, ic * 1024 + ns * 512: ic * 1024 + (ns + 1) * 512],
                    start=(ic == 0), stop=(ic == DC - 1)))
                lastp = comp("tensor", fn, wts)
                wts = ()
        ep_state["ofm_readers"][par] = lastp
        cy = comp("vector", (lambda e, qi=qi, slot=slot, pr=pr: e.tensor_tensor(
            out=sb["O"][:, qi * 1024:(qi + 1) * 1024],
            in0=ps[pr][:, 0:1024],
            in1=sb["late8"][:, slot * 1024:(slot + 1) * 1024], op=ALU.add)),
            [("ts", lastp), ("d_xtm%d" % slot, ld), ("vs", cnorm)])
        pfb[pr + "a"] = [("vs", cy)]
        pfb[pr + "b"] = [("vs", cy)]
        xtm_slot_free[slot] = cy
        cb1 = comp("vector", (lambda e, qi=qi: e.bn_stats(
            sb["bst"][:, 0:6], sb["O"][:, qi * 1024: qi * 1024 + 512])),
            [("vs", cy), ("ss", ep_s), ("vs", cnt["vs"])])
        cb2 = comp("vector", (lambda e, qi=qi: e.bn_stats(
            sb["bst"][:, 6:12], sb["O"][:, qi * 1024 + 512: qi * 1024 + 1024])),
            [("vs", cb1)])
        cag = comp("vector", (lambda e: e.bn_aggr(
            sb["bmv"][:, 0:2], sb["bst"][:, 0:12])), [("vs", cb2)])
        crs0 = comp("scalar", (lambda e: e.activation(
            out=sb["mm2"][:, 0:1], in_=sb["bmv"][:, 1:2], func=AF.Sqrt,
            bias=sb["epsv"][:, 0:1])), [("vs", cag), ("ss", ep_s)])
        crs = comp("vector", (lambda e: e.reciprocal(
            out=sb["tv"][:, 0:1], in_=sb["mm2"][:, 0:1])),
            [("ss", crs0), ("vs", cnt["vs"])])
        cx1 = comp("vector", (lambda e, qi=qi: e.tensor_scalar(
            out=sb["O"][:, qi * 1024:(qi + 1) * 1024],
            in0=sb["O"][:, qi * 1024:(qi + 1) * 1024],
            scalar1=sb["bmv"][:, 0:1], scalar2=sb["tv"][:, 0:1],
            op0=ALU.subtract, op1=ALU.mult)), [("vs", crs)])
        ep_s = crs0
        ep_v = cx1
        x1_ready_v[qi] = cx1

    T_ts, T_vs, T_ss = cnt["ts"], cnt["vs"], cnt["ss"]
    SS_ATTN = cnt["ss"]
    es_a.close()

    es_b = ExitStack()
    sbuf("A", [128, HC * TO], BF16, es_b)
    sbuf("Bb", [128, HC * TO], BF16, es_b)
    sbuf("x1f", [128, DC * TO], BF16, es_b)
    NWB = 3
    for i in range(NWB):
        sbuf("wb%d" % i, [128, HC * 128], BF16, es_b)
    sbuf("xc0", [128, 1024], BF16, es_b)
    sbuf("xc1", [128, 1024], BF16, es_b)
    sbuf("ea", [128, TO], F32, es_b)
    sbuf("Ob0", [128, 1024], BF16, es_b)
    sbuf("Ofm0", [128, 1024], BF16, es_b)
    ps["F0"] = es_b.enter_context(nc.psum_tensor("F0", [128, 1024], F32))
    ps["F1"] = es_b.enter_context(nc.psum_tensor("F1", [128, 1024], F32))
    ps["F2"] = es_b.enter_context(nc.psum_tensor("F2", [128, 1024], F32))
    ps["PTF"] = es_b.enter_context(nc.psum_tensor("PTF", [128, 1024], BF16))
    ps["PTE"] = es_b.enter_context(nc.psum_tensor("PTE", [128, 1024], BF16))
    rot = {"i": 0}
    ep_state = {"ob_free": {0: 0, 1: 0}, "ofm_readers": {0: 0, 1: 0}, "pte_evac": 0}


    wslot_free = [0, 0, 0, 0]
    ring = 0

    def wload(src_dram, oc, ncols, waits_extra=()):
        nonlocal ring
        slot = ring % NWB
        ring += 1
        sem = "d_w%d" % slot
        w = [("ts", wslot_free[slot])] + list(waits_extra)
        if src_dram in G_GATE:
            w.append(G_GATE[src_dram])
        my = dma("sync", (lambda e, src_dram=src_dram, oc=oc, ncols=ncols,
                          slot=slot: e.dma_start(
            out=sb["wb%d" % slot][:, 0:ncols],
            in_=dt[src_dram][oc * 128:(oc + 1) * 128, 0:ncols])), sem, waits=w)
        return slot, my

    def w3load(j, waits_extra=()):
        nonlocal ring
        slot = ring % NWB
        ring += 1
        sem = "d_w%d" % slot
        w = [("ts", wslot_free[slot]), G_GATE["w3s"]] + list(waits_extra)
        my = dma("sync", (lambda e, j=j, slot=slot: e.dma_start(
            out=sb["wb%d" % slot][:, 0:512],
            in_=dt["w3s"][j * 128:(j + 1) * 128, :])), sem, waits=w)
        return slot, my

    pfb = {k: [] for k in ("F0a", "F0b", "F1a", "F1b", "F2a", "F2b", "PTF")}
    BANK = {
        "F0a": lambda: ps["F0"][:, 0:512], "F0b": lambda: ps["F0"][:, 512:1024],
        "F1a": lambda: ps["F1"][:, 0:512], "F1b": lambda: ps["F1"][:, 512:1024],
        "F2a": lambda: ps["F2"][:, 0:512], "F2b": lambda: ps["F2"][:, 512:1024],
    }

    x1f_readers = 0
    A_readers = 0
    B_readers = 0
    ea_free = 0
    za_free = 0
    out_dmas = 0

    def emit_x1cast(tau):
        ev = 0
        for tb in range(4):
            xcn = "xc0" if tb % 2 == 0 else "xc1"
            qt = 4 * tau + tb
            cv = comp("vector", (lambda e, qt=qt, xcn=xcn: e.tensor_copy(
                out=sb[xcn][:], in_=sb["O"][:, qt * 1024:(qt + 1) * 1024])),
                [("ts", p_free.get("xc" + str(tb % 2), 0)),
                 ("vs", x1_ready_v[qt])])
            wts = [("vs", cv), ("ss", ev if (tau > 0 or tb > 0) else SS_ATTN)]
            last = 0
            for dc in range(DC):
                fn = (lambda e, dc=dc, xcn=xcn: e.transpose(
                    ps["PTF"][:, dc * 128:(dc + 1) * 128],
                    sb[xcn][:, dc * 128:(dc + 1) * 128],
                    sb["identb"][:]))
                last = comp("tensor", fn, wts)
                wts = ()
            p_free["xc" + str(tb % 2)] = last
            ev = comp("scalar", (lambda e, tb=tb: e.copy(
                out=sb["x1f"][:].rearrange("p (n t) -> p n t", t=TO)[:, :, tb * 128:(tb + 1) * 128],
                in_=ps["PTF"][:].rearrange("p (n t) -> p n t", t=128))),
                [("ts", last), ("ts", x1f_readers)])
        return ev

    for qi in range(4):
        emit_ep(qi)
    wait("sync", [("ts", T_ts)])
    next_x1f_evac = emit_x1cast(0)
    _ep_next = [4]
    for tau in range(NFT):
        x1f_evac = next_x1f_evac
        uA_done = 0
        for oc in range(HC):
            pr = ["F0", "F1", "F2"][rot["i"] % 3]
            rot["i"] += 1
            slot1, l1 = wload("w1s", oc, DC * 128)
            slot2, l2 = wload("w2s", oc, DC * 128)
            if tau == 0:
                want_ep = 4 + ((oc + 1) * (NQT - 4)) // HC
                while _ep_next[0] < min(NQT, want_ep):
                    emit_ep(_ep_next[0])
                    _ep_next[0] += 1
            wts = [("d_w%d" % slot1, l1), ("ss", x1f_evac)] + pfb[pr + "a"] + pfb[pr + "b"]
            lasta = lastb = 0
            for ic in range(DC):
                fn = (lambda e, pr=pr, ic=ic, slot=slot1: e.matmul(
                    ps[pr][:, 0:512],
                    sb["wb%d" % slot][:, ic * 128:(ic + 1) * 128],
                    sb["x1f"][:, ic * TO:(ic + 1) * TO],
                    start=(ic == 0), stop=(ic == DC - 1)))
                lasta = comp("tensor", fn, wts)
                wts = ()
            wts = [("d_w%d" % slot2, l2)]
            for ic in range(DC):
                fn = (lambda e, pr=pr, ic=ic, slot=slot2: e.matmul(
                    ps[pr][:, 512:1024],
                    sb["wb%d" % slot][:, ic * 128:(ic + 1) * 128],
                    sb["x1f"][:, ic * TO:(ic + 1) * TO],
                    start=(ic == 0), stop=(ic == DC - 1)))
                lastb = comp("tensor", fn, wts)
                wts = ()
            wslot_free[slot1] = lasta
            wslot_free[slot2] = lastb
            x1f_readers = lastb
            ca = comp("scalar", (lambda e, pr=pr, oc=oc: e.activation(
                out=sb["ea"][:], in_=ps[pr][:, 0:512], func=AF.Silu,
                scale=sb["s1"][:, oc:oc + 1])), [("ts", lasta), ("vs", ea_free)])
            cu = comp("vector", (lambda e, pr=pr, oc=oc: e.scalar_tensor_tensor(
                out=sb["A"][:, oc * TO:(oc + 1) * TO], in0=ps[pr][:, 512:1024],
                scalar=sb["s2"][:, oc:oc + 1], in1=sb["ea"][:],
                op0=ALU.mult, op1=ALU.mult)),
                [("ts", lastb), ("ss", ca), ("ts", A_readers if oc == 0 else 0)])
            ea_free = cu
            pfb[pr + "a"] = [("ss", ca)]
            pfb[pr + "b"] = [("vs", cu)]
            uA_done = cu

        def h_stage(src_sb, dst_sb, wname, srt, src_ready, dst_readers):
            halves = ["F2a", "F2b", "F0a", "F0b"]
            u_done = 0
            for oc in range(HC):
                bk = halves[oc % 4]
                slot, ld = wload(wname, oc, HC * 128)
                wts = [("d_w%d" % slot, ld)] + ([src_ready] if oc == 0 else []) + pfb[bk]
                last = 0
                for ic in range(HC):
                    fn = (lambda e, bk=bk, ic=ic, slot=slot, src_sb=src_sb: e.matmul(
                        BANK[bk](),
                        sb["wb%d" % slot][:, ic * 128:(ic + 1) * 128],
                        sb[src_sb][:, ic * TO:(ic + 1) * TO],
                        start=(ic == 0), stop=(ic == HC - 1)))
                    last = comp("tensor", fn, wts)
                    wts = ()
                wslot_free[slot] = last
                cu = comp("scalar", (lambda e, bk=bk, oc=oc, srt=srt, dst_sb=dst_sb: e.activation(
                    out=sb[dst_sb][:, oc * TO:(oc + 1) * TO], in_=BANK[bk](),
                    func=AF.Gelu, scale=sb[srt][:, oc:oc + 1])),
                    [("ts", last), ("ts", dst_readers if oc == 0 else 0)])
                pfb[bk] = [("ss", cu)]
                u_done = cu
            return u_done

        if tau == 0:
            while _ep_next[0] < NQT:
                emit_ep(_ep_next[0])
                _ep_next[0] += 1
        uB_done = h_stage("A", "Bb", "h1s", "s1h", ("vs", uA_done), B_readers)
        A_readers_save = cnt["ts"]
        uA2_done = h_stage("Bb", "A", "h2s", "s2h", ("ss", uB_done), A_readers_save)
        B_readers = cnt["ts"]
        A_readers = 0
        if tau + 1 < NFT:
            next_x1f_evac = emit_x1cast(tau + 1)

        z_done = {}
        for oh in range(2):
            FT = ["F0a", "F0b", "F1a", "F1b"] if oh == 0 else ["F0a", "F0b", "F2a", "F2b"]
            wts_f = []
            for tb in range(4):
                wts_f += pfb[FT[tb]]
            lastmm = {tb: 0 for tb in range(4)}
            for ic in range(HC):
                slot, ld = w3load(oh * HC + ic)
                wts = [("d_w%d" % slot, ld), ("ss", uA2_done if (oh == 0 and ic == 0) else 0)] + \
                    (wts_f if ic == 0 else [])
                for tb in range(4):
                    fn = (lambda e, bk=FT[tb], ic=ic, slot=slot, tb=tb, tau=tau: e.matmul(
                        BANK[bk](),
                        sb["A"][:, ic * TO + tb * 128: ic * TO + tb * 128 + 128],
                        sb["wb%d" % slot][:, 0:512],
                        start=(ic == 0), stop=(ic == HC - 1)))
                    lastmm[tb] = comp("tensor", fn, wts)
                    wts = ()
                wslot_free[slot] = lastmm[3]
            A_readers = lastmm[3]
            for tb in range(4):
                qt = 4 * tau + tb
                czm = comp("vector", (lambda e, bk=FT[tb], oh=oh: e.tensor_tensor(
                    out=BANK[bk](), in0=BANK[bk](),
                    in1=sb["s3b"][:, oh * 512:(oh + 1) * 512],
                    op=ALU.mult)), [("ts", lastmm[tb])])
                cz = comp("vector", (lambda e, bk=FT[tb], qt=qt, oh=oh: e.tensor_tensor(
                    out=sb["O"][:, qt * 1024 + oh * 512: qt * 1024 + (oh + 1) * 512],
                    in0=BANK[bk](),
                    in1=sb["O"][:, qt * 1024 + oh * 512: qt * 1024 + (oh + 1) * 512],
                    op=ALU.add)), [("vs", czm)])
                pfb[FT[tb]] = [("vs", cz)]
                z_done[(tb, oh)] = cz

        ln_done = []
        for tb in range(4):
            qt = 4 * tau + tb
            cb1 = comp("vector", (lambda e, qt=qt: e.bn_stats(
                sb["bst"][:, 0:6], sb["O"][:, qt * 1024: qt * 1024 + 512])),
                [("vs", z_done[(tb, 0)]), ("vs", z_done[(tb, 1)]),
                 ("ss", ep_s), ("vs", cnt["vs"])])
            cb2 = comp("vector", (lambda e, qt=qt: e.bn_stats(
                sb["bst"][:, 6:12], sb["O"][:, qt * 1024 + 512: qt * 1024 + 1024])),
                [("vs", cb1)])
            cag = comp("vector", (lambda e: e.bn_aggr(
                sb["bmv"][:, 0:2], sb["bst"][:, 0:12])), [("vs", cb2)])
            crs0 = comp("scalar", (lambda e: e.activation(
                out=sb["mm2"][:, 0:1], in_=sb["bmv"][:, 1:2], func=AF.Sqrt,
                bias=sb["epsv"][:, 0:1])), [("vs", cag), ("ss", ep_s)])
            crs = comp("vector", (lambda e: e.reciprocal(
                out=sb["tv"][:, 0:1], in_=sb["mm2"][:, 0:1])),
                [("ss", crs0), ("vs", cnt["vs"])])
            cx1 = comp("vector", (lambda e, qt=qt: e.tensor_scalar(
                out=sb["O"][:, qt * 1024:(qt + 1) * 1024],
                in0=sb["O"][:, qt * 1024:(qt + 1) * 1024],
                scalar1=sb["bmv"][:, 0:1], scalar2=sb["tv"][:, 0:1],
                op0=ALU.subtract, op1=ALU.mult)), [("vs", crs)])
            ep_s = crs0
            ln_done.append(cx1)
        out_dmas = dma("gpsimd", (lambda e, tau=tau: e.dma_start(
            out=dt["out"].rearrange("(t p) d -> p t d", p=128)[:, 4 * tau:4 * tau + 4, :],
            in_=sb["O"][:, 4 * tau * 1024:(4 * tau + 4) * 1024].rearrange(
                "p (t d) -> p t d", d=1024))),
            "d_out", waits=[("vs", v) for v in ln_done])

    wait("sync", [("d_out", dcnt["d_out"])])

    with es_top:
      with es_wt:
        with es_b:
            with nc.Block() as block:
                @block.sync
                def _(e):
                    for f in streams["sync"]:
                        f(e)

                @block.tensor
                def _(e):
                    for f in streams["tensor"]:
                        f(e)

                @block.vector
                def _(e):
                    for f in streams["vector"]:
                        f(e)

                @block.scalar
                def _(e):
                    for f in streams["scalar"]:
                        f(e)

                @block.gpsimd
                def _(e):
                    for f in streams["gpsimd"]:
                        f(e)

    n_inst = sum(len(v) for v in streams.values())
    return nc, n_inst



def _pack_oc(wt, n_ic, n_oc, ocw=128):
    return np.ascontiguousarray(
        wt.reshape(n_ic, 128, n_oc, ocw).transpose(2, 0, 1, 3).reshape(n_oc * n_ic * 128, ocw))


def _pack_pc(wt, n_ic, n_oc):
    return np.ascontiguousarray(
        wt.reshape(n_ic, 128, n_oc, 128).transpose(2, 1, 0, 3).reshape(n_oc * 128, n_ic * 128))


def _stage(cfg: Cfg, inputs):
    D, DC, HC = cfg.D, cfg.DC, cfg.HC
    x = np.asarray(inputs["L0_attn_x"], dtype=np.float32)
    W = np.asarray(inputs["L0_attn_W"], dtype=np.float32)
    B = x.shape[0]
    shared = {}
    shared["wattT"] = np.ascontiguousarray(W.T)
    shared["wattK"] = np.ascontiguousarray(W[:1024, :])
    w1t = np.ascontiguousarray(np.asarray(inputs["L0_ffn_w1_w"], dtype=np.float32).T)
    w2t = np.ascontiguousarray(np.asarray(inputs["L0_ffn_w2_w"], dtype=np.float32).T)
    h1t = np.ascontiguousarray(np.asarray(inputs["L0_ffn_h1_w"], dtype=np.float32).T)
    h2t = np.ascontiguousarray(np.asarray(inputs["L0_ffn_h2_w"], dtype=np.float32).T)
    w3t = np.ascontiguousarray(np.asarray(inputs["L0_ffn_w3_w"], dtype=np.float32).T)
    shared["w1tp"] = _pack_pc(w1t, DC, HC)
    shared["w2tp"] = _pack_pc(w2t, DC, HC)
    shared["h1tp"] = _pack_pc(h1t, HC, HC)
    shared["h2tp"] = _pack_pc(h2t, HC, HC)
    shared["w3tp"] = _pack_oc(w3t, HC, 2, 512)
    for nm, key in (("s1", "L0_ffn_w1_s"), ("s2", "L0_ffn_w2_s"),
                    ("s1h", "L0_ffn_h1_s"), ("s2h", "L0_ffn_h2_s")):
        s = np.asarray(inputs[key], dtype=np.float32)
        shared[nm] = np.ascontiguousarray(s.reshape(HC, 128).T)
    shared["s3b"] = np.ascontiguousarray(
        np.tile(np.asarray(inputs["L0_ffn_w3_s"], dtype=np.float32)[None, :], (128, 1)))
    shared["uio"] = np.ascontiguousarray(
        (np.arange(1024, dtype=np.float32)[None, :]
         - np.arange(128, dtype=np.float32)[:, None]).astype(np.float16))
    shared["identb"] = np.eye(128, dtype=ml_dtypes.bfloat16)

    xfm_all = [np.ascontiguousarray(x[b].T) for b in range(B)]
    per_seq = cfg.n_cores // B
    in_maps = []
    for c in range(cfg.n_cores):
        seq = c // per_seq
        r = c % per_seq
        rows = [(cfg.cps * p + r) * 128 for p in range(cfg.NQT)]
        m = dict(shared)
        m["xfm"] = xfm_all[seq]
        m["xraw"] = x[seq]
        m["xq"] = np.ascontiguousarray(np.concatenate(
            [xfm_all[seq][:, q0:q0 + 128] for q0 in rows], axis=1))
        m["xtm"] = np.ascontiguousarray(np.concatenate(
            [x[seq][q0:q0 + 128, :] for q0 in rows], axis=0))
        th = np.zeros((128, cfg.NVIS), dtype=np.float32)
        v = 0
        for kt in range(cfg.NKT):
            for s in cfg.slots[kt]:
                th[:, v] = float(rows[s] - kt * 1024)
                v += 1
        m["theta"] = th
        in_maps.append(m)
    return in_maps


def run_cfg(cfg: Cfg, inputs, runner=None):
    in_maps = _stage(cfg, inputs)
    nc, n_inst = build_program(cfg)
    if runner is None:
        res = run_bass_kernel_spmd(nc, in_maps, core_ids=list(range(cfg.n_cores)))
        results = res.results
    else:
        results = runner(nc, in_maps)
    x = np.asarray(inputs["L0_attn_x"])
    B, S, D = x.shape
    out = np.empty((B, S, D), dtype=np.float32)
    per_seq = cfg.n_cores // B
    for c in range(cfg.n_cores):
        seq = c // per_seq
        r = c % per_seq
        res = results[c]["out"]
        for p in range(cfg.NQT):
            q0 = (cfg.cps * p + r) * 128
            out[seq, q0:q0 + 128, :] = res[p * 128:(p + 1) * 128]
    return out


def kernel(**inputs):
    return run_cfg(CFG_FULL, inputs)


# version 21
# speedup vs baseline: 1.0110x; 1.0110x over previous
import numpy as np
from contextlib import ExitStack
import ml_dtypes

import concourse.bass as bass
import concourse.mybir as mybir
from concourse.bass_utils import run_bass_kernel_spmd

F32 = mybir.dt.float32
BF16 = mybir.dt.bfloat16
AF = mybir.ActivationFunctionType
ALU = mybir.AluOpType
AX = mybir.AxisListType

NEG_BIG = -30000.0


class Cfg:
    def __init__(self, seq=8192, qpc=2048, n_cores=8):
        self.D = 1024
        self.DH = 4096
        self.DC = 8
        self.HC = 32
        self.seq = seq
        self.qpc = qpc
        self.NKT = seq // 1024
        self.NQT = qpc // 128
        self.TO = 512
        self.NFT = qpc // 512
        self.n_cores = n_cores
        self.cps = n_cores // 2
        self.G = 8 // self.cps
        self.slots = [[p for p in range(self.NQT) if p // self.G >= kt]
                      for kt in range(self.NKT)]
        self.NVIS = sum(len(s) for s in self.slots)


CFG_FULL = Cfg(8192, 2048, 8)
CFG_MINI = Cfg(4096, 2048, 4)


def build_program(cfg: Cfg):
    D, DH, DC, HC = cfg.D, cfg.DH, cfg.DC, cfg.HC
    NKT, NQT, TO, NFT = cfg.NKT, cfg.NQT, cfg.TO, cfg.NFT
    QPC, SEQ = cfg.qpc, cfg.seq
    NVIS = cfg.NVIS
    SLOTS = cfg.slots

    nc = bass.Bass("TRN2", target_bir_lowering=False, debug=False)

    dt = {}
    dt["xfm"] = nc.dram_tensor("xfm", [D, SEQ], F32, kind="ExternalInput").ap()
    dt["xq"] = nc.dram_tensor("xq", [D, QPC], F32, kind="ExternalInput").ap()
    dt["xtm"] = nc.dram_tensor("xtm", [QPC, D], F32, kind="ExternalInput").ap()
    dt["wattT"] = nc.dram_tensor("wattT", [D, 2 * D], F32, kind="ExternalInput").ap()
    dt["wattK"] = nc.dram_tensor("wattK", [D, D], F32, kind="ExternalInput").ap()
    dt["xraw"] = nc.dram_tensor("xraw", [SEQ, D], F32, kind="ExternalInput").ap()
    dt["w1tp"] = nc.dram_tensor("w1tp", [HC * 128, DC * 128], F32, kind="ExternalInput").ap()
    dt["w2tp"] = nc.dram_tensor("w2tp", [HC * 128, DC * 128], F32, kind="ExternalInput").ap()
    dt["h1tp"] = nc.dram_tensor("h1tp", [HC * 128, HC * 128], F32, kind="ExternalInput").ap()
    dt["h2tp"] = nc.dram_tensor("h2tp", [HC * 128, HC * 128], F32, kind="ExternalInput").ap()
    dt["w3tp"] = nc.dram_tensor("w3tp", [2 * HC * 128, 512], F32, kind="ExternalInput").ap()
    for s in ("s1", "s2", "s1h", "s2h"):
        dt[s] = nc.dram_tensor(s, [128, HC], F32, kind="ExternalInput").ap()
    dt["s3b"] = nc.dram_tensor("s3b", [128, D], F32, kind="ExternalInput").ap()
    dt["theta"] = nc.dram_tensor("theta", [128, NVIS], F32, kind="ExternalInput").ap()
    dt["uio"] = nc.dram_tensor("uio", [128, 1024], mybir.dt.float16, kind="ExternalInput").ap()
    dt["identb"] = nc.dram_tensor("identb", [128, 128], BF16, kind="ExternalInput").ap()
    dt["out"] = nc.dram_tensor("out", [QPC, D], F32, kind="ExternalOutput").ap()
    dt["w1s"] = nc.dram_tensor("w1s", [HC * 128, DC * 128], BF16, kind="Internal").ap()
    dt["w2s"] = nc.dram_tensor("w2s", [HC * 128, DC * 128], BF16, kind="Internal").ap()
    dt["h1s"] = nc.dram_tensor("h1s", [HC * 128, HC * 128], BF16, kind="Internal").ap()
    dt["h2s"] = nc.dram_tensor("h2s", [HC * 128, HC * 128], BF16, kind="Internal").ap()
    dt["w3s"] = nc.dram_tensor("w3s", [2 * HC * 128, 512], BF16, kind="Internal").ap()

    sem_names = ["ts", "vs", "ss", "gs",
                 "d_const", "g_xk0", "g_xk1", "g_q0", "g_q1", "d_xtm0", "d_xtm1", "d_out",
                 "g_wt", "g_wk", "g_xt0", "g_xt1", "g_cw1", "g_cw2", "g_cw3", "g_ch1", "g_ch2",
                 "d_w0", "d_w1", "d_w2", "d_w3",
                 "d_wc0", "d_wc1", "d_wc2"]
    es_top = ExitStack()
    SEM = {n: es_top.enter_context(nc.semaphore(n)) for n in sem_names}

    sb = {}

    def sbuf(name, shape, dtp, stack):
        sb[name] = stack.enter_context(nc.sbuf_tensor("sb_" + name, shape, dtp))
        return sb[name]

    sbuf("O", [128, NQT * 1024], F32, es_top)
    for s in ("s1", "s2", "s1h", "s2h"):
        sbuf(s, [128, HC], F32, es_top)
    sbuf("s3b", [128, D], F32, es_top)
    sbuf("identb", [128, 128], BF16, es_top)
    sbuf("lrun", [128, NQT], F32, es_top)
    for s in ("rh", "s1v", "s2v", "nmean", "mm2", "tv", "epsv"):
        sbuf(s, [128, 1], F32, es_top)
    sbuf("bst", [128, 12], F32, es_top)
    sbuf("bmv", [128, 2], F32, es_top)
    sbuf("late8", [128, 2048], F32, es_top)

    streams = {"sync": [], "tensor": [], "vector": [], "scalar": [], "gpsimd": []}
    cnt = {"ts": 0, "vs": 0, "ss": 0, "gs": 0}
    dcnt = {n: 0 for n in sem_names if n not in ("ts", "vs", "ss", "gs")}

    SELF = {"tensor": "ts", "vector": "vs", "scalar": "ss", "gpsimd": "gs"}

    def comp(eng, build_fn, waits=()):
        s = SELF[eng]
        cnt[s] += 1
        my = cnt[s]
        w = tuple((a, b) for (a, b) in waits if b > 0)

        def f(e, build_fn=build_fn, w=w, s=s):
            for sname, v in w:
                e.wait_ge(SEM[sname], v)
            build_fn(e).then_inc(SEM[s], 1)
        streams[eng].append(f)
        return my

    def dma(eng, build_fn, sem_name, waits=()):
        dcnt[sem_name] += 16
        my = dcnt[sem_name]
        w = tuple((a, b) for (a, b) in waits if b > 0)

        def f(e, build_fn=build_fn, w=w, sem_name=sem_name):
            for sname, v in w:
                e.wait_ge(SEM[sname], v)
            build_fn(e).then_inc(SEM[sem_name], 16)
        streams[eng].append(f)
        return my

    def wait(eng, waits):
        w = tuple((a, b) for (a, b) in waits if b > 0)

        def f(e, w=w):
            for sname, v in w:
                e.wait_ge(SEM[sname], v)
        streams[eng].append(f)

    es_wt = ExitStack()
    sbuf("WTv", [128, DC * 1024], BF16, es_wt)
    es_a = ExitStack()
    sbuf("QP", [128, DC * QPC], BF16, es_a)
    sbuf("xk0", [128, DC * 1024], BF16, es_a)
    sbuf("XT0", [128, DC * 1024], BF16, es_a)
    sbuf("P0", [128, 1024], BF16, es_a)
    sbuf("P1", [128, 1024], BF16, es_a)
    sbuf("PT0", [128, 1024], BF16, es_a)
    sbuf("PT1", [128, 1024], BF16, es_a)
    sbuf("uio", [128, 1024], mybir.dt.float16, es_a)
    sbuf("theta", [128, NVIS], F32, es_a)
    sbuf("msk", [128, 1024], BF16, es_a)
    sbuf("negm", [128, NQT], F32, es_a)
    sbuf("mpos", [128, NQT], F32, es_a)
    sbuf("alpha", [128, 4], F32, es_a)
    sbuf("lpart", [128, 4], F32, es_a)
    sbuf("nmp", [128, 1], F32, es_a)

    for name in ("uio", "theta", "s1", "s2", "s1h", "s2h", "s3b", "identb"):
        dma("sync", (lambda e, n=name: e.dma_start(out=sb[n][:], in_=dt[n])), "d_const")
    C_CONST = dcnt["d_const"]

    comp("vector", lambda e: e.memset(sb["epsv"][:], 1e-5))
    es_q = ExitStack()
    sbuf("Qq0", [128, DC * 512], BF16, es_q)
    sbuf("Qq1", [128, DC * 512], BF16, es_q)
    sbuf("WK", [128, DC * D], BF16, es_q)
    ps = {}
    ps["S0"] = es_a.enter_context(nc.psum_tensor("S0", [128, 1024], F32))
    ps["S1"] = es_a.enter_context(nc.psum_tensor("S1", [128, 1024], F32))
    ps["PV"] = es_a.enter_context(nc.psum_tensor("PV", [128, 1024], F32))
    ps["PT0"] = es_a.enter_context(nc.psum_tensor("PT0p", [128, 1024], BF16))
    ps["PT1"] = es_a.enter_context(nc.psum_tensor("PT1p", [128, 1024], BF16))

    for dc in range(DC):
        dma("gpsimd", (lambda e, dc=dc: e.dma_start(
            out=sb["WK"][:, dc * D:(dc + 1) * D],
            in_=dt["wattK"][dc * 128:(dc + 1) * 128, :])), "g_wk")
        dma("gpsimd", (lambda e, dc=dc: e.dma_start(
            out=sb["Qq0"][:, dc * 512:(dc + 1) * 512],
            in_=dt["xq"][dc * 128:(dc + 1) * 128, 0:512])), "g_q0")
    G_WK = dcnt["g_wk"]

    cast_chunks = []

    def cast_rows(src, dst, rows, cols, sem):
        step = max(128, (2 * 1024 * 1024) // (cols * 4))
        n = 0
        for r0 in range(0, rows, step):
            r1 = min(rows, r0 + step)
            cast_chunks.append((src, dst, r0, r1, sem))
            n += 16
        return n

    G_CW1 = cast_rows("w1tp", "w1s", HC * 128, DC * 128, "g_cw1")
    G_CW2 = cast_rows("w2tp", "w2s", HC * 128, DC * 128, "g_cw2")
    G_CW3 = cast_rows("w3tp", "w3s", 2 * HC * 128, 512, "g_cw3")
    G_CH1 = cast_rows("h1tp", "h1s", HC * 128, HC * 128, "g_ch1")
    G_CH2 = cast_rows("h2tp", "h2s", HC * 128, HC * 128, "g_ch2")
    N_TAIL = 24
    cast_tail = cast_chunks[len(cast_chunks) - N_TAIL:]
    del cast_chunks[len(cast_chunks) - N_TAIL:]
    CAST_GATE = {"w1s": ("g_cw1", G_CW1), "w2s": ("g_cw2", G_CW2),
                 "h1s": ("g_ch1", G_CH1), "h2s": ("g_ch2", G_CH2)}
    _cast_i = [0]
    _tail_i = [0]

    def _emit_chunk(tup):
        s_, d_, r0, r1, sem = tup
        dma("gpsimd", (lambda e, r0=r0, r1=r1, s_=s_, d_=d_: e.dma_start(
            out=dt[d_][r0:r1, :], in_=dt[s_][r0:r1, :])), sem)

    def emit_cast_chunks(n):
        bound = min(len(cast_chunks), _cast_i[0] + n)
        while _cast_i[0] < bound:
            _emit_chunk(cast_chunks[_cast_i[0]])
            _cast_i[0] += 1

    def emit_tail_chunks(n):
        bound = min(len(cast_tail), _tail_i[0] + n)
        while _tail_i[0] < bound:
            _emit_chunk(cast_tail[_tail_i[0]])
            _tail_i[0] += 1

    last_pv_mm = 0
    pair_free = {"S0": [], "S1": [], "PV": []}
    pt_free = {0: 0, 1: 0}
    p_free = {0: 0, 1: 0}
    xk_war = {0: 0, 1: 0}
    xt_war = {0: 0, 1: 0}
    xk_done = {}
    xt_done = {}

    vS, vMadd, vNegm, vExp, vAlpha, vPT, vPTe, vPV, vO, vL = ({} for _ in range(10))
    _vctr = [0]
    _vmap = {}

    def vid(kt, qi):
        key = (kt, qi)
        if key not in _vmap:
            _vmap[key] = _vctr[0]
            _vctr[0] += 1
        return _vmap[key]

    def xk_load_cast(kt, qp_gate=0):
        buf = "xk%d" % (kt % 2)
        last = 0
        for ic in range(DC):
            wts = []
            if ic == 0:
                if xk_war[kt % 2]:
                    wts.append(("ts", xk_war[kt % 2]))
                if qp_gate:
                    wts.append(("ts", qp_gate))
            last = dma("gpsimd", (lambda e, ic=ic, kt=kt, buf=buf: e.dma_start(
                out=sb[buf][:, ic * 1024:(ic + 1) * 1024],
                in_=dt["xfm"][ic * 128:(ic + 1) * 128, kt * 1024:(kt + 1) * 1024])),
                "g_xk%d" % (kt % 2), waits=wts)
        xk_done[kt] = last

    def xt_load_cast(kt, qp_gate=0):
        buf = "XT%d" % (kt % 2)
        last = 0
        for kc in range(DC):
            wts = []
            if kc == 0:
                if xt_war[kt % 2]:
                    wts.append(("ts", xt_war[kt % 2]))
                if qp_gate:
                    wts.append(("ts", qp_gate))
            last = dma("gpsimd", (lambda e, kc=kc, kt=kt, buf=buf: e.dma_start(
                out=sb[buf][:, kc * 1024:(kc + 1) * 1024],
                in_=dt["xraw"][kt * 1024 + kc * 128: kt * 1024 + (kc + 1) * 128, :])),
                "g_xt%d" % (kt % 2), waits=wts)
        xt_done[kt] = last

    def emit_qprime():
        halves = [("S0", 0), ("S0", 512), ("S1", 0), ("S1", 512)]
        hfree = [[], [], [], []]
        bank_last = {"S0": 0, "S1": 0}
        q_last_mm = {}
        ev = 0
        for q in range(QPC // 512):
            qb = "Qq%d" % (q % 2)
            if q >= 2:
                for ic in range(DC):
                    wts = [("ts", q_last_mm[q - 2])] if ic == 0 else []
                    dma("gpsimd", (lambda e, ic=ic, q=q, qb=qb: e.dma_start(
                        out=sb[qb][:, ic * 512:(ic + 1) * 512],
                        in_=dt["xq"][ic * 128:(ic + 1) * 128, q * 512:(q + 1) * 512])),
                        "g_q%d" % (q % 2), waits=wts)
            for ib in range(DC):
                hi = (q * DC + ib) % 4
                pr, off = halves[hi]
                wts0 = list(hfree[hi])
                last = 0
                for dc in range(DC):
                    wts = wts0
                    wts0 = []
                    if ib == 0 and dc == 0:
                        wts = wts + [("g_q%d" % (q % 2), 128 * (q // 2 + 1))]
                        if q == 0:
                            wts = wts + [("g_wk", G_WK)]
                    fn = (lambda e, pr=pr, off=off, dc=dc, ib=ib, qb=qb: e.matmul(
                        ps[pr][:, off:off + 512],
                        sb["WK"][:, dc * D + ib * 128: dc * D + ib * 128 + 128],
                        sb[qb][:, dc * 512:(dc + 1) * 512],
                        start=(dc == 0), stop=(dc == DC - 1)))
                    last = comp("tensor", fn, wts)
                ev = comp("scalar", (lambda e, pr=pr, off=off, ib=ib, q=q: e.copy(
                    out=sb["QP"][:, ib * QPC + q * 512: ib * QPC + (q + 1) * 512],
                    in_=ps[pr][:, off:off + 512])), [("ts", last)])
                hfree[hi] = [("ss", ev)]
                bank_last[pr] = ev
            q_last_mm[q] = cnt["ts"]
        pair_free["S0"] = [("ss", bank_last["S0"])]
        pair_free["S1"] = [("ss", bank_last["S1"])]
        return ev

    def emit_S(kt, qi, first_of_kt):
        v = vid(kt, qi)
        buf = "xk%d" % (kt % 2)
        pr = "S0" if qi % 2 == 0 else "S1"
        wts = list(pair_free[pr])
        if first_of_kt:
            wts.append(("g_xk%d" % (kt % 2), xk_done[kt]))
        if kt == 0 and qi == SLOTS[0][0]:
            wts += [("ss", QP_DONE), ("d_const", C_CONST)]
        lastmm = 0
        for ns in range(2):
            for ic in range(DC):
                fn = (lambda e, pr=pr, ns=ns, ic=ic, qi=qi, buf=buf: e.matmul(
                    ps[pr][:, ns * 512:(ns + 1) * 512],
                    sb["QP"][:, ic * QPC + qi * 128: ic * QPC + qi * 128 + 128],
                    sb[buf][:, ic * 1024 + ns * 512: ic * 1024 + (ns + 1) * 512],
                    start=(ic == 0), stop=(ic == DC - 1)))
                lastmm = comp("tensor", fn, wts)
                wts = ()
        vS[v] = lastmm
        xk_war[kt % 2] = lastmm
        pair_free[pr] = []

    def emit_mask(kt, qi):
        v = vid(kt, qi)
        pr = "S0" if qi % 2 == 0 else "S1"
        c1 = comp("vector", (lambda e, v=v: e.tensor_scalar(
            out=sb["msk"][:], in0=sb["uio"][:],
            scalar1=sb["theta"][:, v:v + 1], scalar2=NEG_BIG,
            op0=ALU.is_gt, op1=ALU.mult)),
            [("vs", cnt["vs"]), ("d_const", C_CONST if v == 0 else 0)])
        c2 = comp("vector", (lambda e, pr=pr: e.tensor_tensor(
            out=ps[pr][:, :], in0=ps[pr][:, :], in1=sb["msk"][:], op=ALU.add)),
            [("ts", vS[v]), ("vs", c1)])
        vMadd[v] = c2

    def emit_negm(kt, qi):
        v = vid(kt, qi)
        pr = "S0" if qi % 2 == 0 else "S1"
        c1 = comp("vector", (lambda e, pr=pr: e.reduce_max(
            sb["nmp"][:, 0:1], ps[pr][:, :], axis=AX.X, negate=True)),
            [("vs", vMadd[v])])
        if kt == 0:
            c2 = comp("vector", (lambda e, qi=qi: e.tensor_copy(
                out=sb["negm"][:, qi:qi + 1], in_=sb["nmp"][:, 0:1])), [("vs", c1)])
        else:
            c2 = comp("vector", (lambda e, qi=qi: e.tensor_tensor(
                out=sb["negm"][:, qi:qi + 1], in0=sb["negm"][:, qi:qi + 1],
                in1=sb["nmp"][:, 0:1], op=ALU.min)), [("vs", c1)])
        vNegm[v] = c2

    def emit_exp(kt, qi):
        v = vid(kt, qi)
        pr = "S0" if qi % 2 == 0 else "S1"
        par = qi % 2
        sl = v % 4
        wts = [("vs", vNegm[v]), ("vs", vL.get(v - 4, 0))]
        if kt > 0:
            ca = comp("scalar", (lambda e, qi=qi, sl=sl: e.activation(
                out=sb["alpha"][:, sl:sl + 1], in_=sb["mpos"][:, qi:qi + 1],
                func=AF.Exp, bias=sb["negm"][:, qi:qi + 1])), wts)
            vAlpha[v] = ca
            wts = [("ss", ca)]
        else:
            vAlpha[v] = 0
        ce = comp("scalar", (lambda e, pr=pr, par=par, qi=qi, sl=sl: e.activation(
            out=sb["P%d" % par][:], in_=ps[pr][:, :], func=AF.Exp,
            bias=sb["negm"][:, qi:qi + 1], accum_out=sb["lpart"][:, sl:sl + 1])),
            [("ts", p_free[par])] + wts)
        cm = comp("scalar", (lambda e, qi=qi: e.activation(
            out=sb["mpos"][:, qi:qi + 1], in_=sb["negm"][:, qi:qi + 1],
            func=AF.Copy, scale=-1.0)), [("ss", ce)])
        vExp[v] = ce
        pair_free[pr] = [("ss", ce)]

    def emit_PT(kt, qi):
        v = vid(kt, qi)
        par = qi % 2
        wts = [("ss", vExp[v]), ("ss", pt_free[par])]
        last = 0
        for kc in range(DC):
            fn = (lambda e, par=par, kc=kc: e.transpose(
                ps["PT%d" % par][:, kc * 128:(kc + 1) * 128],
                sb["P%d" % par][:, kc * 128:(kc + 1) * 128],
                sb["identb"][:]))
            last = comp("tensor", fn, wts)
            wts = ()
        vPT[v] = last
        p_free[par] = last

    def emit_PTevac(kt, qi):
        v = vid(kt, qi)
        par = qi % 2
        c1 = comp("scalar", (lambda e, par=par: e.copy(
            out=sb["PT%d" % par][:], in_=ps["PT%d" % par][:, :])),
            [("ts", vPT[v])])
        vPTe[v] = c1
        pt_free[par] = c1

    def emit_PV(kt, qi, first_of_kt):
        nonlocal last_pv_mm
        v = vid(kt, qi)
        par = qi % 2
        buf = "XT%d" % (kt % 2)
        wts = [("ss", vPTe[v])] + pair_free["PV"]
        if first_of_kt:
            wts.append(("g_xt%d" % (kt % 2), xt_done[kt]))
        last = 0
        for ns in range(2):
            for kc in range(DC):
                fn = (lambda e, par=par, ns=ns, kc=kc, buf=buf: e.matmul(
                    ps["PV"][:, ns * 512:(ns + 1) * 512],
                    sb["PT%d" % par][:, kc * 128:(kc + 1) * 128],
                    sb[buf][:, kc * 1024 + ns * 512: kc * 1024 + (ns + 1) * 512],
                    start=(kc == 0), stop=(kc == DC - 1)))
                last = comp("tensor", fn, wts)
                wts = ()
        vPV[v] = last
        last_pv_mm = last
        xt_war[kt % 2] = last
        pair_free["PV"] = []

    def emit_O(kt, qi):
        v = vid(kt, qi)
        sl = v % 4
        if kt == 0:
            c1 = comp("vector", (lambda e, qi=qi: e.tensor_copy(
                out=sb["O"][:, qi * 1024:(qi + 1) * 1024], in_=ps["PV"][:, :])),
                [("ts", vPV[v])])
            cl = comp("vector", (lambda e, qi=qi, sl=sl: e.tensor_copy(
                out=sb["lrun"][:, qi:qi + 1], in_=sb["lpart"][:, sl:sl + 1])),
                [("ss", vExp[v])])
        else:
            c1 = comp("vector", (lambda e, qi=qi, sl=sl: e.scalar_tensor_tensor(
                out=sb["O"][:, qi * 1024:(qi + 1) * 1024],
                in0=sb["O"][:, qi * 1024:(qi + 1) * 1024],
                scalar=sb["alpha"][:, sl:sl + 1], in1=ps["PV"][:, :],
                op0=ALU.mult, op1=ALU.add)),
                [("ts", vPV[v]), ("ss", vAlpha[v])])
            cl = comp("vector", (lambda e, qi=qi, sl=sl: e.scalar_tensor_tensor(
                out=sb["lrun"][:, qi:qi + 1], in0=sb["lrun"][:, qi:qi + 1],
                scalar=sb["alpha"][:, sl:sl + 1], in1=sb["lpart"][:, sl:sl + 1],
                op0=ALU.mult, op1=ALU.add)), [("ss", vExp[v])])
        vO[v] = c1
        vL[v] = cl
        pair_free["PV"] = [("vs", c1)]

    n_chunks = len(cast_chunks)
    xk_load_cast(0)
    for ic in range(DC):
        dma("gpsimd", (lambda e, ic=ic: e.dma_start(
            out=sb["Qq1"][:, ic * 512:(ic + 1) * 512],
            in_=dt["xq"][ic * 128:(ic + 1) * 128, 512:1024])), "g_q1")
    xt_load_cast(0)
    QP_DONE = emit_qprime()
    qp_last_mm = cnt["ts"]
    for ic in range(DC):
        dma("gpsimd", (lambda e, ic=ic: e.dma_start(
            out=sb["WTv"][:, ic * 1024:(ic + 1) * 1024],
            in_=dt["wattT"][ic * 128:(ic + 1) * 128, D:2 * D])), "g_wt")
    G_WT = dcnt["g_wt"]
    es_q.close()
    sbuf("xk1", [128, DC * 1024], BF16, es_a)
    sbuf("XT1", [128, DC * 1024], BF16, es_a)
    vis_i = 0
    want_prev = 0
    for kt in range(NKT):
        if kt + 1 < NKT:
            xk_load_cast(kt + 1, qp_gate=qp_last_mm if kt == 0 else 0)
            xt_load_cast(kt + 1, qp_gate=qp_last_mm if kt == 0 else 0)
        sl = SLOTS[kt]
        nv = len(sl)
        for k in range(nv + 2):
            if k < nv:
                want = min(n_chunks, int(n_chunks * (vis_i + 1) / NVIS))
                emit_cast_chunks(want - want_prev)
                want_prev = want
                vis_i += 1
                emit_S(kt, sl[k], first_of_kt=(k == 0))
                emit_mask(kt, sl[k])
                emit_negm(kt, sl[k])
            if 1 <= k <= nv:
                emit_exp(kt, sl[k - 1])
                emit_PT(kt, sl[k - 1])
            if 2 <= k <= nv + 1:
                emit_PTevac(kt, sl[k - 2])
                emit_PV(kt, sl[k - 2], first_of_kt=(k == 2))
                emit_O(kt, sl[k - 2])

    emit_cast_chunks(len(cast_chunks))

    INV_D = 1.0 / D
    xtm_slot_free = [0, 0]
    x1_ready_v = {}
    ep_s = 0
    ep_v = 0

    def emit_ep(qi):
        nonlocal ep_s, ep_v
        slot = qi % 2
        ld = dma("gpsimd", (lambda e, qi=qi, slot=slot: e.dma_start(
            out=sb["late8"][:, slot * 1024:(slot + 1) * 1024],
            in_=dt["xtm"][qi * 128:(qi + 1) * 128, :])),
            "d_xtm%d" % slot, waits=[("vs", xtm_slot_free[slot])])
        lastkt = qi // cfg.G
        par = 0
        cr0 = comp("vector", (lambda e, qi=qi: e.tensor_scalar_mul(
            out=sb["rh"][:, 0:1], in0=sb["lrun"][:, qi:qi + 1], scalar1=2.0)),
            [("vs", vL[vid(lastkt, qi)]), ("ss", ep_s), ("vs", cnt["vs"])])
        cr = comp("vector", (lambda e: e.reciprocal(
            out=sb["rh"][:, 0:1], in_=sb["rh"][:, 0:1])), [("vs", cr0)])
        cnorm = comp("vector", (lambda e, qi=qi, par=par: e.tensor_scalar_mul(
            out=sb["Ob%d" % par][:], in0=sb["O"][:, qi * 1024:(qi + 1) * 1024],
            scalar1=sb["rh"][:, 0:1])),
            [("vs", cr), ("vs", vO[vid(lastkt, qi)]),
             ("ts", ep_state["ob_free"][par]),
             ("ts", T_ts if qi == 0 else 0)])
        wts = [("vs", cnorm), ("ss", ep_state["pte_evac"])]
        last = 0
        for ic in range(DC):
            fn = (lambda e, par=par, ic=ic: e.transpose(
                ps["PTE"][:, ic * 128:(ic + 1) * 128],
                sb["Ob%d" % par][:, ic * 128:(ic + 1) * 128],
                sb["identb"][:]))
            last = comp("tensor", fn, wts)
            wts = ()
        ep_state["ob_free"][par] = last
        pev = comp("scalar", (lambda e, par=par: e.copy(
            out=sb["Ofm%d" % par][:], in_=ps["PTE"][:, :])),
            [("ts", last), ("ts", ep_state["ofm_readers"][par])])
        ep_state["pte_evac"] = pev
        pr = ["F0", "F1", "F2"][rot["i"] % 3]
        rot["i"] += 1
        wts = [("ss", pev)] + pfb[pr + "a"] + pfb[pr + "b"]
        if qi == 0:
            wts.append(("g_wt", G_WT))
        lastp = 0
        for ns in range(2):
            for ic in range(DC):
                fn = (lambda e, pr=pr, par=par, ns=ns, ic=ic: e.matmul(
                    ps[pr][:, ns * 512:(ns + 1) * 512],
                    sb["Ofm%d" % par][:, ic * 128:(ic + 1) * 128],
                    sb["WTv"][:, ic * 1024 + ns * 512: ic * 1024 + (ns + 1) * 512],
                    start=(ic == 0), stop=(ic == DC - 1)))
                lastp = comp("tensor", fn, wts)
                wts = ()
        ep_state["ofm_readers"][par] = lastp
        cy = comp("vector", (lambda e, qi=qi, slot=slot, pr=pr: e.tensor_tensor(
            out=sb["O"][:, qi * 1024:(qi + 1) * 1024],
            in0=ps[pr][:, 0:1024],
            in1=sb["late8"][:, slot * 1024:(slot + 1) * 1024], op=ALU.add)),
            [("ts", lastp), ("d_xtm%d" % slot, ld), ("vs", cnorm)])
        pfb[pr + "a"] = [("vs", cy)]
        pfb[pr + "b"] = [("vs", cy)]
        xtm_slot_free[slot] = cy
        cb1 = comp("vector", (lambda e, qi=qi: e.bn_stats(
            sb["bst"][:, 0:6], sb["O"][:, qi * 1024: qi * 1024 + 512])),
            [("vs", cy), ("ss", ep_s), ("vs", cnt["vs"])])
        cb2 = comp("vector", (lambda e, qi=qi: e.bn_stats(
            sb["bst"][:, 6:12], sb["O"][:, qi * 1024 + 512: qi * 1024 + 1024])),
            [("vs", cb1)])
        cag = comp("vector", (lambda e: e.bn_aggr(
            sb["bmv"][:, 0:2], sb["bst"][:, 0:12])), [("vs", cb2)])
        crs0 = comp("scalar", (lambda e: e.activation(
            out=sb["mm2"][:, 0:1], in_=sb["bmv"][:, 1:2], func=AF.Sqrt,
            bias=sb["epsv"][:, 0:1])), [("vs", cag), ("ss", ep_s)])
        crs = comp("vector", (lambda e: e.reciprocal(
            out=sb["tv"][:, 0:1], in_=sb["mm2"][:, 0:1])),
            [("ss", crs0), ("vs", cnt["vs"])])
        cx1 = comp("vector", (lambda e, qi=qi: e.tensor_scalar(
            out=sb["O"][:, qi * 1024:(qi + 1) * 1024],
            in0=sb["O"][:, qi * 1024:(qi + 1) * 1024],
            scalar1=sb["bmv"][:, 0:1], scalar2=sb["tv"][:, 0:1],
            op0=ALU.subtract, op1=ALU.mult)), [("vs", crs)])
        ep_s = crs0
        ep_v = cx1
        x1_ready_v[qi] = cx1

    T_ts, T_vs, T_ss = cnt["ts"], cnt["vs"], cnt["ss"]
    SS_ATTN = cnt["ss"]
    es_a.close()

    es_b = ExitStack()
    sbuf("A", [128, HC * TO], BF16, es_b)
    sbuf("Bb", [128, HC * TO], BF16, es_b)
    sbuf("x1f", [128, DC * TO], BF16, es_b)
    NWB = 3
    for i in range(NWB):
        sbuf("wb%d" % i, [128, HC * 128], BF16, es_b)
    sbuf("wc0", [128, 1024], BF16, es_b)
    sbuf("wc1", [128, 1024], BF16, es_b)
    sbuf("wc2", [128, 1024], BF16, es_b)
    sbuf("xc0", [128, 1024], BF16, es_b)
    sbuf("xc1", [128, 1024], BF16, es_b)
    sbuf("ea", [128, TO], F32, es_b)
    sbuf("Ob0", [128, 1024], BF16, es_b)
    sbuf("Ofm0", [128, 1024], BF16, es_b)
    ps["F0"] = es_b.enter_context(nc.psum_tensor("F0", [128, 1024], F32))
    ps["F1"] = es_b.enter_context(nc.psum_tensor("F1", [128, 1024], F32))
    ps["F2"] = es_b.enter_context(nc.psum_tensor("F2", [128, 1024], F32))
    ps["PTF"] = es_b.enter_context(nc.psum_tensor("PTF", [128, 1024], BF16))
    ps["PTE"] = es_b.enter_context(nc.psum_tensor("PTE", [128, 1024], BF16))
    rot = {"i": 0}
    ep_state = {"ob_free": {0: 0, 1: 0}, "ofm_readers": {0: 0, 1: 0},
                "pte_evac": T_ss}

    wslot_free = [0] * NWB
    ring = 0

    def wload(src_dram, oc, ncols, waits_extra=()):
        nonlocal ring
        slot = ring % NWB
        ring += 1
        sem = "d_w%d" % slot
        w = [("ts", wslot_free[slot])] + list(waits_extra)
        if src_dram in CAST_GATE:
            w.append(CAST_GATE[src_dram])
        my = dma("sync", (lambda e, src_dram=src_dram, oc=oc, ncols=ncols,
                          slot=slot: e.dma_start(
            out=sb["wb%d" % slot][:, 0:ncols],
            in_=dt[src_dram][oc * 128:(oc + 1) * 128, 0:ncols])), sem, waits=w)
        return slot, my

    wc_free = [0, 0, 0]
    wc_ring = [0]

    def w3load(j):
        slot = wc_ring[0] % 3
        wc_ring[0] += 1
        sem = "d_wc%d" % slot
        w = [("ts", wc_free[slot]), ("g_cw3", G_CW3)]
        my = dma("sync", (lambda e, j=j, slot=slot: e.dma_start(
            out=sb["wc%d" % slot][:, 0:512],
            in_=dt["w3s"][j * 128:(j + 1) * 128, :])), sem, waits=w)
        return slot, my

    pfb = {k: [("vs", T_vs), ("ss", T_ss)]
           for k in ("F0a", "F0b", "F1a", "F1b", "F2a", "F2b", "PTF")}
    BANK = {
        "F0a": lambda: ps["F0"][:, 0:512], "F0b": lambda: ps["F0"][:, 512:1024],
        "F1a": lambda: ps["F1"][:, 0:512], "F1b": lambda: ps["F1"][:, 512:1024],
        "F2a": lambda: ps["F2"][:, 0:512], "F2b": lambda: ps["F2"][:, 512:1024],
    }

    x1f_readers = 0
    A_readers = 0
    B_readers = 0
    ea_free = 0
    za_free = 0
    out_dmas = 0

    def emit_x1cast(tau):
        ev = 0
        for tb in range(4):
            xcn = "xc0" if tb % 2 == 0 else "xc1"
            qt = 4 * tau + tb
            cv = comp("vector", (lambda e, qt=qt, xcn=xcn: e.tensor_copy(
                out=sb[xcn][:], in_=sb["O"][:, qt * 1024:(qt + 1) * 1024])),
                [("ts", p_free.get("xc" + str(tb % 2), 0)),
                 ("vs", x1_ready_v[qt]),
                 ("ts", T_ts if tau == 0 else 0)])
            wts = [("vs", cv), ("ss", ev if (tau > 0 or tb > 0) else SS_ATTN)]
            last = 0
            for dc in range(DC):
                fn = (lambda e, dc=dc, xcn=xcn: e.transpose(
                    ps["PTF"][:, dc * 128:(dc + 1) * 128],
                    sb[xcn][:, dc * 128:(dc + 1) * 128],
                    sb["identb"][:]))
                last = comp("tensor", fn, wts)
                wts = ()
            p_free["xc" + str(tb % 2)] = last
            ev = comp("scalar", (lambda e, tb=tb: e.copy(
                out=sb["x1f"][:].rearrange("p (n t) -> p n t", t=TO)[:, :, tb * 128:(tb + 1) * 128],
                in_=ps["PTF"][:].rearrange("p (n t) -> p n t", t=128))),
                [("ts", last), ("ts", x1f_readers)])
        return ev

    for qi in range(4):
        emit_ep(qi)
    wait("sync", [("ts", T_ts), ("vs", T_vs), ("ss", T_ss)])
    next_x1f_evac = emit_x1cast(0)
    _ep_next = [4]
    for tau in range(NFT):
        x1f_evac = next_x1f_evac
        uA_done = 0
        for oc in range(HC):
            pr = ["F0", "F1", "F2"][rot["i"] % 3]
            rot["i"] += 1
            slot1, l1 = wload("w1s", oc, DC * 128)
            slot2, l2 = wload("w2s", oc, DC * 128)
            if tau == 0:
                emit_tail_chunks(1)
                want_ep = 4 + ((oc + 1) * (NQT - 4)) // HC
                while _ep_next[0] < min(NQT, want_ep):
                    emit_tail_chunks(1)
                    emit_ep(_ep_next[0])
                    _ep_next[0] += 1
            wts = [("d_w%d" % slot1, l1), ("ss", x1f_evac)] + pfb[pr + "a"] + pfb[pr + "b"]
            lasta = lastb = 0
            for ic in range(DC):
                fn = (lambda e, pr=pr, ic=ic, slot=slot1: e.matmul(
                    ps[pr][:, 0:512],
                    sb["wb%d" % slot][:, ic * 128:(ic + 1) * 128],
                    sb["x1f"][:, ic * TO:(ic + 1) * TO],
                    start=(ic == 0), stop=(ic == DC - 1)))
                lasta = comp("tensor", fn, wts)
                wts = ()
            wts = [("d_w%d" % slot2, l2)]
            for ic in range(DC):
                fn = (lambda e, pr=pr, ic=ic, slot=slot2: e.matmul(
                    ps[pr][:, 512:1024],
                    sb["wb%d" % slot][:, ic * 128:(ic + 1) * 128],
                    sb["x1f"][:, ic * TO:(ic + 1) * TO],
                    start=(ic == 0), stop=(ic == DC - 1)))
                lastb = comp("tensor", fn, wts)
                wts = ()
            wslot_free[slot1] = lasta
            wslot_free[slot2] = lastb
            x1f_readers = lastb
            ca = comp("scalar", (lambda e, pr=pr, oc=oc: e.activation(
                out=sb["ea"][:], in_=ps[pr][:, 0:512], func=AF.Silu,
                scale=sb["s1"][:, oc:oc + 1])), [("ts", lasta), ("vs", ea_free)])
            cu = comp("vector", (lambda e, pr=pr, oc=oc: e.scalar_tensor_tensor(
                out=sb["A"][:, oc * TO:(oc + 1) * TO], in0=ps[pr][:, 512:1024],
                scalar=sb["s2"][:, oc:oc + 1], in1=sb["ea"][:],
                op0=ALU.mult, op1=ALU.mult)),
                [("ts", lastb), ("ss", ca), ("ts", A_readers if oc == 0 else 0)])
            ea_free = cu
            pfb[pr + "a"] = [("ss", ca)]
            pfb[pr + "b"] = [("vs", cu)]
            uA_done = cu

        def h_stage(src_sb, dst_sb, wname, srt, src_ready, dst_readers):
            halves = ["F2a", "F2b", "F0a", "F0b"]
            u_done = 0
            u_evs = []
            for oc in range(HC):
                bk = halves[oc % 4]
                slot, ld = wload(wname, oc, HC * 128)
                wts = [("d_w%d" % slot, ld)] + ([src_ready] if oc == 0 else []) + pfb[bk]
                last = 0
                for ic in range(HC):
                    fn = (lambda e, bk=bk, ic=ic, slot=slot, src_sb=src_sb: e.matmul(
                        BANK[bk](),
                        sb["wb%d" % slot][:, ic * 128:(ic + 1) * 128],
                        sb[src_sb][:, ic * TO:(ic + 1) * TO],
                        start=(ic == 0), stop=(ic == HC - 1)))
                    last = comp("tensor", fn, wts)
                    wts = ()
                wslot_free[slot] = last
                cu = comp("scalar", (lambda e, bk=bk, oc=oc, srt=srt, dst_sb=dst_sb: e.activation(
                    out=sb[dst_sb][:, oc * TO:(oc + 1) * TO], in_=BANK[bk](),
                    func=AF.Gelu, scale=sb[srt][:, oc:oc + 1])),
                    [("ts", last), ("ts", dst_readers if oc == 0 else 0)])
                pfb[bk] = [("ss", cu)]
                u_done = cu
                u_evs.append(cu)
            return u_done, u_evs

        if tau == 0:
            while _ep_next[0] < NQT:
                emit_tail_chunks(1)
                emit_ep(_ep_next[0])
                _ep_next[0] += 1
        uB_done, _ = h_stage("A", "Bb", "h1s", "s1h", ("vs", uA_done), B_readers)
        A_readers_save = cnt["ts"]
        if tau == 0:
            emit_tail_chunks(len(cast_tail))
        uA2_done, uA2_evs = h_stage("Bb", "A", "h2s", "s2h", ("ss", uB_done), A_readers_save)
        B_readers = cnt["ts"]
        A_readers = 0
        if tau + 1 < NFT:
            next_x1f_evac = emit_x1cast(tau + 1)

        z_done = {}
        for oh in range(2):
            FT = ["F0a", "F0b", "F1a", "F1b"] if oh == 0 else ["F0a", "F0b", "F2a", "F2b"]
            wts_f = []
            for tb in range(4):
                wts_f += pfb[FT[tb]]
            lastmm = {tb: 0 for tb in range(4)}
            for ic in range(HC):
                slot, ld = w3load(oh * HC + ic)
                wts = [("d_wc%d" % slot, ld)]
                if oh == 0:
                    wts.append(("ss", uA2_evs[ic]))
                if ic == 0:
                    wts = wts + wts_f
                for tb in range(4):
                    fn = (lambda e, bk=FT[tb], ic=ic, slot=slot, tb=tb: e.matmul(
                        BANK[bk](),
                        sb["A"][:, ic * TO + tb * 128: ic * TO + tb * 128 + 128],
                        sb["wc%d" % slot][:, 0:512],
                        start=(ic == 0), stop=(ic == HC - 1)))
                    lastmm[tb] = comp("tensor", fn, wts)
                    wts = ()
                wc_free[slot] = lastmm[3]
            A_readers = lastmm[3]
            for tb in range(4):
                qt = 4 * tau + tb
                czm = comp("vector", (lambda e, bk=FT[tb], oh=oh: e.tensor_tensor(
                    out=BANK[bk](), in0=BANK[bk](),
                    in1=sb["s3b"][:, oh * 512:(oh + 1) * 512],
                    op=ALU.mult)), [("ts", lastmm[tb])])
                cz = comp("vector", (lambda e, bk=FT[tb], qt=qt, oh=oh: e.tensor_tensor(
                    out=sb["O"][:, qt * 1024 + oh * 512: qt * 1024 + (oh + 1) * 512],
                    in0=BANK[bk](),
                    in1=sb["O"][:, qt * 1024 + oh * 512: qt * 1024 + (oh + 1) * 512],
                    op=ALU.add)), [("vs", czm)])
                pfb[FT[tb]] = [("vs", cz)]
                z_done[(tb, oh)] = cz

        for tb in range(4):
            qt = 4 * tau + tb
            cb1 = comp("vector", (lambda e, qt=qt: e.bn_stats(
                sb["bst"][:, 0:6], sb["O"][:, qt * 1024: qt * 1024 + 512])),
                [("vs", z_done[(tb, 0)]), ("vs", z_done[(tb, 1)]),
                 ("ss", ep_s), ("vs", cnt["vs"])])
            cb2 = comp("vector", (lambda e, qt=qt: e.bn_stats(
                sb["bst"][:, 6:12], sb["O"][:, qt * 1024 + 512: qt * 1024 + 1024])),
                [("vs", cb1)])
            cag = comp("vector", (lambda e: e.bn_aggr(
                sb["bmv"][:, 0:2], sb["bst"][:, 0:12])), [("vs", cb2)])
            crs0 = comp("scalar", (lambda e: e.activation(
                out=sb["mm2"][:, 0:1], in_=sb["bmv"][:, 1:2], func=AF.Sqrt,
                bias=sb["epsv"][:, 0:1])), [("vs", cag), ("ss", ep_s)])
            crs = comp("vector", (lambda e: e.reciprocal(
                out=sb["tv"][:, 0:1], in_=sb["mm2"][:, 0:1])),
                [("ss", crs0), ("vs", cnt["vs"])])
            cx1 = comp("vector", (lambda e, qt=qt: e.tensor_scalar(
                out=sb["O"][:, qt * 1024:(qt + 1) * 1024],
                in0=sb["O"][:, qt * 1024:(qt + 1) * 1024],
                scalar1=sb["bmv"][:, 0:1], scalar2=sb["tv"][:, 0:1],
                op0=ALU.subtract, op1=ALU.mult)), [("vs", crs)])
            ep_s = crs0
            out_dmas = dma("gpsimd", (lambda e, qt=qt: e.dma_start(
                out=dt["out"].rearrange("(t p) d -> p t d", p=128)[:, qt:qt + 1, :],
                in_=sb["O"][:, qt * 1024:(qt + 1) * 1024].rearrange(
                    "p (t d) -> p t d", d=1024))),
                "d_out", waits=[("vs", cx1)])

    wait("sync", [("d_out", dcnt["d_out"])])

    with es_top:
      with es_wt:
        with es_b:
            with nc.Block() as block:
                @block.sync
                def _(e):
                    for f in streams["sync"]:
                        f(e)

                @block.tensor
                def _(e):
                    for f in streams["tensor"]:
                        f(e)

                @block.vector
                def _(e):
                    for f in streams["vector"]:
                        f(e)

                @block.scalar
                def _(e):
                    for f in streams["scalar"]:
                        f(e)

                @block.gpsimd
                def _(e):
                    for f in streams["gpsimd"]:
                        f(e)

    n_inst = sum(len(v) for v in streams.values())
    return nc, n_inst



def _pack_oc(wt, n_ic, n_oc, ocw=128):
    return np.ascontiguousarray(
        wt.reshape(n_ic, 128, n_oc, ocw).transpose(2, 0, 1, 3).reshape(n_oc * n_ic * 128, ocw))


def _pack_pc(wt, n_ic, n_oc):
    return np.ascontiguousarray(
        wt.reshape(n_ic, 128, n_oc, 128).transpose(2, 1, 0, 3).reshape(n_oc * 128, n_ic * 128))


def _stage(cfg: Cfg, inputs):
    D, DC, HC = cfg.D, cfg.DC, cfg.HC
    x = np.asarray(inputs["L0_attn_x"], dtype=np.float32)
    W = np.asarray(inputs["L0_attn_W"], dtype=np.float32)
    B = x.shape[0]
    shared = {}
    shared["wattT"] = np.ascontiguousarray(W.T)
    shared["wattK"] = np.ascontiguousarray(W[:1024, :])
    w1t = np.ascontiguousarray(np.asarray(inputs["L0_ffn_w1_w"], dtype=np.float32).T)
    w2t = np.ascontiguousarray(np.asarray(inputs["L0_ffn_w2_w"], dtype=np.float32).T)
    h1t = np.ascontiguousarray(np.asarray(inputs["L0_ffn_h1_w"], dtype=np.float32).T)
    h2t = np.ascontiguousarray(np.asarray(inputs["L0_ffn_h2_w"], dtype=np.float32).T)
    w3t = np.ascontiguousarray(np.asarray(inputs["L0_ffn_w3_w"], dtype=np.float32).T)
    shared["w1tp"] = _pack_pc(w1t, DC, HC)
    shared["w2tp"] = _pack_pc(w2t, DC, HC)
    shared["h1tp"] = _pack_pc(h1t, HC, HC)
    shared["h2tp"] = _pack_pc(h2t, HC, HC)
    shared["w3tp"] = _pack_oc(w3t, HC, 2, 512)
    for nm, key in (("s1", "L0_ffn_w1_s"), ("s2", "L0_ffn_w2_s"),
                    ("s1h", "L0_ffn_h1_s"), ("s2h", "L0_ffn_h2_s")):
        s = np.asarray(inputs[key], dtype=np.float32)
        shared[nm] = np.ascontiguousarray(s.reshape(HC, 128).T)
    shared["s3b"] = np.ascontiguousarray(
        np.tile(np.asarray(inputs["L0_ffn_w3_s"], dtype=np.float32)[None, :], (128, 1)))
    shared["uio"] = np.ascontiguousarray(
        (np.arange(1024, dtype=np.float32)[None, :]
         - np.arange(128, dtype=np.float32)[:, None]).astype(np.float16))
    shared["identb"] = np.eye(128, dtype=ml_dtypes.bfloat16)

    xfm_all = [np.ascontiguousarray(x[b].T) for b in range(B)]
    per_seq = cfg.n_cores // B
    in_maps = []
    for c in range(cfg.n_cores):
        seq = c // per_seq
        r = c % per_seq
        rows = [(cfg.cps * p + r) * 128 for p in range(cfg.NQT)]
        m = dict(shared)
        m["xfm"] = xfm_all[seq]
        m["xraw"] = x[seq]
        m["xq"] = np.ascontiguousarray(np.concatenate(
            [xfm_all[seq][:, q0:q0 + 128] for q0 in rows], axis=1))
        m["xtm"] = np.ascontiguousarray(np.concatenate(
            [x[seq][q0:q0 + 128, :] for q0 in rows], axis=0))
        th = np.zeros((128, cfg.NVIS), dtype=np.float32)
        v = 0
        for kt in range(cfg.NKT):
            for s in cfg.slots[kt]:
                th[:, v] = float(rows[s] - kt * 1024)
                v += 1
        m["theta"] = th
        in_maps.append(m)
    return in_maps


def run_cfg(cfg: Cfg, inputs, runner=None):
    in_maps = _stage(cfg, inputs)
    nc, n_inst = build_program(cfg)
    if runner is None:
        res = run_bass_kernel_spmd(nc, in_maps, core_ids=list(range(cfg.n_cores)))
        results = res.results
    else:
        results = runner(nc, in_maps)
    x = np.asarray(inputs["L0_attn_x"])
    B, S, D = x.shape
    out = np.empty((B, S, D), dtype=np.float32)
    per_seq = cfg.n_cores // B
    for c in range(cfg.n_cores):
        seq = c // per_seq
        r = c % per_seq
        res = results[c]["out"]
        for p in range(cfg.NQT):
            q0 = (cfg.cps * p + r) * 128
            out[seq, q0:q0 + 128, :] = res[p * 128:(p + 1) * 128]
    return out


def kernel(**inputs):
    return run_cfg(CFG_FULL, inputs)


# version 22
# speedup vs baseline: 1.0173x; 1.0062x over previous
import numpy as np
from contextlib import ExitStack
import ml_dtypes

import concourse.bass as bass
import concourse.mybir as mybir
from concourse.bass_utils import run_bass_kernel_spmd

F32 = mybir.dt.float32
BF16 = mybir.dt.bfloat16
AF = mybir.ActivationFunctionType
ALU = mybir.AluOpType
AX = mybir.AxisListType

NEG_BIG = -30000.0


class Cfg:
    def __init__(self, seq=8192, qpc=2048, n_cores=8):
        self.D = 1024
        self.DH = 4096
        self.DC = 8
        self.HC = 32
        self.seq = seq
        self.qpc = qpc
        self.NKT = seq // 1024
        self.NQT = qpc // 128
        self.TO = 512
        self.NFT = qpc // 512
        self.n_cores = n_cores
        self.cps = n_cores // 2
        self.G = 8 // self.cps
        self.slots = [[p for p in range(self.NQT) if p // self.G >= kt]
                      for kt in range(self.NKT)]
        self.NVIS = sum(len(s) for s in self.slots)


CFG_FULL = Cfg(8192, 2048, 8)
CFG_MINI = Cfg(4096, 2048, 4)


def build_program(cfg: Cfg):
    D, DH, DC, HC = cfg.D, cfg.DH, cfg.DC, cfg.HC
    NKT, NQT, TO, NFT = cfg.NKT, cfg.NQT, cfg.TO, cfg.NFT
    QPC, SEQ = cfg.qpc, cfg.seq
    NVIS = cfg.NVIS
    SLOTS = cfg.slots

    nc = bass.Bass("TRN2", target_bir_lowering=False, debug=False)

    dt = {}
    dt["xfm"] = nc.dram_tensor("xfm", [D, SEQ], F32, kind="ExternalInput").ap()
    dt["xq"] = nc.dram_tensor("xq", [D, QPC], F32, kind="ExternalInput").ap()
    dt["xtm"] = nc.dram_tensor("xtm", [QPC, D], F32, kind="ExternalInput").ap()
    dt["wattT"] = nc.dram_tensor("wattT", [D, 2 * D], F32, kind="ExternalInput").ap()
    dt["wattK"] = nc.dram_tensor("wattK", [D, D], F32, kind="ExternalInput").ap()
    dt["xraw"] = nc.dram_tensor("xraw", [SEQ, D], F32, kind="ExternalInput").ap()
    dt["w1tp"] = nc.dram_tensor("w1tp", [HC * 128, DC * 128], F32, kind="ExternalInput").ap()
    dt["w2tp"] = nc.dram_tensor("w2tp", [HC * 128, DC * 128], F32, kind="ExternalInput").ap()
    dt["h1tp"] = nc.dram_tensor("h1tp", [HC * 128, HC * 128], F32, kind="ExternalInput").ap()
    dt["h2tp"] = nc.dram_tensor("h2tp", [HC * 128, HC * 128], F32, kind="ExternalInput").ap()
    dt["w3tp"] = nc.dram_tensor("w3tp", [2 * HC * 128, 512], F32, kind="ExternalInput").ap()
    for s in ("s1", "s2", "s1h", "s2h"):
        dt[s] = nc.dram_tensor(s, [128, HC], F32, kind="ExternalInput").ap()
    dt["s3b"] = nc.dram_tensor("s3b", [128, D], F32, kind="ExternalInput").ap()
    dt["theta"] = nc.dram_tensor("theta", [128, NVIS], F32, kind="ExternalInput").ap()
    dt["uio"] = nc.dram_tensor("uio", [128, 1024], mybir.dt.float16, kind="ExternalInput").ap()
    dt["identb"] = nc.dram_tensor("identb", [128, 128], BF16, kind="ExternalInput").ap()
    dt["out"] = nc.dram_tensor("out", [QPC, D], F32, kind="ExternalOutput").ap()
    dt["w1s"] = nc.dram_tensor("w1s", [HC * 128, DC * 128], BF16, kind="Internal").ap()
    dt["w2s"] = nc.dram_tensor("w2s", [HC * 128, DC * 128], BF16, kind="Internal").ap()
    dt["h1s"] = nc.dram_tensor("h1s", [HC * 128, HC * 128], BF16, kind="Internal").ap()
    dt["h2s"] = nc.dram_tensor("h2s", [HC * 128, HC * 128], BF16, kind="Internal").ap()
    dt["w3s"] = nc.dram_tensor("w3s", [2 * HC * 128, 512], BF16, kind="Internal").ap()

    sem_names = ["ts", "vs", "ss", "gs",
                 "d_const", "g_xk0", "g_xk1", "g_q0", "g_q1", "d_xtm0", "d_xtm1", "d_out",
                 "g_wt", "g_wk", "g_xt0", "g_xt1", "g_cw1", "g_cw2", "g_cw3", "g_ch1", "g_ch2",
                 "d_w0", "d_w1", "d_w2", "d_w3",
                 "d_wc0", "d_wc1", "d_wc2"]
    es_top = ExitStack()
    SEM = {n: es_top.enter_context(nc.semaphore(n)) for n in sem_names}

    sb = {}

    def sbuf(name, shape, dtp, stack):
        sb[name] = stack.enter_context(nc.sbuf_tensor("sb_" + name, shape, dtp))
        return sb[name]

    sbuf("O", [128, NQT * 1024], F32, es_top)
    for s in ("s1", "s2", "s1h", "s2h"):
        sbuf(s, [128, HC], F32, es_top)
    sbuf("s3b", [128, D], F32, es_top)
    sbuf("identb", [128, 128], BF16, es_top)
    sbuf("lrun", [128, NQT], F32, es_top)
    for s in ("rh", "s1v", "s2v", "nmean", "mm2", "tv", "epsv"):
        sbuf(s, [128, 1], F32, es_top)
    sbuf("bst", [128, 12], F32, es_top)
    sbuf("bmv", [128, 2], F32, es_top)
    sbuf("late8", [128, 2048], F32, es_top)

    streams = {"sync": [], "tensor": [], "vector": [], "scalar": [], "gpsimd": []}
    cnt = {"ts": 0, "vs": 0, "ss": 0, "gs": 0}
    dcnt = {n: 0 for n in sem_names if n not in ("ts", "vs", "ss", "gs")}

    SELF = {"tensor": "ts", "vector": "vs", "scalar": "ss", "gpsimd": "gs"}

    def comp(eng, build_fn, waits=()):
        s = SELF[eng]
        cnt[s] += 1
        my = cnt[s]
        w = tuple((a, b) for (a, b) in waits if b > 0)

        def f(e, build_fn=build_fn, w=w, s=s):
            for sname, v in w:
                e.wait_ge(SEM[sname], v)
            build_fn(e).then_inc(SEM[s], 1)
        streams[eng].append(f)
        return my

    def dma(eng, build_fn, sem_name, waits=()):
        dcnt[sem_name] += 16
        my = dcnt[sem_name]
        w = tuple((a, b) for (a, b) in waits if b > 0)

        def f(e, build_fn=build_fn, w=w, sem_name=sem_name):
            for sname, v in w:
                e.wait_ge(SEM[sname], v)
            build_fn(e).then_inc(SEM[sem_name], 16)
        streams[eng].append(f)
        return my

    def wait(eng, waits):
        w = tuple((a, b) for (a, b) in waits if b > 0)

        def f(e, w=w):
            for sname, v in w:
                e.wait_ge(SEM[sname], v)
        streams[eng].append(f)

    es_wt = ExitStack()
    sbuf("WTv", [128, DC * 1024], BF16, es_wt)
    es_a = ExitStack()
    sbuf("QP", [128, DC * QPC], BF16, es_a)
    sbuf("xk0", [128, DC * 1024], BF16, es_a)
    sbuf("XT0", [128, DC * 1024], BF16, es_a)
    sbuf("P0", [128, 1024], BF16, es_a)
    sbuf("P1", [128, 1024], BF16, es_a)
    sbuf("PT0", [128, 1024], BF16, es_a)
    sbuf("PT1", [128, 1024], BF16, es_a)
    sbuf("uio", [128, 1024], mybir.dt.float16, es_a)
    sbuf("theta", [128, NVIS], F32, es_a)
    sbuf("msk", [128, 1024], BF16, es_a)
    sbuf("negm", [128, NQT], F32, es_a)
    sbuf("mpos", [128, NQT], F32, es_a)
    sbuf("alpha", [128, 4], F32, es_a)
    sbuf("lpart", [128, 4], F32, es_a)
    sbuf("nmp", [128, 1], F32, es_a)

    for name in ("uio", "theta", "s1", "s2", "s1h", "s2h", "s3b", "identb"):
        dma("sync", (lambda e, n=name: e.dma_start(out=sb[n][:], in_=dt[n])), "d_const")
    C_CONST = dcnt["d_const"]

    comp("vector", lambda e: e.memset(sb["epsv"][:], 1e-5))
    es_q = ExitStack()
    sbuf("Qq0", [128, DC * 512], BF16, es_q)
    sbuf("Qq1", [128, DC * 512], BF16, es_q)
    sbuf("WK", [128, DC * D], BF16, es_q)
    ps = {}
    ps["S0"] = es_a.enter_context(nc.psum_tensor("S0", [128, 1024], F32))
    ps["S1"] = es_a.enter_context(nc.psum_tensor("S1", [128, 1024], F32))
    ps["PV"] = es_a.enter_context(nc.psum_tensor("PV", [128, 1024], F32))
    ps["PT0"] = es_a.enter_context(nc.psum_tensor("PT0p", [128, 1024], BF16))
    ps["PT1"] = es_a.enter_context(nc.psum_tensor("PT1p", [128, 1024], BF16))

    dma("gpsimd", (lambda e: e.dma_start(
        out=sb["WK"][:].rearrange("p (dc c) -> p dc c", c=D),
        in_=dt["wattK"].rearrange("(dc p) c -> p dc c", p=128))), "g_wk")
    dma("gpsimd", (lambda e: e.dma_start(
        out=sb["Qq0"][:].rearrange("p (ic c) -> p ic c", c=512),
        in_=dt["xq"][:, 0:512].rearrange("(ic p) c -> p ic c", p=128))), "g_q0")
    G_WK = dcnt["g_wk"]

    cast_chunks = []

    def cast_rows(src, dst, rows, cols, sem):
        step = max(128, (2 * 1024 * 1024) // (cols * 4))
        n = 0
        for r0 in range(0, rows, step):
            r1 = min(rows, r0 + step)
            cast_chunks.append((src, dst, r0, r1, sem))
            n += 16
        return n

    G_CW1 = cast_rows("w1tp", "w1s", HC * 128, DC * 128, "g_cw1")
    G_CW2 = cast_rows("w2tp", "w2s", HC * 128, DC * 128, "g_cw2")
    G_CW3 = cast_rows("w3tp", "w3s", 2 * HC * 128, 512, "g_cw3")
    G_CH1 = cast_rows("h1tp", "h1s", HC * 128, HC * 128, "g_ch1")
    G_CH2 = cast_rows("h2tp", "h2s", HC * 128, HC * 128, "g_ch2")
    N_TAIL = 0
    cast_tail = cast_chunks[len(cast_chunks) - N_TAIL:]
    del cast_chunks[len(cast_chunks) - N_TAIL:]
    CAST_GATE = {"w1s": ("g_cw1", G_CW1), "w2s": ("g_cw2", G_CW2),
                 "h1s": ("g_ch1", G_CH1), "h2s": ("g_ch2", G_CH2)}
    _cast_i = [0]
    _tail_i = [0]

    def _emit_chunk(tup):
        s_, d_, r0, r1, sem = tup
        dma("gpsimd", (lambda e, r0=r0, r1=r1, s_=s_, d_=d_: e.dma_start(
            out=dt[d_][r0:r1, :], in_=dt[s_][r0:r1, :])), sem)

    def emit_cast_chunks(n):
        bound = min(len(cast_chunks), _cast_i[0] + n)
        while _cast_i[0] < bound:
            _emit_chunk(cast_chunks[_cast_i[0]])
            _cast_i[0] += 1

    def emit_tail_chunks(n):
        bound = min(len(cast_tail), _tail_i[0] + n)
        while _tail_i[0] < bound:
            _emit_chunk(cast_tail[_tail_i[0]])
            _tail_i[0] += 1

    last_pv_mm = 0
    pair_free = {"S0": [], "S1": [], "PV": []}
    pt_free = {0: 0, 1: 0}
    p_free = {0: 0, 1: 0}
    xk_war = {0: 0, 1: 0}
    xt_war = {0: 0, 1: 0}
    xk_done = {}
    xt_done = {}

    vS, vMadd, vNegm, vExp, vAlpha, vPT, vPTe, vPV, vO, vL = ({} for _ in range(10))
    _vctr = [0]
    _vmap = {}

    def vid(kt, qi):
        key = (kt, qi)
        if key not in _vmap:
            _vmap[key] = _vctr[0]
            _vctr[0] += 1
        return _vmap[key]

    def xk_load_cast(kt, qp_gate=0):
        buf = "xk%d" % (kt % 2)
        wts = []
        if xk_war[kt % 2]:
            wts.append(("ts", xk_war[kt % 2]))
        if qp_gate:
            wts.append(("ts", qp_gate))
        xk_done[kt] = dma("gpsimd", (lambda e, kt=kt, buf=buf: e.dma_start(
            out=sb[buf][:].rearrange("p (ic c) -> p ic c", c=1024),
            in_=dt["xfm"][:, kt * 1024:(kt + 1) * 1024].rearrange(
                "(ic p) c -> p ic c", p=128))),
            "g_xk%d" % (kt % 2), waits=wts)

    def xt_load_cast(kt, qp_gate=0):
        buf = "XT%d" % (kt % 2)
        wts = []
        if xt_war[kt % 2]:
            wts.append(("ts", xt_war[kt % 2]))
        if qp_gate:
            wts.append(("ts", qp_gate))
        xt_done[kt] = dma("gpsimd", (lambda e, kt=kt, buf=buf: e.dma_start(
            out=sb[buf][:].rearrange("p (kc c) -> p kc c", c=1024),
            in_=dt["xraw"][kt * 1024:(kt + 1) * 1024, :].rearrange(
                "(kc p) c -> p kc c", p=128))),
            "g_xt%d" % (kt % 2), waits=wts)

    def emit_qprime():
        halves = [("S0", 0), ("S0", 512), ("S1", 0), ("S1", 512)]
        hfree = [[], [], [], []]
        bank_last = {"S0": 0, "S1": 0}
        q_last_mm = {}
        ev = 0
        for q in range(QPC // 512):
            qb = "Qq%d" % (q % 2)
            if q >= 2:
                dma("gpsimd", (lambda e, q=q, qb=qb: e.dma_start(
                    out=sb[qb][:].rearrange("p (ic c) -> p ic c", c=512),
                    in_=dt["xq"][:, q * 512:(q + 1) * 512].rearrange(
                        "(ic p) c -> p ic c", p=128))),
                    "g_q%d" % (q % 2), waits=[("ts", q_last_mm[q - 2])])
            for ib in range(DC):
                hi = (q * DC + ib) % 4
                pr, off = halves[hi]
                wts0 = list(hfree[hi])
                last = 0
                for dc in range(DC):
                    wts = wts0
                    wts0 = []
                    if ib == 0 and dc == 0:
                        wts = wts + [("g_q%d" % (q % 2), 16 * (q // 2 + 1))]
                        if q == 0:
                            wts = wts + [("g_wk", G_WK)]
                    fn = (lambda e, pr=pr, off=off, dc=dc, ib=ib, qb=qb: e.matmul(
                        ps[pr][:, off:off + 512],
                        sb["WK"][:, dc * D + ib * 128: dc * D + ib * 128 + 128],
                        sb[qb][:, dc * 512:(dc + 1) * 512],
                        start=(dc == 0), stop=(dc == DC - 1)))
                    last = comp("tensor", fn, wts)
                ev = comp("scalar", (lambda e, pr=pr, off=off, ib=ib, q=q: e.copy(
                    out=sb["QP"][:, ib * QPC + q * 512: ib * QPC + (q + 1) * 512],
                    in_=ps[pr][:, off:off + 512])), [("ts", last)])
                hfree[hi] = [("ss", ev)]
                bank_last[pr] = ev
            q_last_mm[q] = cnt["ts"]
        pair_free["S0"] = [("ss", bank_last["S0"])]
        pair_free["S1"] = [("ss", bank_last["S1"])]
        return ev

    def emit_S(kt, qi, first_of_kt):
        v = vid(kt, qi)
        buf = "xk%d" % (kt % 2)
        pr = "S0" if qi % 2 == 0 else "S1"
        wts = list(pair_free[pr])
        if first_of_kt:
            wts.append(("g_xk%d" % (kt % 2), xk_done[kt]))
        if kt == 0 and qi == SLOTS[0][0]:
            wts += [("ss", QP_DONE), ("d_const", C_CONST)]
        lastmm = 0
        for ns in range(2):
            for ic in range(DC):
                fn = (lambda e, pr=pr, ns=ns, ic=ic, qi=qi, buf=buf: e.matmul(
                    ps[pr][:, ns * 512:(ns + 1) * 512],
                    sb["QP"][:, ic * QPC + qi * 128: ic * QPC + qi * 128 + 128],
                    sb[buf][:, ic * 1024 + ns * 512: ic * 1024 + (ns + 1) * 512],
                    start=(ic == 0), stop=(ic == DC - 1)))
                lastmm = comp("tensor", fn, wts)
                wts = ()
        vS[v] = lastmm
        xk_war[kt % 2] = lastmm
        pair_free[pr] = []

    def emit_mask(kt, qi):
        v = vid(kt, qi)
        pr = "S0" if qi % 2 == 0 else "S1"
        c1 = comp("vector", (lambda e, v=v: e.tensor_scalar(
            out=sb["msk"][:], in0=sb["uio"][:],
            scalar1=sb["theta"][:, v:v + 1], scalar2=NEG_BIG,
            op0=ALU.is_gt, op1=ALU.mult)),
            [("vs", cnt["vs"]), ("d_const", C_CONST if v == 0 else 0)])
        c2 = comp("vector", (lambda e, pr=pr: e.tensor_tensor(
            out=ps[pr][:, :], in0=ps[pr][:, :], in1=sb["msk"][:], op=ALU.add)),
            [("ts", vS[v]), ("vs", c1)])
        vMadd[v] = c2

    def emit_negm(kt, qi):
        v = vid(kt, qi)
        pr = "S0" if qi % 2 == 0 else "S1"
        c1 = comp("vector", (lambda e, pr=pr: e.reduce_max(
            sb["nmp"][:, 0:1], ps[pr][:, :], axis=AX.X, negate=True)),
            [("vs", vMadd[v])])
        if kt == 0:
            c2 = comp("vector", (lambda e, qi=qi: e.tensor_copy(
                out=sb["negm"][:, qi:qi + 1], in_=sb["nmp"][:, 0:1])), [("vs", c1)])
        else:
            c2 = comp("vector", (lambda e, qi=qi: e.tensor_tensor(
                out=sb["negm"][:, qi:qi + 1], in0=sb["negm"][:, qi:qi + 1],
                in1=sb["nmp"][:, 0:1], op=ALU.min)), [("vs", c1)])
        vNegm[v] = c2

    def emit_exp(kt, qi):
        v = vid(kt, qi)
        pr = "S0" if qi % 2 == 0 else "S1"
        par = qi % 2
        sl = v % 4
        wts = [("vs", vNegm[v]), ("vs", vL.get(v - 4, 0))]
        if kt > 0:
            ca = comp("scalar", (lambda e, qi=qi, sl=sl: e.activation(
                out=sb["alpha"][:, sl:sl + 1], in_=sb["mpos"][:, qi:qi + 1],
                func=AF.Exp, bias=sb["negm"][:, qi:qi + 1])), wts)
            vAlpha[v] = ca
            wts = [("ss", ca)]
        else:
            vAlpha[v] = 0
        ce = comp("scalar", (lambda e, pr=pr, par=par, qi=qi, sl=sl: e.activation(
            out=sb["P%d" % par][:], in_=ps[pr][:, :], func=AF.Exp,
            bias=sb["negm"][:, qi:qi + 1], accum_out=sb["lpart"][:, sl:sl + 1])),
            [("ts", p_free[par])] + wts)
        cm = comp("scalar", (lambda e, qi=qi: e.activation(
            out=sb["mpos"][:, qi:qi + 1], in_=sb["negm"][:, qi:qi + 1],
            func=AF.Copy, scale=-1.0)), [("ss", ce)])
        vExp[v] = ce
        pair_free[pr] = [("ss", ce)]

    def emit_PT(kt, qi):
        v = vid(kt, qi)
        par = qi % 2
        wts = [("ss", vExp[v]), ("ss", pt_free[par])]
        last = 0
        for kc in range(DC):
            fn = (lambda e, par=par, kc=kc: e.transpose(
                ps["PT%d" % par][:, kc * 128:(kc + 1) * 128],
                sb["P%d" % par][:, kc * 128:(kc + 1) * 128],
                sb["identb"][:]))
            last = comp("tensor", fn, wts)
            wts = ()
        vPT[v] = last
        p_free[par] = last

    def emit_PTevac(kt, qi):
        v = vid(kt, qi)
        par = qi % 2
        c1 = comp("scalar", (lambda e, par=par: e.copy(
            out=sb["PT%d" % par][:], in_=ps["PT%d" % par][:, :])),
            [("ts", vPT[v])])
        vPTe[v] = c1
        pt_free[par] = c1

    def emit_PV(kt, qi, first_of_kt):
        nonlocal last_pv_mm
        v = vid(kt, qi)
        par = qi % 2
        buf = "XT%d" % (kt % 2)
        wts = [("ss", vPTe[v])] + pair_free["PV"]
        if first_of_kt:
            wts.append(("g_xt%d" % (kt % 2), xt_done[kt]))
        last = 0
        for ns in range(2):
            for kc in range(DC):
                fn = (lambda e, par=par, ns=ns, kc=kc, buf=buf: e.matmul(
                    ps["PV"][:, ns * 512:(ns + 1) * 512],
                    sb["PT%d" % par][:, kc * 128:(kc + 1) * 128],
                    sb[buf][:, kc * 1024 + ns * 512: kc * 1024 + (ns + 1) * 512],
                    start=(kc == 0), stop=(kc == DC - 1)))
                last = comp("tensor", fn, wts)
                wts = ()
        vPV[v] = last
        last_pv_mm = last
        xt_war[kt % 2] = last
        pair_free["PV"] = []

    def emit_O(kt, qi):
        v = vid(kt, qi)
        sl = v % 4
        if kt == 0:
            c1 = comp("vector", (lambda e, qi=qi: e.tensor_copy(
                out=sb["O"][:, qi * 1024:(qi + 1) * 1024], in_=ps["PV"][:, :])),
                [("ts", vPV[v])])
            cl = comp("vector", (lambda e, qi=qi, sl=sl: e.tensor_copy(
                out=sb["lrun"][:, qi:qi + 1], in_=sb["lpart"][:, sl:sl + 1])),
                [("ss", vExp[v])])
        else:
            c1 = comp("vector", (lambda e, qi=qi, sl=sl: e.scalar_tensor_tensor(
                out=sb["O"][:, qi * 1024:(qi + 1) * 1024],
                in0=sb["O"][:, qi * 1024:(qi + 1) * 1024],
                scalar=sb["alpha"][:, sl:sl + 1], in1=ps["PV"][:, :],
                op0=ALU.mult, op1=ALU.add)),
                [("ts", vPV[v]), ("ss", vAlpha[v])])
            cl = comp("vector", (lambda e, qi=qi, sl=sl: e.scalar_tensor_tensor(
                out=sb["lrun"][:, qi:qi + 1], in0=sb["lrun"][:, qi:qi + 1],
                scalar=sb["alpha"][:, sl:sl + 1], in1=sb["lpart"][:, sl:sl + 1],
                op0=ALU.mult, op1=ALU.add)), [("ss", vExp[v])])
        vO[v] = c1
        vL[v] = cl
        pair_free["PV"] = [("vs", c1)]

    n_chunks = len(cast_chunks)
    xk_load_cast(0)
    dma("gpsimd", (lambda e: e.dma_start(
        out=sb["Qq1"][:].rearrange("p (ic c) -> p ic c", c=512),
        in_=dt["xq"][:, 512:1024].rearrange("(ic p) c -> p ic c", p=128))), "g_q1")
    xt_load_cast(0)
    QP_DONE = emit_qprime()
    qp_last_mm = cnt["ts"]
    dma("gpsimd", (lambda e: e.dma_start(
        out=sb["WTv"][:].rearrange("p (ic c) -> p ic c", c=1024),
        in_=dt["wattT"][:, D:2 * D].rearrange("(ic p) c -> p ic c", p=128))), "g_wt")
    G_WT = dcnt["g_wt"]
    es_q.close()
    sbuf("xk1", [128, DC * 1024], BF16, es_a)
    sbuf("XT1", [128, DC * 1024], BF16, es_a)
    vis_i = 0
    want_prev = 0
    for kt in range(NKT):
        if kt + 1 < NKT:
            xk_load_cast(kt + 1, qp_gate=qp_last_mm if kt == 0 else 0)
            xt_load_cast(kt + 1, qp_gate=qp_last_mm if kt == 0 else 0)
        sl = SLOTS[kt]
        nv = len(sl)
        for k in range(nv + 2):
            if k < nv:
                want = min(n_chunks, int(n_chunks * (vis_i + 1) / NVIS))
                emit_cast_chunks(want - want_prev)
                want_prev = want
                vis_i += 1
                emit_S(kt, sl[k], first_of_kt=(k == 0))
                emit_mask(kt, sl[k])
                emit_negm(kt, sl[k])
            if 1 <= k <= nv:
                emit_exp(kt, sl[k - 1])
                emit_PT(kt, sl[k - 1])
            if 2 <= k <= nv + 1:
                emit_PTevac(kt, sl[k - 2])
                emit_PV(kt, sl[k - 2], first_of_kt=(k == 2))
                emit_O(kt, sl[k - 2])

    emit_cast_chunks(len(cast_chunks))

    INV_D = 1.0 / D
    xtm_slot_free = [0, 0]
    x1_ready_v = {}
    ep_s = 0
    ep_v = 0

    def emit_ep(qi):
        nonlocal ep_s, ep_v
        slot = qi % 2
        ld = dma("gpsimd", (lambda e, qi=qi, slot=slot: e.dma_start(
            out=sb["late8"][:, slot * 1024:(slot + 1) * 1024],
            in_=dt["xtm"][qi * 128:(qi + 1) * 128, :])),
            "d_xtm%d" % slot, waits=[("vs", xtm_slot_free[slot])])
        lastkt = qi // cfg.G
        par = 0
        cr0 = comp("vector", (lambda e, qi=qi: e.tensor_scalar_mul(
            out=sb["rh"][:, 0:1], in0=sb["lrun"][:, qi:qi + 1], scalar1=2.0)),
            [("vs", vL[vid(lastkt, qi)]), ("ss", ep_s), ("vs", cnt["vs"])])
        cr = comp("vector", (lambda e: e.reciprocal(
            out=sb["rh"][:, 0:1], in_=sb["rh"][:, 0:1])), [("vs", cr0)])
        cnorm = comp("vector", (lambda e, qi=qi, par=par: e.tensor_scalar_mul(
            out=sb["Ob%d" % par][:], in0=sb["O"][:, qi * 1024:(qi + 1) * 1024],
            scalar1=sb["rh"][:, 0:1])),
            [("vs", cr), ("vs", vO[vid(lastkt, qi)]),
             ("ts", ep_state["ob_free"][par]),
             ("ts", T_ts if qi == 0 else 0)])
        wts = [("vs", cnorm), ("ss", ep_state["pte_evac"])]
        last = 0
        for ic in range(DC):
            fn = (lambda e, par=par, ic=ic: e.transpose(
                ps["PTE"][:, ic * 128:(ic + 1) * 128],
                sb["Ob%d" % par][:, ic * 128:(ic + 1) * 128],
                sb["identb"][:]))
            last = comp("tensor", fn, wts)
            wts = ()
        ep_state["ob_free"][par] = last
        pev = comp("scalar", (lambda e, par=par: e.copy(
            out=sb["Ofm%d" % par][:], in_=ps["PTE"][:, :])),
            [("ts", last), ("ts", ep_state["ofm_readers"][par])])
        ep_state["pte_evac"] = pev
        pr = ["F0", "F1", "F2"][rot["i"] % 3]
        rot["i"] += 1
        wts = [("ss", pev)] + pfb[pr + "a"] + pfb[pr + "b"]
        if qi == 0:
            wts.append(("g_wt", G_WT))
        lastp = 0
        for ns in range(2):
            for ic in range(DC):
                fn = (lambda e, pr=pr, par=par, ns=ns, ic=ic: e.matmul(
                    ps[pr][:, ns * 512:(ns + 1) * 512],
                    sb["Ofm%d" % par][:, ic * 128:(ic + 1) * 128],
                    sb["WTv"][:, ic * 1024 + ns * 512: ic * 1024 + (ns + 1) * 512],
                    start=(ic == 0), stop=(ic == DC - 1)))
                lastp = comp("tensor", fn, wts)
                wts = ()
        ep_state["ofm_readers"][par] = lastp
        cy = comp("vector", (lambda e, qi=qi, slot=slot, pr=pr: e.tensor_tensor(
            out=sb["O"][:, qi * 1024:(qi + 1) * 1024],
            in0=ps[pr][:, 0:1024],
            in1=sb["late8"][:, slot * 1024:(slot + 1) * 1024], op=ALU.add)),
            [("ts", lastp), ("d_xtm%d" % slot, ld), ("vs", cnorm)])
        pfb[pr + "a"] = [("vs", cy)]
        pfb[pr + "b"] = [("vs", cy)]
        xtm_slot_free[slot] = cy
        cb1 = comp("vector", (lambda e, qi=qi: e.bn_stats(
            sb["bst"][:, 0:6], sb["O"][:, qi * 1024: qi * 1024 + 512])),
            [("vs", cy), ("ss", ep_s), ("vs", cnt["vs"])])
        cb2 = comp("vector", (lambda e, qi=qi: e.bn_stats(
            sb["bst"][:, 6:12], sb["O"][:, qi * 1024 + 512: qi * 1024 + 1024])),
            [("vs", cb1)])
        cag = comp("vector", (lambda e: e.bn_aggr(
            sb["bmv"][:, 0:2], sb["bst"][:, 0:12])), [("vs", cb2)])
        crs0 = comp("scalar", (lambda e: e.activation(
            out=sb["mm2"][:, 0:1], in_=sb["bmv"][:, 1:2], func=AF.Sqrt,
            bias=sb["epsv"][:, 0:1])), [("vs", cag), ("ss", ep_s)])
        crs = comp("vector", (lambda e: e.reciprocal(
            out=sb["tv"][:, 0:1], in_=sb["mm2"][:, 0:1])),
            [("ss", crs0), ("vs", cnt["vs"])])
        cx1 = comp("vector", (lambda e, qi=qi: e.tensor_scalar(
            out=sb["O"][:, qi * 1024:(qi + 1) * 1024],
            in0=sb["O"][:, qi * 1024:(qi + 1) * 1024],
            scalar1=sb["bmv"][:, 0:1], scalar2=sb["tv"][:, 0:1],
            op0=ALU.subtract, op1=ALU.mult)), [("vs", crs)])
        ep_s = crs0
        ep_v = cx1
        x1_ready_v[qi] = cx1

    T_ts, T_vs, T_ss = cnt["ts"], cnt["vs"], cnt["ss"]
    SS_ATTN = cnt["ss"]
    es_a.close()

    es_b = ExitStack()
    sbuf("A", [128, HC * TO], BF16, es_b)
    sbuf("Bb", [128, HC * TO], BF16, es_b)
    sbuf("x1f", [128, DC * TO], BF16, es_b)
    NWB = 3
    for i in range(NWB):
        sbuf("wb%d" % i, [128, HC * 128], BF16, es_b)
    sbuf("wc0", [128, 1024], BF16, es_b)
    sbuf("wc1", [128, 1024], BF16, es_b)
    sbuf("wc2", [128, 1024], BF16, es_b)
    sbuf("xc0", [128, 1024], BF16, es_b)
    sbuf("xc1", [128, 1024], BF16, es_b)
    sbuf("ea", [128, TO], F32, es_b)
    sbuf("Ob0", [128, 1024], BF16, es_b)
    sbuf("Ofm0", [128, 1024], BF16, es_b)
    ps["F0"] = es_b.enter_context(nc.psum_tensor("F0", [128, 1024], F32))
    ps["F1"] = es_b.enter_context(nc.psum_tensor("F1", [128, 1024], F32))
    ps["F2"] = es_b.enter_context(nc.psum_tensor("F2", [128, 1024], F32))
    ps["PTF"] = es_b.enter_context(nc.psum_tensor("PTF", [128, 1024], BF16))
    ps["PTE"] = es_b.enter_context(nc.psum_tensor("PTE", [128, 1024], BF16))
    rot = {"i": 0}
    ep_state = {"ob_free": {0: 0, 1: 0}, "ofm_readers": {0: 0, 1: 0},
                "pte_evac": T_ss}

    wslot_free = [0] * NWB
    ring = 0

    def wload(src_dram, oc, ncols, waits_extra=(), queue="sync"):
        nonlocal ring
        slot = ring % NWB
        ring += 1
        sem = "d_w%d" % slot
        w = [("ts", wslot_free[slot])] + list(waits_extra)
        if src_dram in CAST_GATE:
            w.append(CAST_GATE[src_dram])
        my = dma(queue, (lambda e, src_dram=src_dram, oc=oc, ncols=ncols,
                         slot=slot: e.dma_start(
            out=sb["wb%d" % slot][:, 0:ncols],
            in_=dt[src_dram][oc * 128:(oc + 1) * 128, 0:ncols])), sem, waits=w)
        return slot, my

    wc_free = [0, 0, 0]
    wc_ring = [0]

    def w3load(j):
        slot = wc_ring[0] % 3
        wc_ring[0] += 1
        sem = "d_wc%d" % slot
        w = [("ts", wc_free[slot]), ("g_cw3", G_CW3)]
        my = dma("sync", (lambda e, j=j, slot=slot: e.dma_start(
            out=sb["wc%d" % slot][:, 0:512],
            in_=dt["w3s"][j * 128:(j + 1) * 128, :])), sem, waits=w)
        return slot, my

    pfb = {k: [("vs", T_vs), ("ss", T_ss)]
           for k in ("F0a", "F0b", "F1a", "F1b", "F2a", "F2b", "PTF")}
    BANK = {
        "F0a": lambda: ps["F0"][:, 0:512], "F0b": lambda: ps["F0"][:, 512:1024],
        "F1a": lambda: ps["F1"][:, 0:512], "F1b": lambda: ps["F1"][:, 512:1024],
        "F2a": lambda: ps["F2"][:, 0:512], "F2b": lambda: ps["F2"][:, 512:1024],
    }

    x1f_readers = 0
    A_readers = 0
    B_readers = 0
    ea_free = 0
    za_free = 0
    out_dmas = 0

    def emit_x1cast(tau):
        ev = 0
        for tb in range(4):
            xcn = "xc0" if tb % 2 == 0 else "xc1"
            qt = 4 * tau + tb
            cv = comp("vector", (lambda e, qt=qt, xcn=xcn: e.tensor_copy(
                out=sb[xcn][:], in_=sb["O"][:, qt * 1024:(qt + 1) * 1024])),
                [("ts", p_free.get("xc" + str(tb % 2), 0)),
                 ("vs", x1_ready_v[qt]),
                 ("ts", T_ts if tau == 0 else 0)])
            wts = [("vs", cv), ("ss", ev if (tau > 0 or tb > 0) else SS_ATTN)]
            last = 0
            for dc in range(DC):
                fn = (lambda e, dc=dc, xcn=xcn: e.transpose(
                    ps["PTF"][:, dc * 128:(dc + 1) * 128],
                    sb[xcn][:, dc * 128:(dc + 1) * 128],
                    sb["identb"][:]))
                last = comp("tensor", fn, wts)
                wts = ()
            p_free["xc" + str(tb % 2)] = last
            ev = comp("scalar", (lambda e, tb=tb: e.copy(
                out=sb["x1f"][:].rearrange("p (n t) -> p n t", t=TO)[:, :, tb * 128:(tb + 1) * 128],
                in_=ps["PTF"][:].rearrange("p (n t) -> p n t", t=128))),
                [("ts", last), ("ts", x1f_readers)])
        return ev

    for qi in range(4):
        emit_ep(qi)
    wait("sync", [("ts", T_ts), ("vs", T_vs), ("ss", T_ss)])
    next_x1f_evac = emit_x1cast(0)
    _ep_next = [4]
    for tau in range(NFT):
        x1f_evac = next_x1f_evac
        uA_done = 0
        for oc in range(HC):
            pr = ["F0", "F1", "F2"][rot["i"] % 3]
            rot["i"] += 1
            slot1, l1 = wload("w1s", oc, DC * 128)
            slot2, l2 = wload("w2s", oc, DC * 128, queue="gpsimd")
            if tau == 0:
                emit_tail_chunks(1)
                want_ep = 4 + ((oc + 1) * (NQT - 4)) // HC
                while _ep_next[0] < min(NQT, want_ep):
                    emit_tail_chunks(1)
                    emit_ep(_ep_next[0])
                    _ep_next[0] += 1
            wts = [("d_w%d" % slot1, l1), ("ss", x1f_evac)] + pfb[pr + "a"] + pfb[pr + "b"]
            lasta = lastb = 0
            for ic in range(DC):
                fn = (lambda e, pr=pr, ic=ic, slot=slot1: e.matmul(
                    ps[pr][:, 0:512],
                    sb["wb%d" % slot][:, ic * 128:(ic + 1) * 128],
                    sb["x1f"][:, ic * TO:(ic + 1) * TO],
                    start=(ic == 0), stop=(ic == DC - 1)))
                lasta = comp("tensor", fn, wts)
                wts = ()
            wts = [("d_w%d" % slot2, l2)]
            for ic in range(DC):
                fn = (lambda e, pr=pr, ic=ic, slot=slot2: e.matmul(
                    ps[pr][:, 512:1024],
                    sb["wb%d" % slot][:, ic * 128:(ic + 1) * 128],
                    sb["x1f"][:, ic * TO:(ic + 1) * TO],
                    start=(ic == 0), stop=(ic == DC - 1)))
                lastb = comp("tensor", fn, wts)
                wts = ()
            wslot_free[slot1] = lasta
            wslot_free[slot2] = lastb
            x1f_readers = lastb
            ca = comp("scalar", (lambda e, pr=pr, oc=oc: e.activation(
                out=sb["ea"][:], in_=ps[pr][:, 0:512], func=AF.Silu,
                scale=sb["s1"][:, oc:oc + 1])), [("ts", lasta), ("vs", ea_free)])
            cu = comp("vector", (lambda e, pr=pr, oc=oc: e.scalar_tensor_tensor(
                out=sb["A"][:, oc * TO:(oc + 1) * TO], in0=ps[pr][:, 512:1024],
                scalar=sb["s2"][:, oc:oc + 1], in1=sb["ea"][:],
                op0=ALU.mult, op1=ALU.mult)),
                [("ts", lastb), ("ss", ca), ("ts", A_readers if oc == 0 else 0)])
            ea_free = cu
            pfb[pr + "a"] = [("ss", ca)]
            pfb[pr + "b"] = [("vs", cu)]
            uA_done = cu

        def h_stage(src_sb, dst_sb, wname, srt, src_ready, dst_readers):
            halves = ["F2a", "F2b", "F0a", "F0b"]
            u_done = 0
            u_evs = []
            for oc in range(HC):
                bk = halves[oc % 4]
                slot, ld = wload(wname, oc, HC * 128)
                wts = [("d_w%d" % slot, ld)] + ([src_ready] if oc == 0 else []) + pfb[bk]
                last = 0
                for ic in range(HC):
                    fn = (lambda e, bk=bk, ic=ic, slot=slot, src_sb=src_sb: e.matmul(
                        BANK[bk](),
                        sb["wb%d" % slot][:, ic * 128:(ic + 1) * 128],
                        sb[src_sb][:, ic * TO:(ic + 1) * TO],
                        start=(ic == 0), stop=(ic == HC - 1)))
                    last = comp("tensor", fn, wts)
                    wts = ()
                wslot_free[slot] = last
                cu = comp("scalar", (lambda e, bk=bk, oc=oc, srt=srt, dst_sb=dst_sb: e.activation(
                    out=sb[dst_sb][:, oc * TO:(oc + 1) * TO], in_=BANK[bk](),
                    func=AF.Gelu, scale=sb[srt][:, oc:oc + 1])),
                    [("ts", last), ("ts", dst_readers if oc == 0 else 0)])
                pfb[bk] = [("ss", cu)]
                u_done = cu
                u_evs.append(cu)
            return u_done, u_evs

        if tau == 0:
            while _ep_next[0] < NQT:
                emit_tail_chunks(1)
                emit_ep(_ep_next[0])
                _ep_next[0] += 1
        uB_done, _ = h_stage("A", "Bb", "h1s", "s1h", ("vs", uA_done), B_readers)
        A_readers_save = cnt["ts"]
        if tau == 0:
            emit_tail_chunks(len(cast_tail))
        uA2_done, uA2_evs = h_stage("Bb", "A", "h2s", "s2h", ("ss", uB_done), A_readers_save)
        B_readers = cnt["ts"]
        A_readers = 0
        if tau + 1 < NFT:
            next_x1f_evac = emit_x1cast(tau + 1)

        z_done = {}
        for oh in range(2):
            FT = ["F0a", "F0b", "F1a", "F1b"] if oh == 0 else ["F0a", "F0b", "F2a", "F2b"]
            wts_f = []
            for tb in range(4):
                wts_f += pfb[FT[tb]]
            lastmm = {tb: 0 for tb in range(4)}
            for ic in range(HC):
                slot, ld = w3load(oh * HC + ic)
                wts = [("d_wc%d" % slot, ld)]
                if oh == 0:
                    wts.append(("ss", uA2_evs[ic]))
                if ic == 0:
                    wts = wts + wts_f
                for tb in range(4):
                    fn = (lambda e, bk=FT[tb], ic=ic, slot=slot, tb=tb: e.matmul(
                        BANK[bk](),
                        sb["A"][:, ic * TO + tb * 128: ic * TO + tb * 128 + 128],
                        sb["wc%d" % slot][:, 0:512],
                        start=(ic == 0), stop=(ic == HC - 1)))
                    lastmm[tb] = comp("tensor", fn, wts)
                    wts = ()
                wc_free[slot] = lastmm[3]
            A_readers = lastmm[3]
            for tb in range(4):
                qt = 4 * tau + tb
                czm = comp("vector", (lambda e, bk=FT[tb], oh=oh: e.tensor_tensor(
                    out=BANK[bk](), in0=BANK[bk](),
                    in1=sb["s3b"][:, oh * 512:(oh + 1) * 512],
                    op=ALU.mult)), [("ts", lastmm[tb])])
                cz = comp("vector", (lambda e, bk=FT[tb], qt=qt, oh=oh: e.tensor_tensor(
                    out=sb["O"][:, qt * 1024 + oh * 512: qt * 1024 + (oh + 1) * 512],
                    in0=BANK[bk](),
                    in1=sb["O"][:, qt * 1024 + oh * 512: qt * 1024 + (oh + 1) * 512],
                    op=ALU.add)), [("vs", czm)])
                pfb[FT[tb]] = [("vs", cz)]
                z_done[(tb, oh)] = cz

        for tb in range(4):
            qt = 4 * tau + tb
            cb1 = comp("vector", (lambda e, qt=qt: e.bn_stats(
                sb["bst"][:, 0:6], sb["O"][:, qt * 1024: qt * 1024 + 512])),
                [("vs", z_done[(tb, 0)]), ("vs", z_done[(tb, 1)]),
                 ("ss", ep_s), ("vs", cnt["vs"])])
            cb2 = comp("vector", (lambda e, qt=qt: e.bn_stats(
                sb["bst"][:, 6:12], sb["O"][:, qt * 1024 + 512: qt * 1024 + 1024])),
                [("vs", cb1)])
            cag = comp("vector", (lambda e: e.bn_aggr(
                sb["bmv"][:, 0:2], sb["bst"][:, 0:12])), [("vs", cb2)])
            crs0 = comp("scalar", (lambda e: e.activation(
                out=sb["mm2"][:, 0:1], in_=sb["bmv"][:, 1:2], func=AF.Sqrt,
                bias=sb["epsv"][:, 0:1])), [("vs", cag), ("ss", ep_s)])
            crs = comp("vector", (lambda e: e.reciprocal(
                out=sb["tv"][:, 0:1], in_=sb["mm2"][:, 0:1])),
                [("ss", crs0), ("vs", cnt["vs"])])
            cx1 = comp("vector", (lambda e, qt=qt: e.tensor_scalar(
                out=sb["O"][:, qt * 1024:(qt + 1) * 1024],
                in0=sb["O"][:, qt * 1024:(qt + 1) * 1024],
                scalar1=sb["bmv"][:, 0:1], scalar2=sb["tv"][:, 0:1],
                op0=ALU.subtract, op1=ALU.mult)), [("vs", crs)])
            ep_s = crs0
            out_dmas = dma("gpsimd", (lambda e, qt=qt: e.dma_start(
                out=dt["out"].rearrange("(t p) d -> p t d", p=128)[:, qt:qt + 1, :],
                in_=sb["O"][:, qt * 1024:(qt + 1) * 1024].rearrange(
                    "p (t d) -> p t d", d=1024))),
                "d_out", waits=[("vs", cx1)])

    wait("sync", [("d_out", dcnt["d_out"])])

    with es_top:
      with es_wt:
        with es_b:
            with nc.Block() as block:
                @block.sync
                def _(e):
                    for f in streams["sync"]:
                        f(e)

                @block.tensor
                def _(e):
                    for f in streams["tensor"]:
                        f(e)

                @block.vector
                def _(e):
                    for f in streams["vector"]:
                        f(e)

                @block.scalar
                def _(e):
                    for f in streams["scalar"]:
                        f(e)

                @block.gpsimd
                def _(e):
                    for f in streams["gpsimd"]:
                        f(e)

    n_inst = sum(len(v) for v in streams.values())
    return nc, n_inst



def _pack_oc(wt, n_ic, n_oc, ocw=128):
    return np.ascontiguousarray(
        wt.reshape(n_ic, 128, n_oc, ocw).transpose(2, 0, 1, 3).reshape(n_oc * n_ic * 128, ocw))


def _pack_pc(wt, n_ic, n_oc):
    return np.ascontiguousarray(
        wt.reshape(n_ic, 128, n_oc, 128).transpose(2, 1, 0, 3).reshape(n_oc * 128, n_ic * 128))


def _stage(cfg: Cfg, inputs):
    D, DC, HC = cfg.D, cfg.DC, cfg.HC
    x = np.asarray(inputs["L0_attn_x"], dtype=np.float32)
    W = np.asarray(inputs["L0_attn_W"], dtype=np.float32)
    B = x.shape[0]
    shared = {}
    shared["wattT"] = np.ascontiguousarray(W.T)
    shared["wattK"] = np.ascontiguousarray(W[:1024, :])
    w1t = np.ascontiguousarray(np.asarray(inputs["L0_ffn_w1_w"], dtype=np.float32).T)
    w2t = np.ascontiguousarray(np.asarray(inputs["L0_ffn_w2_w"], dtype=np.float32).T)
    h1t = np.ascontiguousarray(np.asarray(inputs["L0_ffn_h1_w"], dtype=np.float32).T)
    h2t = np.ascontiguousarray(np.asarray(inputs["L0_ffn_h2_w"], dtype=np.float32).T)
    w3t = np.ascontiguousarray(np.asarray(inputs["L0_ffn_w3_w"], dtype=np.float32).T)
    shared["w1tp"] = _pack_pc(w1t, DC, HC)
    shared["w2tp"] = _pack_pc(w2t, DC, HC)
    shared["h1tp"] = _pack_pc(h1t, HC, HC)
    shared["h2tp"] = _pack_pc(h2t, HC, HC)
    shared["w3tp"] = _pack_oc(w3t, HC, 2, 512)
    for nm, key in (("s1", "L0_ffn_w1_s"), ("s2", "L0_ffn_w2_s"),
                    ("s1h", "L0_ffn_h1_s"), ("s2h", "L0_ffn_h2_s")):
        s = np.asarray(inputs[key], dtype=np.float32)
        shared[nm] = np.ascontiguousarray(s.reshape(HC, 128).T)
    shared["s3b"] = np.ascontiguousarray(
        np.tile(np.asarray(inputs["L0_ffn_w3_s"], dtype=np.float32)[None, :], (128, 1)))
    shared["uio"] = np.ascontiguousarray(
        (np.arange(1024, dtype=np.float32)[None, :]
         - np.arange(128, dtype=np.float32)[:, None]).astype(np.float16))
    shared["identb"] = np.eye(128, dtype=ml_dtypes.bfloat16)

    xfm_all = [np.ascontiguousarray(x[b].T) for b in range(B)]
    per_seq = cfg.n_cores // B
    in_maps = []
    for c in range(cfg.n_cores):
        seq = c // per_seq
        r = c % per_seq
        rows = [(cfg.cps * p + r) * 128 for p in range(cfg.NQT)]
        m = dict(shared)
        m["xfm"] = xfm_all[seq]
        m["xraw"] = x[seq]
        m["xq"] = np.ascontiguousarray(np.concatenate(
            [xfm_all[seq][:, q0:q0 + 128] for q0 in rows], axis=1))
        m["xtm"] = np.ascontiguousarray(np.concatenate(
            [x[seq][q0:q0 + 128, :] for q0 in rows], axis=0))
        th = np.zeros((128, cfg.NVIS), dtype=np.float32)
        v = 0
        for kt in range(cfg.NKT):
            for s in cfg.slots[kt]:
                th[:, v] = float(rows[s] - kt * 1024)
                v += 1
        m["theta"] = th
        in_maps.append(m)
    return in_maps


def run_cfg(cfg: Cfg, inputs, runner=None):
    in_maps = _stage(cfg, inputs)
    nc, n_inst = build_program(cfg)
    if runner is None:
        res = run_bass_kernel_spmd(nc, in_maps, core_ids=list(range(cfg.n_cores)))
        results = res.results
    else:
        results = runner(nc, in_maps)
    x = np.asarray(inputs["L0_attn_x"])
    B, S, D = x.shape
    out = np.empty((B, S, D), dtype=np.float32)
    per_seq = cfg.n_cores // B
    for c in range(cfg.n_cores):
        seq = c // per_seq
        r = c % per_seq
        res = results[c]["out"]
        for p in range(cfg.NQT):
            q0 = (cfg.cps * p + r) * 128
            out[seq, q0:q0 + 128, :] = res[p * 128:(p + 1) * 128]
    return out


def kernel(**inputs):
    return run_cfg(CFG_FULL, inputs)


# version 23
# speedup vs baseline: 1.0338x; 1.0162x over previous
import numpy as np
from contextlib import ExitStack
import ml_dtypes

import concourse.bass as bass
import concourse.mybir as mybir
from concourse.bass_utils import run_bass_kernel_spmd

F32 = mybir.dt.float32
BF16 = mybir.dt.bfloat16
AF = mybir.ActivationFunctionType
ALU = mybir.AluOpType
AX = mybir.AxisListType

NEG_BIG = -30000.0


class Cfg:
    def __init__(self, seq=8192, qpc=2048, n_cores=8):
        self.D = 1024
        self.DH = 4096
        self.DC = 8
        self.HC = 32
        self.seq = seq
        self.qpc = qpc
        self.NKT = seq // 1024
        self.NQT = qpc // 128
        self.TO = 512
        self.NFT = qpc // 512
        self.n_cores = n_cores
        self.cps = n_cores // 2
        self.G = 8 // self.cps
        self.slots = [[p for p in range(self.NQT) if p // self.G >= kt]
                      for kt in range(self.NKT)]
        self.NVIS = sum(len(s) for s in self.slots)


CFG_FULL = Cfg(8192, 2048, 8)
CFG_MINI = Cfg(4096, 2048, 4)


def build_program(cfg: Cfg):
    D, DH, DC, HC = cfg.D, cfg.DH, cfg.DC, cfg.HC
    NKT, NQT, TO, NFT = cfg.NKT, cfg.NQT, cfg.TO, cfg.NFT
    QPC, SEQ = cfg.qpc, cfg.seq
    NVIS = cfg.NVIS
    SLOTS = cfg.slots

    nc = bass.Bass("TRN2", target_bir_lowering=False, debug=False)

    dt = {}
    dt["xfm"] = nc.dram_tensor("xfm", [D, SEQ], F32, kind="ExternalInput").ap()
    dt["xq"] = nc.dram_tensor("xq", [D, QPC], F32, kind="ExternalInput").ap()
    dt["xtm"] = nc.dram_tensor("xtm", [QPC, D], F32, kind="ExternalInput").ap()
    dt["wattT"] = nc.dram_tensor("wattT", [D, 2 * D], F32, kind="ExternalInput").ap()
    dt["wattK"] = nc.dram_tensor("wattK", [D, D], F32, kind="ExternalInput").ap()
    dt["xraw"] = nc.dram_tensor("xraw", [SEQ, D], F32, kind="ExternalInput").ap()
    dt["w1tp"] = nc.dram_tensor("w1tp", [HC * 128, DC * 128], F32, kind="ExternalInput").ap()
    dt["w2tp"] = nc.dram_tensor("w2tp", [HC * 128, DC * 128], F32, kind="ExternalInput").ap()
    dt["h1tp"] = nc.dram_tensor("h1tp", [HC * 128, HC * 128], F32, kind="ExternalInput").ap()
    dt["h2tp"] = nc.dram_tensor("h2tp", [HC * 128, HC * 128], F32, kind="ExternalInput").ap()
    dt["w3tp"] = nc.dram_tensor("w3tp", [2 * HC * 128, 512], F32, kind="ExternalInput").ap()
    for s in ("s1", "s2", "s1h", "s2h"):
        dt[s] = nc.dram_tensor(s, [128, HC], F32, kind="ExternalInput").ap()
    dt["s3b"] = nc.dram_tensor("s3b", [128, D], F32, kind="ExternalInput").ap()
    dt["theta"] = nc.dram_tensor("theta", [128, NVIS], F32, kind="ExternalInput").ap()
    dt["uio"] = nc.dram_tensor("uio", [128, 1024], mybir.dt.float16, kind="ExternalInput").ap()
    dt["identb"] = nc.dram_tensor("identb", [128, 128], BF16, kind="ExternalInput").ap()
    dt["out"] = nc.dram_tensor("out", [QPC, D], F32, kind="ExternalOutput").ap()
    dt["w1s"] = nc.dram_tensor("w1s", [HC * 128, DC * 128], BF16, kind="Internal").ap()
    dt["w2s"] = nc.dram_tensor("w2s", [HC * 128, DC * 128], BF16, kind="Internal").ap()
    dt["w3s"] = nc.dram_tensor("w3s", [2 * HC * 128, 512], BF16, kind="Internal").ap()

    sem_names = ["ts", "vs", "ss", "gs",
                 "d_const", "g_xk0", "g_xk1", "g_q0", "g_q1", "d_xtm0", "d_xtm1", "d_out",
                 "g_wt", "g_wk", "g_xt0", "g_xt1", "g_cw1", "g_cw2", "g_cw3", "g_ch1", "g_ch2",
                 "d_w0", "d_w1", "d_w2", "d_w3",
                 "d_wc0", "d_wc1", "d_wc2"]
    es_top = ExitStack()
    SEM = {n: es_top.enter_context(nc.semaphore(n)) for n in sem_names}

    sb = {}

    def sbuf(name, shape, dtp, stack):
        sb[name] = stack.enter_context(nc.sbuf_tensor("sb_" + name, shape, dtp))
        return sb[name]

    sbuf("O", [128, NQT * 1024], F32, es_top)
    for s in ("s1", "s2", "s1h", "s2h"):
        sbuf(s, [128, HC], F32, es_top)
    sbuf("s3b", [128, D], F32, es_top)
    sbuf("identb", [128, 128], BF16, es_top)
    sbuf("lrun", [128, NQT], F32, es_top)
    for s in ("rh", "s1v", "s2v", "nmean", "mm2", "tv", "epsv"):
        sbuf(s, [128, 1], F32, es_top)
    sbuf("bst", [128, 12], F32, es_top)
    sbuf("bmv", [128, 2], F32, es_top)
    sbuf("late8", [128, 2048], F32, es_top)

    streams = {"sync": [], "tensor": [], "vector": [], "scalar": [], "gpsimd": []}
    cnt = {"ts": 0, "vs": 0, "ss": 0, "gs": 0}
    dcnt = {n: 0 for n in sem_names if n not in ("ts", "vs", "ss", "gs")}

    SELF = {"tensor": "ts", "vector": "vs", "scalar": "ss", "gpsimd": "gs"}

    def comp(eng, build_fn, waits=()):
        s = SELF[eng]
        cnt[s] += 1
        my = cnt[s]
        w = tuple((a, b) for (a, b) in waits if b > 0)

        def f(e, build_fn=build_fn, w=w, s=s):
            for sname, v in w:
                e.wait_ge(SEM[sname], v)
            build_fn(e).then_inc(SEM[s], 1)
        streams[eng].append(f)
        return my

    def dma(eng, build_fn, sem_name, waits=()):
        dcnt[sem_name] += 16
        my = dcnt[sem_name]
        w = tuple((a, b) for (a, b) in waits if b > 0)

        def f(e, build_fn=build_fn, w=w, sem_name=sem_name):
            for sname, v in w:
                e.wait_ge(SEM[sname], v)
            build_fn(e).then_inc(SEM[sem_name], 16)
        streams[eng].append(f)
        return my

    def wait(eng, waits):
        w = tuple((a, b) for (a, b) in waits if b > 0)

        def f(e, w=w):
            for sname, v in w:
                e.wait_ge(SEM[sname], v)
        streams[eng].append(f)

    es_wt = ExitStack()
    sbuf("WTv", [128, DC * 1024], BF16, es_wt)
    es_a = ExitStack()
    sbuf("QP", [128, DC * QPC], BF16, es_a)
    sbuf("xk0", [128, DC * 1024], BF16, es_a)
    sbuf("XT0", [128, DC * 1024], BF16, es_a)
    sbuf("P0", [128, 1024], BF16, es_a)
    sbuf("P1", [128, 1024], BF16, es_a)
    sbuf("PT0", [128, 1024], BF16, es_a)
    sbuf("PT1", [128, 1024], BF16, es_a)
    sbuf("uio", [128, 1024], mybir.dt.float16, es_a)
    sbuf("theta", [128, NVIS], F32, es_a)
    sbuf("msk", [128, 1024], BF16, es_a)
    sbuf("negm", [128, NQT], F32, es_a)
    sbuf("mpos", [128, NQT], F32, es_a)
    sbuf("alpha", [128, 4], F32, es_a)
    sbuf("lpart", [128, 4], F32, es_a)
    sbuf("nmp", [128, 1], F32, es_a)

    for name in ("uio", "theta", "s1", "s2", "s1h", "s2h", "s3b", "identb"):
        dma("sync", (lambda e, n=name: e.dma_start(out=sb[n][:], in_=dt[n])), "d_const")
    C_CONST = dcnt["d_const"]

    comp("vector", lambda e: e.memset(sb["epsv"][:], 1e-5))
    es_q = ExitStack()
    sbuf("Qq0", [128, DC * 512], BF16, es_q)
    sbuf("Qq1", [128, DC * 512], BF16, es_q)
    sbuf("WK", [128, DC * D], BF16, es_q)
    ps = {}
    ps["S0"] = es_a.enter_context(nc.psum_tensor("S0", [128, 1024], F32))
    ps["S1"] = es_a.enter_context(nc.psum_tensor("S1", [128, 1024], F32))
    ps["PV"] = es_a.enter_context(nc.psum_tensor("PV", [128, 1024], F32))
    ps["PT0"] = es_a.enter_context(nc.psum_tensor("PT0p", [128, 1024], BF16))
    ps["PT1"] = es_a.enter_context(nc.psum_tensor("PT1p", [128, 1024], BF16))

    dma("gpsimd", (lambda e: e.dma_start(
        out=sb["WK"][:].rearrange("p (dc c) -> p dc c", c=D),
        in_=dt["wattK"].rearrange("(dc p) c -> p dc c", p=128))), "g_wk")
    dma("gpsimd", (lambda e: e.dma_start(
        out=sb["Qq0"][:].rearrange("p (ic c) -> p ic c", c=512),
        in_=dt["xq"][:, 0:512].rearrange("(ic p) c -> p ic c", p=128))), "g_q0")
    G_WK = dcnt["g_wk"]

    cast_chunks = []

    def cast_rows(src, dst, rows, cols, sem):
        step = max(128, (2 * 1024 * 1024) // (cols * 4))
        n = 0
        for r0 in range(0, rows, step):
            r1 = min(rows, r0 + step)
            cast_chunks.append((src, dst, r0, r1, sem))
            n += 16
        return n

    G_CW1 = cast_rows("w1tp", "w1s", HC * 128, DC * 128, "g_cw1")
    G_CW2 = cast_rows("w2tp", "w2s", HC * 128, DC * 128, "g_cw2")
    G_CW3 = cast_rows("w3tp", "w3s", 2 * HC * 128, 512, "g_cw3")
    cast_tail = []
    CAST_GATE = {"w1s": ("g_cw1", G_CW1), "w2s": ("g_cw2", G_CW2)}
    _cast_i = [0]
    _tail_i = [0]

    def _emit_chunk(tup):
        s_, d_, r0, r1, sem = tup
        dma("gpsimd", (lambda e, r0=r0, r1=r1, s_=s_, d_=d_: e.dma_start(
            out=dt[d_][r0:r1, :], in_=dt[s_][r0:r1, :])), sem)

    def emit_cast_chunks(n):
        bound = min(len(cast_chunks), _cast_i[0] + n)
        while _cast_i[0] < bound:
            _emit_chunk(cast_chunks[_cast_i[0]])
            _cast_i[0] += 1

    def emit_tail_chunks(n):
        bound = min(len(cast_tail), _tail_i[0] + n)
        while _tail_i[0] < bound:
            _emit_chunk(cast_tail[_tail_i[0]])
            _tail_i[0] += 1

    last_pv_mm = 0
    pair_free = {"S0": [], "S1": [], "PV": []}
    pt_free = {0: 0, 1: 0}
    p_free = {0: 0, 1: 0}
    xk_war = {0: 0, 1: 0}
    xt_war = {0: 0, 1: 0}
    xk_done = {}
    xt_done = {}

    vS, vMadd, vNegm, vExp, vAlpha, vPT, vPTe, vPV, vO, vL = ({} for _ in range(10))
    _vctr = [0]
    _vmap = {}

    def vid(kt, qi):
        key = (kt, qi)
        if key not in _vmap:
            _vmap[key] = _vctr[0]
            _vctr[0] += 1
        return _vmap[key]

    def xk_load_cast(kt, qp_gate=0):
        buf = "xk%d" % (kt % 2)
        wts = []
        if xk_war[kt % 2]:
            wts.append(("ts", xk_war[kt % 2]))
        if qp_gate:
            wts.append(("ts", qp_gate))
        xk_done[kt] = dma("gpsimd", (lambda e, kt=kt, buf=buf: e.dma_start(
            out=sb[buf][:].rearrange("p (ic c) -> p ic c", c=1024),
            in_=dt["xfm"][:, kt * 1024:(kt + 1) * 1024].rearrange(
                "(ic p) c -> p ic c", p=128))),
            "g_xk%d" % (kt % 2), waits=wts)

    def xt_load_cast(kt, qp_gate=0):
        buf = "XT%d" % (kt % 2)
        wts = []
        if xt_war[kt % 2]:
            wts.append(("ts", xt_war[kt % 2]))
        if qp_gate:
            wts.append(("ts", qp_gate))
        xt_done[kt] = dma("gpsimd", (lambda e, kt=kt, buf=buf: e.dma_start(
            out=sb[buf][:].rearrange("p (kc c) -> p kc c", c=1024),
            in_=dt["xraw"][kt * 1024:(kt + 1) * 1024, :].rearrange(
                "(kc p) c -> p kc c", p=128))),
            "g_xt%d" % (kt % 2), waits=wts)

    def emit_qprime():
        halves = [("S0", 0), ("S0", 512), ("S1", 0), ("S1", 512)]
        hfree = [[], [], [], []]
        bank_last = {"S0": 0, "S1": 0}
        q_last_mm = {}
        ev = 0
        for q in range(QPC // 512):
            qb = "Qq%d" % (q % 2)
            if q >= 2:
                dma("gpsimd", (lambda e, q=q, qb=qb: e.dma_start(
                    out=sb[qb][:].rearrange("p (ic c) -> p ic c", c=512),
                    in_=dt["xq"][:, q * 512:(q + 1) * 512].rearrange(
                        "(ic p) c -> p ic c", p=128))),
                    "g_q%d" % (q % 2), waits=[("ts", q_last_mm[q - 2])])
            for ib in range(DC):
                hi = (q * DC + ib) % 4
                pr, off = halves[hi]
                wts0 = list(hfree[hi])
                last = 0
                for dc in range(DC):
                    wts = wts0
                    wts0 = []
                    if ib == 0 and dc == 0:
                        wts = wts + [("g_q%d" % (q % 2), 16 * (q // 2 + 1))]
                        if q == 0:
                            wts = wts + [("g_wk", G_WK)]
                    fn = (lambda e, pr=pr, off=off, dc=dc, ib=ib, qb=qb: e.matmul(
                        ps[pr][:, off:off + 512],
                        sb["WK"][:, dc * D + ib * 128: dc * D + ib * 128 + 128],
                        sb[qb][:, dc * 512:(dc + 1) * 512],
                        start=(dc == 0), stop=(dc == DC - 1)))
                    last = comp("tensor", fn, wts)
                ev = comp("scalar", (lambda e, pr=pr, off=off, ib=ib, q=q: e.copy(
                    out=sb["QP"][:, ib * QPC + q * 512: ib * QPC + (q + 1) * 512],
                    in_=ps[pr][:, off:off + 512])), [("ts", last)])
                hfree[hi] = [("ss", ev)]
                bank_last[pr] = ev
            q_last_mm[q] = cnt["ts"]
        pair_free["S0"] = [("ss", bank_last["S0"])]
        pair_free["S1"] = [("ss", bank_last["S1"])]
        return ev

    def emit_S(kt, qi, first_of_kt):
        v = vid(kt, qi)
        buf = "xk%d" % (kt % 2)
        pr = "S0" if qi % 2 == 0 else "S1"
        wts = list(pair_free[pr])
        if first_of_kt:
            wts.append(("g_xk%d" % (kt % 2), xk_done[kt]))
        if kt == 0 and qi == SLOTS[0][0]:
            wts += [("ss", QP_DONE), ("d_const", C_CONST)]
        lastmm = 0
        for ns in range(2):
            for ic in range(DC):
                fn = (lambda e, pr=pr, ns=ns, ic=ic, qi=qi, buf=buf: e.matmul(
                    ps[pr][:, ns * 512:(ns + 1) * 512],
                    sb["QP"][:, ic * QPC + qi * 128: ic * QPC + qi * 128 + 128],
                    sb[buf][:, ic * 1024 + ns * 512: ic * 1024 + (ns + 1) * 512],
                    start=(ic == 0), stop=(ic == DC - 1)))
                lastmm = comp("tensor", fn, wts)
                wts = ()
        vS[v] = lastmm
        xk_war[kt % 2] = lastmm
        pair_free[pr] = []

    def emit_mask(kt, qi):
        v = vid(kt, qi)
        pr = "S0" if qi % 2 == 0 else "S1"
        c1 = comp("vector", (lambda e, v=v: e.tensor_scalar(
            out=sb["msk"][:], in0=sb["uio"][:],
            scalar1=sb["theta"][:, v:v + 1], scalar2=NEG_BIG,
            op0=ALU.is_gt, op1=ALU.mult)),
            [("vs", cnt["vs"]), ("d_const", C_CONST if v == 0 else 0)])
        c2 = comp("vector", (lambda e, pr=pr: e.tensor_tensor(
            out=ps[pr][:, :], in0=ps[pr][:, :], in1=sb["msk"][:], op=ALU.add)),
            [("ts", vS[v]), ("vs", c1)])
        vMadd[v] = c2

    def emit_negm(kt, qi):
        v = vid(kt, qi)
        pr = "S0" if qi % 2 == 0 else "S1"
        c1 = comp("vector", (lambda e, pr=pr: e.reduce_max(
            sb["nmp"][:, 0:1], ps[pr][:, :], axis=AX.X, negate=True)),
            [("vs", vMadd[v])])
        if kt == 0:
            c2 = comp("vector", (lambda e, qi=qi: e.tensor_copy(
                out=sb["negm"][:, qi:qi + 1], in_=sb["nmp"][:, 0:1])), [("vs", c1)])
        else:
            c2 = comp("vector", (lambda e, qi=qi: e.tensor_tensor(
                out=sb["negm"][:, qi:qi + 1], in0=sb["negm"][:, qi:qi + 1],
                in1=sb["nmp"][:, 0:1], op=ALU.min)), [("vs", c1)])
        vNegm[v] = c2

    def emit_exp(kt, qi):
        v = vid(kt, qi)
        pr = "S0" if qi % 2 == 0 else "S1"
        par = qi % 2
        sl = v % 4
        wts = [("vs", vNegm[v]), ("vs", vL.get(v - 4, 0))]
        if kt > 0:
            ca = comp("scalar", (lambda e, qi=qi, sl=sl: e.activation(
                out=sb["alpha"][:, sl:sl + 1], in_=sb["mpos"][:, qi:qi + 1],
                func=AF.Exp, bias=sb["negm"][:, qi:qi + 1])), wts)
            vAlpha[v] = ca
            wts = [("ss", ca)]
        else:
            vAlpha[v] = 0
        ce = comp("scalar", (lambda e, pr=pr, par=par, qi=qi, sl=sl: e.activation(
            out=sb["P%d" % par][:], in_=ps[pr][:, :], func=AF.Exp,
            bias=sb["negm"][:, qi:qi + 1], accum_out=sb["lpart"][:, sl:sl + 1])),
            [("ts", p_free[par])] + wts)
        cm = comp("scalar", (lambda e, qi=qi: e.activation(
            out=sb["mpos"][:, qi:qi + 1], in_=sb["negm"][:, qi:qi + 1],
            func=AF.Copy, scale=-1.0)), [("ss", ce)])
        vExp[v] = ce
        pair_free[pr] = [("ss", ce)]

    def emit_PT(kt, qi):
        v = vid(kt, qi)
        par = qi % 2
        wts = [("ss", vExp[v]), ("ss", pt_free[par])]
        last = 0
        for kc in range(DC):
            fn = (lambda e, par=par, kc=kc: e.transpose(
                ps["PT%d" % par][:, kc * 128:(kc + 1) * 128],
                sb["P%d" % par][:, kc * 128:(kc + 1) * 128],
                sb["identb"][:]))
            last = comp("tensor", fn, wts)
            wts = ()
        vPT[v] = last
        p_free[par] = last

    def emit_PTevac(kt, qi):
        v = vid(kt, qi)
        par = qi % 2
        c1 = comp("scalar", (lambda e, par=par: e.copy(
            out=sb["PT%d" % par][:], in_=ps["PT%d" % par][:, :])),
            [("ts", vPT[v])])
        vPTe[v] = c1
        pt_free[par] = c1

    def emit_PV(kt, qi, first_of_kt):
        nonlocal last_pv_mm
        v = vid(kt, qi)
        par = qi % 2
        buf = "XT%d" % (kt % 2)
        wts = [("ss", vPTe[v])] + pair_free["PV"]
        if first_of_kt:
            wts.append(("g_xt%d" % (kt % 2), xt_done[kt]))
        last = 0
        for ns in range(2):
            for kc in range(DC):
                fn = (lambda e, par=par, ns=ns, kc=kc, buf=buf: e.matmul(
                    ps["PV"][:, ns * 512:(ns + 1) * 512],
                    sb["PT%d" % par][:, kc * 128:(kc + 1) * 128],
                    sb[buf][:, kc * 1024 + ns * 512: kc * 1024 + (ns + 1) * 512],
                    start=(kc == 0), stop=(kc == DC - 1)))
                last = comp("tensor", fn, wts)
                wts = ()
        vPV[v] = last
        last_pv_mm = last
        xt_war[kt % 2] = last
        pair_free["PV"] = []

    def emit_O(kt, qi):
        v = vid(kt, qi)
        sl = v % 4
        if kt == 0:
            c1 = comp("vector", (lambda e, qi=qi: e.tensor_copy(
                out=sb["O"][:, qi * 1024:(qi + 1) * 1024], in_=ps["PV"][:, :])),
                [("ts", vPV[v])])
            cl = comp("vector", (lambda e, qi=qi, sl=sl: e.tensor_copy(
                out=sb["lrun"][:, qi:qi + 1], in_=sb["lpart"][:, sl:sl + 1])),
                [("ss", vExp[v])])
        else:
            c1 = comp("vector", (lambda e, qi=qi, sl=sl: e.scalar_tensor_tensor(
                out=sb["O"][:, qi * 1024:(qi + 1) * 1024],
                in0=sb["O"][:, qi * 1024:(qi + 1) * 1024],
                scalar=sb["alpha"][:, sl:sl + 1], in1=ps["PV"][:, :],
                op0=ALU.mult, op1=ALU.add)),
                [("ts", vPV[v]), ("ss", vAlpha[v])])
            cl = comp("vector", (lambda e, qi=qi, sl=sl: e.scalar_tensor_tensor(
                out=sb["lrun"][:, qi:qi + 1], in0=sb["lrun"][:, qi:qi + 1],
                scalar=sb["alpha"][:, sl:sl + 1], in1=sb["lpart"][:, sl:sl + 1],
                op0=ALU.mult, op1=ALU.add)), [("ss", vExp[v])])
        vO[v] = c1
        vL[v] = cl
        pair_free["PV"] = [("vs", c1)]

    n_chunks = len(cast_chunks)
    xk_load_cast(0)
    dma("gpsimd", (lambda e: e.dma_start(
        out=sb["Qq1"][:].rearrange("p (ic c) -> p ic c", c=512),
        in_=dt["xq"][:, 512:1024].rearrange("(ic p) c -> p ic c", p=128))), "g_q1")
    xt_load_cast(0)
    QP_DONE = emit_qprime()
    qp_last_mm = cnt["ts"]
    dma("gpsimd", (lambda e: e.dma_start(
        out=sb["WTv"][:].rearrange("p (ic c) -> p ic c", c=1024),
        in_=dt["wattT"][:, D:2 * D].rearrange("(ic p) c -> p ic c", p=128))), "g_wt")
    G_WT = dcnt["g_wt"]
    es_q.close()
    sbuf("xk1", [128, DC * 1024], BF16, es_a)
    sbuf("XT1", [128, DC * 1024], BF16, es_a)
    VLIST = [(kt, qi) for kt in range(NKT) for qi in SLOTS[kt]]
    NVV = len(VLIST)
    first_idx = {}
    for i, (kt_, qi_) in enumerate(VLIST):
        if kt_ not in first_idx:
            first_idx[kt_] = i
    vis_i = 0
    want_prev = 0
    for i in range(NVV + 2):
        if i < NVV:
            kt, qi = VLIST[i]
            if i == first_idx[kt] and kt + 1 < NKT:
                xk_load_cast(kt + 1, qp_gate=qp_last_mm if kt == 0 else 0)
            want = min(n_chunks, int(n_chunks * (vis_i + 1) / max(1, 0.7 * NVIS)))
            emit_cast_chunks(want - want_prev)
            want_prev = want
            vis_i += 1
            emit_S(kt, qi, first_of_kt=(i == first_idx[kt]))
            emit_mask(kt, qi)
            emit_negm(kt, qi)
        if 1 <= i <= NVV:
            emit_exp(*VLIST[i - 1])
            emit_PT(*VLIST[i - 1])
        if 2 <= i <= NVV + 1:
            ktp, qip = VLIST[i - 2]
            fokp = (i - 2 == first_idx[ktp])
            if fokp and ktp + 1 < NKT:
                xt_load_cast(ktp + 1, qp_gate=qp_last_mm if ktp == 0 else 0)
            emit_PTevac(ktp, qip)
            emit_PV(ktp, qip, first_of_kt=fokp)
            emit_O(ktp, qip)

    emit_cast_chunks(len(cast_chunks))

    INV_D = 1.0 / D
    xtm_slot_free = [0, 0]
    x1_ready_v = {}
    ep_s = 0
    ep_v = 0

    def emit_ep(qi):
        nonlocal ep_s, ep_v
        slot = qi % 2
        ld = dma("sync", (lambda e, qi=qi, slot=slot: e.dma_start(
            out=sb["late8"][:, slot * 1024:(slot + 1) * 1024],
            in_=dt["xtm"][qi * 128:(qi + 1) * 128, :])),
            "d_xtm%d" % slot, waits=[("vs", xtm_slot_free[slot])])
        lastkt = qi // cfg.G
        par = 0
        cr0 = comp("vector", (lambda e, qi=qi: e.tensor_scalar_mul(
            out=sb["rh"][:, 0:1], in0=sb["lrun"][:, qi:qi + 1], scalar1=2.0)),
            [("vs", vL[vid(lastkt, qi)]), ("ss", ep_s), ("vs", cnt["vs"])])
        cr = comp("vector", (lambda e: e.reciprocal(
            out=sb["rh"][:, 0:1], in_=sb["rh"][:, 0:1])), [("vs", cr0)])
        cnorm = comp("vector", (lambda e, qi=qi, par=par: e.tensor_scalar_mul(
            out=sb["Ob%d" % par][:], in0=sb["O"][:, qi * 1024:(qi + 1) * 1024],
            scalar1=sb["rh"][:, 0:1])),
            [("vs", cr), ("vs", vO[vid(lastkt, qi)]),
             ("ts", ep_state["ob_free"][par]),
             ("ts", T_ts if qi == 0 else 0)])
        wts = [("vs", cnorm), ("ss", ep_state["pte_evac"])]
        last = 0
        for ic in range(DC):
            fn = (lambda e, par=par, ic=ic: e.transpose(
                ps["PTE"][:, ic * 128:(ic + 1) * 128],
                sb["Ob%d" % par][:, ic * 128:(ic + 1) * 128],
                sb["identb"][:]))
            last = comp("tensor", fn, wts)
            wts = ()
        ep_state["ob_free"][par] = last
        pev = comp("scalar", (lambda e, par=par: e.copy(
            out=sb["Ofm%d" % par][:], in_=ps["PTE"][:, :])),
            [("ts", last), ("ts", ep_state["ofm_readers"][par])])
        ep_state["pte_evac"] = pev
        pr = ["F0", "F1", "F2"][rot["i"] % 3]
        rot["i"] += 1
        wts = [("ss", pev)] + pfb[pr + "a"] + pfb[pr + "b"]
        if qi == 0:
            wts.append(("g_wt", G_WT))
        lastp = 0
        for ns in range(2):
            for ic in range(DC):
                fn = (lambda e, pr=pr, par=par, ns=ns, ic=ic: e.matmul(
                    ps[pr][:, ns * 512:(ns + 1) * 512],
                    sb["Ofm%d" % par][:, ic * 128:(ic + 1) * 128],
                    sb["WTv"][:, ic * 1024 + ns * 512: ic * 1024 + (ns + 1) * 512],
                    start=(ic == 0), stop=(ic == DC - 1)))
                lastp = comp("tensor", fn, wts)
                wts = ()
        ep_state["ofm_readers"][par] = lastp
        cy = comp("vector", (lambda e, qi=qi, slot=slot, pr=pr: e.tensor_tensor(
            out=sb["O"][:, qi * 1024:(qi + 1) * 1024],
            in0=ps[pr][:, 0:1024],
            in1=sb["late8"][:, slot * 1024:(slot + 1) * 1024], op=ALU.add)),
            [("ts", lastp), ("d_xtm%d" % slot, ld), ("vs", cnorm)])
        pfb[pr + "a"] = [("vs", cy)]
        pfb[pr + "b"] = [("vs", cy)]
        xtm_slot_free[slot] = cy
        cb1 = comp("vector", (lambda e, qi=qi: e.bn_stats(
            sb["bst"][:, 0:6], sb["O"][:, qi * 1024: qi * 1024 + 512])),
            [("vs", cy), ("ss", ep_s), ("vs", cnt["vs"])])
        cb2 = comp("vector", (lambda e, qi=qi: e.bn_stats(
            sb["bst"][:, 6:12], sb["O"][:, qi * 1024 + 512: qi * 1024 + 1024])),
            [("vs", cb1)])
        cag = comp("vector", (lambda e: e.bn_aggr(
            sb["bmv"][:, 0:2], sb["bst"][:, 0:12])), [("vs", cb2)])
        crs0 = comp("scalar", (lambda e: e.activation(
            out=sb["mm2"][:, 0:1], in_=sb["bmv"][:, 1:2], func=AF.Sqrt,
            bias=sb["epsv"][:, 0:1])), [("vs", cag), ("ss", ep_s)])
        crs = comp("vector", (lambda e: e.reciprocal(
            out=sb["tv"][:, 0:1], in_=sb["mm2"][:, 0:1])),
            [("ss", crs0), ("vs", cnt["vs"])])
        cx1 = comp("vector", (lambda e, qi=qi: e.tensor_scalar(
            out=sb["O"][:, qi * 1024:(qi + 1) * 1024],
            in0=sb["O"][:, qi * 1024:(qi + 1) * 1024],
            scalar1=sb["bmv"][:, 0:1], scalar2=sb["tv"][:, 0:1],
            op0=ALU.subtract, op1=ALU.mult)), [("vs", crs)])
        ep_s = crs0
        ep_v = cx1
        x1_ready_v[qi] = cx1

    T_ts, T_vs, T_ss = cnt["ts"], cnt["vs"], cnt["ss"]
    SS_ATTN = cnt["ss"]
    es_a.close()

    es_b = ExitStack()
    sbuf("A", [128, HC * TO], BF16, es_b)
    sbuf("Bb", [128, HC * TO], BF16, es_b)
    sbuf("x1f", [128, DC * TO], BF16, es_b)
    NWB = 3
    for i in range(NWB):
        sbuf("wb%d" % i, [128, HC * 128], BF16, es_b)
    sbuf("wc0", [128, 1024], BF16, es_b)
    sbuf("wc1", [128, 1024], BF16, es_b)
    sbuf("wc2", [128, 1024], BF16, es_b)
    sbuf("xc0", [128, 1024], BF16, es_b)
    sbuf("xc1", [128, 1024], BF16, es_b)
    sbuf("ea", [128, TO], F32, es_b)
    sbuf("Ob0", [128, 1024], BF16, es_b)
    sbuf("Ofm0", [128, 1024], BF16, es_b)
    ps["F0"] = es_b.enter_context(nc.psum_tensor("F0", [128, 1024], F32))
    ps["F1"] = es_b.enter_context(nc.psum_tensor("F1", [128, 1024], F32))
    ps["F2"] = es_b.enter_context(nc.psum_tensor("F2", [128, 1024], F32))
    ps["PTF"] = es_b.enter_context(nc.psum_tensor("PTF", [128, 1024], BF16))
    ps["PTE"] = es_b.enter_context(nc.psum_tensor("PTE", [128, 1024], BF16))
    rot = {"i": 0}
    ep_state = {"ob_free": {0: 0, 1: 0}, "ofm_readers": {0: 0, 1: 0},
                "pte_evac": T_ss}

    wslot_free = [0] * NWB
    ring = 0

    def wload(src_dram, oc, ncols, waits_extra=(), queue="sync"):
        nonlocal ring
        slot = ring % NWB
        ring += 1
        sem = "d_w%d" % slot
        w = [("ts", wslot_free[slot])] + list(waits_extra)
        if src_dram in CAST_GATE:
            w.append(CAST_GATE[src_dram])
        my = dma(queue, (lambda e, src_dram=src_dram, oc=oc, ncols=ncols,
                         slot=slot: e.dma_start(
            out=sb["wb%d" % slot][:, 0:ncols],
            in_=dt[src_dram][oc * 128:(oc + 1) * 128, 0:ncols])), sem, waits=w)
        return slot, my

    wc_free = [0, 0, 0]
    wc_ring = [0]

    def w3load(j):
        slot = wc_ring[0] % 3
        wc_ring[0] += 1
        sem = "d_wc%d" % slot
        w = [("ts", wc_free[slot]), ("g_cw3", G_CW3)]
        my = dma("sync", (lambda e, j=j, slot=slot: e.dma_start(
            out=sb["wc%d" % slot][:, 0:512],
            in_=dt["w3s"][j * 128:(j + 1) * 128, :])), sem, waits=w)
        return slot, my

    pfb = {k: [("vs", T_vs), ("ss", T_ss)]
           for k in ("F0a", "F0b", "F1a", "F1b", "F2a", "F2b", "PTF")}
    BANK = {
        "F0a": lambda: ps["F0"][:, 0:512], "F0b": lambda: ps["F0"][:, 512:1024],
        "F1a": lambda: ps["F1"][:, 0:512], "F1b": lambda: ps["F1"][:, 512:1024],
        "F2a": lambda: ps["F2"][:, 0:512], "F2b": lambda: ps["F2"][:, 512:1024],
    }

    x1f_readers = 0
    A_readers = 0
    B_readers = 0
    ea_free = 0
    za_free = 0
    out_dmas = 0

    def emit_x1cast(tau):
        ev = 0
        for tb in range(4):
            xcn = "xc0" if tb % 2 == 0 else "xc1"
            qt = 4 * tau + tb
            cv = comp("vector", (lambda e, qt=qt, xcn=xcn: e.tensor_copy(
                out=sb[xcn][:], in_=sb["O"][:, qt * 1024:(qt + 1) * 1024])),
                [("ts", p_free.get("xc" + str(tb % 2), 0)),
                 ("vs", x1_ready_v[qt]),
                 ("ts", T_ts if tau == 0 else 0)])
            wts = [("vs", cv), ("ss", ev if (tau > 0 or tb > 0) else SS_ATTN)]
            last = 0
            for dc in range(DC):
                fn = (lambda e, dc=dc, xcn=xcn: e.transpose(
                    ps["PTF"][:, dc * 128:(dc + 1) * 128],
                    sb[xcn][:, dc * 128:(dc + 1) * 128],
                    sb["identb"][:]))
                last = comp("tensor", fn, wts)
                wts = ()
            p_free["xc" + str(tb % 2)] = last
            ev = comp("scalar", (lambda e, tb=tb: e.copy(
                out=sb["x1f"][:].rearrange("p (n t) -> p n t", t=TO)[:, :, tb * 128:(tb + 1) * 128],
                in_=ps["PTF"][:].rearrange("p (n t) -> p n t", t=128))),
                [("ts", last), ("ts", x1f_readers)])
        return ev

    for qi in range(4):
        emit_ep(qi)
    wait("sync", [("ts", T_ts), ("vs", T_vs), ("ss", T_ss)])
    next_x1f_evac = emit_x1cast(0)
    _ep_next = [4]
    for tau in range(NFT):
        x1f_evac = next_x1f_evac
        uA_done = 0
        for oc in range(HC):
            pr = ["F0", "F1", "F2"][rot["i"] % 3]
            rot["i"] += 1
            slot1, l1 = wload("w1s", oc, DC * 128)
            slot2, l2 = wload("w2s", oc, DC * 128, queue="gpsimd")
            if tau == 0:
                emit_tail_chunks(1)
                want_ep = 4 + ((oc + 1) * (NQT - 4)) // HC
                while _ep_next[0] < min(NQT, want_ep):
                    emit_tail_chunks(1)
                    emit_ep(_ep_next[0])
                    _ep_next[0] += 1
            wts = [("d_w%d" % slot1, l1), ("ss", x1f_evac)] + pfb[pr + "a"] + pfb[pr + "b"]
            lasta = lastb = 0
            for ic in range(DC):
                fn = (lambda e, pr=pr, ic=ic, slot=slot1: e.matmul(
                    ps[pr][:, 0:512],
                    sb["wb%d" % slot][:, ic * 128:(ic + 1) * 128],
                    sb["x1f"][:, ic * TO:(ic + 1) * TO],
                    start=(ic == 0), stop=(ic == DC - 1)))
                lasta = comp("tensor", fn, wts)
                wts = ()
            wts = [("d_w%d" % slot2, l2)]
            for ic in range(DC):
                fn = (lambda e, pr=pr, ic=ic, slot=slot2: e.matmul(
                    ps[pr][:, 512:1024],
                    sb["wb%d" % slot][:, ic * 128:(ic + 1) * 128],
                    sb["x1f"][:, ic * TO:(ic + 1) * TO],
                    start=(ic == 0), stop=(ic == DC - 1)))
                lastb = comp("tensor", fn, wts)
                wts = ()
            wslot_free[slot1] = lasta
            wslot_free[slot2] = lastb
            x1f_readers = lastb
            ca = comp("scalar", (lambda e, pr=pr, oc=oc: e.activation(
                out=sb["ea"][:], in_=ps[pr][:, 0:512], func=AF.Silu,
                scale=sb["s1"][:, oc:oc + 1])), [("ts", lasta), ("vs", ea_free)])
            cu = comp("vector", (lambda e, pr=pr, oc=oc: e.scalar_tensor_tensor(
                out=sb["A"][:, oc * TO:(oc + 1) * TO], in0=ps[pr][:, 512:1024],
                scalar=sb["s2"][:, oc:oc + 1], in1=sb["ea"][:],
                op0=ALU.mult, op1=ALU.mult)),
                [("ts", lastb), ("ss", ca), ("ts", A_readers if oc == 0 else 0)])
            ea_free = cu
            pfb[pr + "a"] = [("ss", ca)]
            pfb[pr + "b"] = [("vs", cu)]
            uA_done = cu

        def h_stage(src_sb, dst_sb, wname, srt, src_ready, dst_readers):
            halves = ["F2a", "F2b", "F0a", "F0b"]
            u_done = 0
            u_evs = []
            for oc in range(HC):
                bk = halves[oc % 4]
                slot, ld = wload(wname, oc, HC * 128, queue="gpsimd")
                wts = [("d_w%d" % slot, ld)] + ([src_ready] if oc == 0 else []) + pfb[bk]
                last = 0
                for ic in range(HC):
                    fn = (lambda e, bk=bk, ic=ic, slot=slot, src_sb=src_sb: e.matmul(
                        BANK[bk](),
                        sb["wb%d" % slot][:, ic * 128:(ic + 1) * 128],
                        sb[src_sb][:, ic * TO:(ic + 1) * TO],
                        start=(ic == 0), stop=(ic == HC - 1)))
                    last = comp("tensor", fn, wts)
                    wts = ()
                wslot_free[slot] = last
                cu = comp("scalar", (lambda e, bk=bk, oc=oc, srt=srt, dst_sb=dst_sb: e.activation(
                    out=sb[dst_sb][:, oc * TO:(oc + 1) * TO], in_=BANK[bk](),
                    func=AF.Gelu, scale=sb[srt][:, oc:oc + 1])),
                    [("ts", last), ("ts", dst_readers if oc == 0 else 0)])
                pfb[bk] = [("ss", cu)]
                u_done = cu
                u_evs.append(cu)
            return u_done, u_evs

        if tau == 0:
            while _ep_next[0] < NQT:
                emit_tail_chunks(1)
                emit_ep(_ep_next[0])
                _ep_next[0] += 1
        uB_done, _ = h_stage("A", "Bb", "h1tp", "s1h", ("vs", uA_done), B_readers)
        A_readers_save = cnt["ts"]
        if tau == 0:
            emit_tail_chunks(len(cast_tail))
        uA2_done, uA2_evs = h_stage("Bb", "A", "h2tp", "s2h", ("ss", uB_done), A_readers_save)
        B_readers = cnt["ts"]
        A_readers = 0
        if tau + 1 < NFT:
            next_x1f_evac = emit_x1cast(tau + 1)

        z_done = {}
        for oh in range(2):
            FT = ["F0a", "F0b", "F1a", "F1b"] if oh == 0 else ["F0a", "F0b", "F2a", "F2b"]
            wts_f = []
            for tb in range(4):
                wts_f += pfb[FT[tb]]
            lastmm = {tb: 0 for tb in range(4)}
            for ic in range(HC):
                slot, ld = w3load(oh * HC + ic)
                wts = [("d_wc%d" % slot, ld)]
                if oh == 0:
                    wts.append(("ss", uA2_evs[ic]))
                if ic == 0:
                    wts = wts + wts_f
                for tb in range(4):
                    fn = (lambda e, bk=FT[tb], ic=ic, slot=slot, tb=tb: e.matmul(
                        BANK[bk](),
                        sb["A"][:, ic * TO + tb * 128: ic * TO + tb * 128 + 128],
                        sb["wc%d" % slot][:, 0:512],
                        start=(ic == 0), stop=(ic == HC - 1)))
                    lastmm[tb] = comp("tensor", fn, wts)
                    wts = ()
                wc_free[slot] = lastmm[3]
            A_readers = lastmm[3]
            for tb in range(4):
                qt = 4 * tau + tb
                czm = comp("vector", (lambda e, bk=FT[tb], oh=oh: e.tensor_tensor(
                    out=BANK[bk](), in0=BANK[bk](),
                    in1=sb["s3b"][:, oh * 512:(oh + 1) * 512],
                    op=ALU.mult)), [("ts", lastmm[tb])])
                cz = comp("vector", (lambda e, bk=FT[tb], qt=qt, oh=oh: e.tensor_tensor(
                    out=sb["O"][:, qt * 1024 + oh * 512: qt * 1024 + (oh + 1) * 512],
                    in0=BANK[bk](),
                    in1=sb["O"][:, qt * 1024 + oh * 512: qt * 1024 + (oh + 1) * 512],
                    op=ALU.add)), [("vs", czm)])
                pfb[FT[tb]] = [("vs", cz)]
                z_done[(tb, oh)] = cz

        for tb in range(4):
            qt = 4 * tau + tb
            cb1 = comp("vector", (lambda e, qt=qt: e.bn_stats(
                sb["bst"][:, 0:6], sb["O"][:, qt * 1024: qt * 1024 + 512])),
                [("vs", z_done[(tb, 0)]), ("vs", z_done[(tb, 1)]),
                 ("ss", ep_s), ("vs", cnt["vs"])])
            cb2 = comp("vector", (lambda e, qt=qt: e.bn_stats(
                sb["bst"][:, 6:12], sb["O"][:, qt * 1024 + 512: qt * 1024 + 1024])),
                [("vs", cb1)])
            cag = comp("vector", (lambda e: e.bn_aggr(
                sb["bmv"][:, 0:2], sb["bst"][:, 0:12])), [("vs", cb2)])
            crs0 = comp("scalar", (lambda e: e.activation(
                out=sb["mm2"][:, 0:1], in_=sb["bmv"][:, 1:2], func=AF.Sqrt,
                bias=sb["epsv"][:, 0:1])), [("vs", cag), ("ss", ep_s)])
            crs = comp("vector", (lambda e: e.reciprocal(
                out=sb["tv"][:, 0:1], in_=sb["mm2"][:, 0:1])),
                [("ss", crs0), ("vs", cnt["vs"])])
            cx1 = comp("vector", (lambda e, qt=qt: e.tensor_scalar(
                out=sb["O"][:, qt * 1024:(qt + 1) * 1024],
                in0=sb["O"][:, qt * 1024:(qt + 1) * 1024],
                scalar1=sb["bmv"][:, 0:1], scalar2=sb["tv"][:, 0:1],
                op0=ALU.subtract, op1=ALU.mult)), [("vs", crs)])
            ep_s = crs0
            out_dmas = dma("gpsimd", (lambda e, qt=qt: e.dma_start(
                out=dt["out"].rearrange("(t p) d -> p t d", p=128)[:, qt:qt + 1, :],
                in_=sb["O"][:, qt * 1024:(qt + 1) * 1024].rearrange(
                    "p (t d) -> p t d", d=1024))),
                "d_out", waits=[("vs", cx1)])

    wait("sync", [("d_out", dcnt["d_out"])])

    with es_top:
      with es_wt:
        with es_b:
            with nc.Block() as block:
                @block.sync
                def _(e):
                    for f in streams["sync"]:
                        f(e)

                @block.tensor
                def _(e):
                    for f in streams["tensor"]:
                        f(e)

                @block.vector
                def _(e):
                    for f in streams["vector"]:
                        f(e)

                @block.scalar
                def _(e):
                    for f in streams["scalar"]:
                        f(e)

                @block.gpsimd
                def _(e):
                    for f in streams["gpsimd"]:
                        f(e)

    n_inst = sum(len(v) for v in streams.values())
    return nc, n_inst



def _pack_oc(wt, n_ic, n_oc, ocw=128):
    return np.ascontiguousarray(
        wt.reshape(n_ic, 128, n_oc, ocw).transpose(2, 0, 1, 3).reshape(n_oc * n_ic * 128, ocw))


def _pack_pc(wt, n_ic, n_oc):
    return np.ascontiguousarray(
        wt.reshape(n_ic, 128, n_oc, 128).transpose(2, 1, 0, 3).reshape(n_oc * 128, n_ic * 128))


def _stage(cfg: Cfg, inputs):
    D, DC, HC = cfg.D, cfg.DC, cfg.HC
    x = np.asarray(inputs["L0_attn_x"], dtype=np.float32)
    W = np.asarray(inputs["L0_attn_W"], dtype=np.float32)
    B = x.shape[0]
    shared = {}
    shared["wattT"] = np.ascontiguousarray(W.T)
    shared["wattK"] = np.ascontiguousarray(W[:1024, :])
    w1t = np.ascontiguousarray(np.asarray(inputs["L0_ffn_w1_w"], dtype=np.float32).T)
    w2t = np.ascontiguousarray(np.asarray(inputs["L0_ffn_w2_w"], dtype=np.float32).T)
    h1t = np.ascontiguousarray(np.asarray(inputs["L0_ffn_h1_w"], dtype=np.float32).T)
    h2t = np.ascontiguousarray(np.asarray(inputs["L0_ffn_h2_w"], dtype=np.float32).T)
    w3t = np.ascontiguousarray(np.asarray(inputs["L0_ffn_w3_w"], dtype=np.float32).T)
    shared["w1tp"] = _pack_pc(w1t, DC, HC)
    shared["w2tp"] = _pack_pc(w2t, DC, HC)
    shared["h1tp"] = _pack_pc(h1t, HC, HC)
    shared["h2tp"] = _pack_pc(h2t, HC, HC)
    shared["w3tp"] = _pack_oc(w3t, HC, 2, 512)
    for nm, key in (("s1", "L0_ffn_w1_s"), ("s2", "L0_ffn_w2_s"),
                    ("s1h", "L0_ffn_h1_s"), ("s2h", "L0_ffn_h2_s")):
        s = np.asarray(inputs[key], dtype=np.float32)
        shared[nm] = np.ascontiguousarray(s.reshape(HC, 128).T)
    shared["s3b"] = np.ascontiguousarray(
        np.tile(np.asarray(inputs["L0_ffn_w3_s"], dtype=np.float32)[None, :], (128, 1)))
    shared["uio"] = np.ascontiguousarray(
        (np.arange(1024, dtype=np.float32)[None, :]
         - np.arange(128, dtype=np.float32)[:, None]).astype(np.float16))
    shared["identb"] = np.eye(128, dtype=ml_dtypes.bfloat16)

    xfm_all = [np.ascontiguousarray(x[b].T) for b in range(B)]
    per_seq = cfg.n_cores // B
    in_maps = []
    for c in range(cfg.n_cores):
        seq = c // per_seq
        r = c % per_seq
        rows = [(cfg.cps * p + r) * 128 for p in range(cfg.NQT)]
        m = dict(shared)
        m["xfm"] = xfm_all[seq]
        m["xraw"] = x[seq]
        m["xq"] = np.ascontiguousarray(np.concatenate(
            [xfm_all[seq][:, q0:q0 + 128] for q0 in rows], axis=1))
        m["xtm"] = np.ascontiguousarray(np.concatenate(
            [x[seq][q0:q0 + 128, :] for q0 in rows], axis=0))
        th = np.zeros((128, cfg.NVIS), dtype=np.float32)
        v = 0
        for kt in range(cfg.NKT):
            for s in cfg.slots[kt]:
                th[:, v] = float(rows[s] - kt * 1024)
                v += 1
        m["theta"] = th
        in_maps.append(m)
    return in_maps


def run_cfg(cfg: Cfg, inputs, runner=None):
    in_maps = _stage(cfg, inputs)
    nc, n_inst = build_program(cfg)
    if runner is None:
        res = run_bass_kernel_spmd(nc, in_maps, core_ids=list(range(cfg.n_cores)))
        results = res.results
    else:
        results = runner(nc, in_maps)
    x = np.asarray(inputs["L0_attn_x"])
    B, S, D = x.shape
    out = np.empty((B, S, D), dtype=np.float32)
    per_seq = cfg.n_cores // B
    for c in range(cfg.n_cores):
        seq = c // per_seq
        r = c % per_seq
        res = results[c]["out"]
        for p in range(cfg.NQT):
            q0 = (cfg.cps * p + r) * 128
            out[seq, q0:q0 + 128, :] = res[p * 128:(p + 1) * 128]
    return out


def kernel(**inputs):
    return run_cfg(CFG_FULL, inputs)


# version 24
# speedup vs baseline: 1.0405x; 1.0065x over previous
import numpy as np
from contextlib import ExitStack
import ml_dtypes

import concourse.bass as bass
import concourse.mybir as mybir
from concourse.bass_utils import run_bass_kernel_spmd

F32 = mybir.dt.float32
BF16 = mybir.dt.bfloat16
AF = mybir.ActivationFunctionType
ALU = mybir.AluOpType
AX = mybir.AxisListType

NEG_BIG = -30000.0


class Cfg:
    def __init__(self, seq=8192, qpc=2048, n_cores=8):
        self.D = 1024
        self.DH = 4096
        self.DC = 8
        self.HC = 32
        self.seq = seq
        self.qpc = qpc
        self.NKT = seq // 1024
        self.NQT = qpc // 128
        self.TO = 512
        self.NFT = qpc // 512
        self.n_cores = n_cores
        self.cps = n_cores // 2
        self.G = 8 // self.cps
        self.slots = [[p for p in range(self.NQT) if p // self.G >= kt]
                      for kt in range(self.NKT)]
        self.NVIS = sum(len(s) for s in self.slots)


CFG_FULL = Cfg(8192, 2048, 8)
CFG_MINI = Cfg(4096, 2048, 4)


def build_program(cfg: Cfg):
    D, DH, DC, HC = cfg.D, cfg.DH, cfg.DC, cfg.HC
    NKT, NQT, TO, NFT = cfg.NKT, cfg.NQT, cfg.TO, cfg.NFT
    QPC, SEQ = cfg.qpc, cfg.seq
    NVIS = cfg.NVIS
    SLOTS = cfg.slots

    nc = bass.Bass("TRN2", target_bir_lowering=False, debug=False)

    dt = {}
    dt["xfm"] = nc.dram_tensor("xfm", [D, SEQ], F32, kind="ExternalInput").ap()
    dt["xq"] = nc.dram_tensor("xq", [D, QPC], F32, kind="ExternalInput").ap()
    dt["xtm"] = nc.dram_tensor("xtm", [QPC, D], F32, kind="ExternalInput").ap()
    dt["wattT"] = nc.dram_tensor("wattT", [D, 2 * D], F32, kind="ExternalInput").ap()
    dt["wattK"] = nc.dram_tensor("wattK", [D, D], F32, kind="ExternalInput").ap()
    dt["xraw"] = nc.dram_tensor("xraw", [SEQ, D], F32, kind="ExternalInput").ap()
    dt["w1tp"] = nc.dram_tensor("w1tp", [HC * 128, DC * 128], F32, kind="ExternalInput").ap()
    dt["w2tp"] = nc.dram_tensor("w2tp", [HC * 128, DC * 128], F32, kind="ExternalInput").ap()
    dt["h1tp"] = nc.dram_tensor("h1tp", [HC * 128, HC * 128], F32, kind="ExternalInput").ap()
    dt["h2tp"] = nc.dram_tensor("h2tp", [HC * 128, HC * 128], F32, kind="ExternalInput").ap()
    dt["w3tp"] = nc.dram_tensor("w3tp", [2 * HC * 128, 512], F32, kind="ExternalInput").ap()
    for s in ("s1", "s2", "s1h", "s2h"):
        dt[s] = nc.dram_tensor(s, [128, HC], F32, kind="ExternalInput").ap()
    dt["s3b"] = nc.dram_tensor("s3b", [128, D], F32, kind="ExternalInput").ap()
    dt["theta"] = nc.dram_tensor("theta", [128, NVIS], F32, kind="ExternalInput").ap()
    dt["uio"] = nc.dram_tensor("uio", [128, 1024], mybir.dt.float16, kind="ExternalInput").ap()
    dt["identb"] = nc.dram_tensor("identb", [128, 128], BF16, kind="ExternalInput").ap()
    dt["out"] = nc.dram_tensor("out", [QPC, D], F32, kind="ExternalOutput").ap()
    dt["w1s"] = nc.dram_tensor("w1s", [HC * 128, DC * 128], BF16, kind="Internal").ap()
    dt["w2s"] = nc.dram_tensor("w2s", [HC * 128, DC * 128], BF16, kind="Internal").ap()
    dt["w3s"] = nc.dram_tensor("w3s", [2 * HC * 128, 512], BF16, kind="Internal").ap()

    sem_names = ["ts", "vs", "ss", "gs",
                 "d_const", "g_xk0", "g_xk1", "g_q0", "g_q1", "d_xtm0", "d_xtm1", "d_out",
                 "g_wt", "g_wk", "g_xt0", "g_xt1", "g_cw1", "g_cw2", "g_cw3", "g_ch1", "g_ch2",
                 "d_w0", "d_w1", "d_w2", "d_w3",
                 "d_wc0", "d_wc1", "d_wc2"]
    es_top = ExitStack()
    SEM = {n: es_top.enter_context(nc.semaphore(n)) for n in sem_names}

    sb = {}

    def sbuf(name, shape, dtp, stack):
        sb[name] = stack.enter_context(nc.sbuf_tensor("sb_" + name, shape, dtp))
        return sb[name]

    sbuf("O", [128, NQT * 1024], F32, es_top)
    for s in ("s1", "s2", "s1h", "s2h"):
        sbuf(s, [128, HC], F32, es_top)
    sbuf("s3b", [128, D], F32, es_top)
    sbuf("identb", [128, 128], BF16, es_top)
    sbuf("lrun", [128, NQT], F32, es_top)
    for s in ("rh", "s1v", "s2v", "nmean", "mm2", "tv", "epsv"):
        sbuf(s, [128, 1], F32, es_top)
    sbuf("bst", [128, 12], F32, es_top)
    sbuf("bmv", [128, 2], F32, es_top)
    sbuf("late8", [128, 2048], F32, es_top)

    streams = {"sync": [], "tensor": [], "vector": [], "scalar": [], "gpsimd": []}
    cnt = {"ts": 0, "vs": 0, "ss": 0, "gs": 0}
    dcnt = {n: 0 for n in sem_names if n not in ("ts", "vs", "ss", "gs")}

    SELF = {"tensor": "ts", "vector": "vs", "scalar": "ss", "gpsimd": "gs"}

    def comp(eng, build_fn, waits=()):
        s = SELF[eng]
        cnt[s] += 1
        my = cnt[s]
        w = tuple((a, b) for (a, b) in waits if b > 0)

        def f(e, build_fn=build_fn, w=w, s=s):
            for sname, v in w:
                e.wait_ge(SEM[sname], v)
            build_fn(e).then_inc(SEM[s], 1)
        streams[eng].append(f)
        return my

    def dma(eng, build_fn, sem_name, waits=()):
        dcnt[sem_name] += 16
        my = dcnt[sem_name]
        w = tuple((a, b) for (a, b) in waits if b > 0)

        def f(e, build_fn=build_fn, w=w, sem_name=sem_name):
            for sname, v in w:
                e.wait_ge(SEM[sname], v)
            build_fn(e).then_inc(SEM[sem_name], 16)
        streams[eng].append(f)
        return my

    def wait(eng, waits):
        w = tuple((a, b) for (a, b) in waits if b > 0)

        def f(e, w=w):
            for sname, v in w:
                e.wait_ge(SEM[sname], v)
        streams[eng].append(f)

    es_wt = ExitStack()
    sbuf("WTv", [128, DC * 1024], BF16, es_wt)
    es_a = ExitStack()
    sbuf("QP", [128, DC * QPC], BF16, es_a)
    sbuf("xk0", [128, DC * 1024], BF16, es_a)
    sbuf("XT0", [128, DC * 1024], BF16, es_a)
    sbuf("P0", [128, 1024], BF16, es_a)
    sbuf("P1", [128, 1024], BF16, es_a)
    sbuf("PT0", [128, 1024], BF16, es_a)
    sbuf("PT1", [128, 1024], BF16, es_a)
    sbuf("uio", [128, 1024], mybir.dt.float16, es_a)
    sbuf("theta", [128, NVIS], F32, es_a)
    sbuf("msk", [128, 1024], BF16, es_a)
    sbuf("negm", [128, NQT], F32, es_a)
    sbuf("mpos", [128, NQT], F32, es_a)
    sbuf("alpha", [128, 4], F32, es_a)
    sbuf("lpart", [128, 4], F32, es_a)
    sbuf("nmp", [128, 1], F32, es_a)

    for name in ("uio", "theta", "s1", "s2", "s1h", "s2h", "s3b", "identb"):
        dma("sync", (lambda e, n=name: e.dma_start(out=sb[n][:], in_=dt[n])), "d_const")
    C_CONST = dcnt["d_const"]

    comp("vector", lambda e: e.memset(sb["epsv"][:], 1e-5))
    es_q = ExitStack()
    sbuf("Qq0", [128, DC * 512], BF16, es_q)
    sbuf("Qq1", [128, DC * 512], BF16, es_q)
    sbuf("WK", [128, DC * D], BF16, es_q)
    ps = {}
    ps["S0"] = es_a.enter_context(nc.psum_tensor("S0", [128, 1024], F32))
    ps["S1"] = es_a.enter_context(nc.psum_tensor("S1", [128, 1024], F32))
    ps["PV"] = es_a.enter_context(nc.psum_tensor("PV", [128, 1024], F32))
    ps["PT0"] = es_a.enter_context(nc.psum_tensor("PT0p", [128, 1024], BF16))
    ps["PT1"] = es_a.enter_context(nc.psum_tensor("PT1p", [128, 1024], BF16))

    dma("gpsimd", (lambda e: e.dma_start(
        out=sb["WK"][:].rearrange("p (dc c) -> p dc c", c=D),
        in_=dt["wattK"].rearrange("(dc p) c -> p dc c", p=128))), "g_wk")
    dma("gpsimd", (lambda e: e.dma_start(
        out=sb["Qq0"][:].rearrange("p (ic c) -> p ic c", c=512),
        in_=dt["xq"][:, 0:512].rearrange("(ic p) c -> p ic c", p=128))), "g_q0")
    G_WK = dcnt["g_wk"]

    cast_chunks = []

    def cast_rows(src, dst, rows, cols, sem):
        step = max(128, (2 * 1024 * 1024) // (cols * 4))
        n = 0
        for r0 in range(0, rows, step):
            r1 = min(rows, r0 + step)
            cast_chunks.append((src, dst, r0, r1, sem))
            n += 16
        return n

    G_CW1 = cast_rows("w1tp", "w1s", HC * 128, DC * 128, "g_cw1")
    G_CW2 = cast_rows("w2tp", "w2s", HC * 128, DC * 128, "g_cw2")
    G_CW3 = cast_rows("w3tp", "w3s", 2 * HC * 128, 512, "g_cw3")
    cast_tail = []
    CAST_GATE = {"w1s": ("g_cw1", G_CW1), "w2s": ("g_cw2", G_CW2)}
    _cast_i = [0]
    _tail_i = [0]

    def _emit_chunk(tup):
        s_, d_, r0, r1, sem = tup
        dma("gpsimd", (lambda e, r0=r0, r1=r1, s_=s_, d_=d_: e.dma_start(
            out=dt[d_][r0:r1, :], in_=dt[s_][r0:r1, :])), sem)

    def emit_cast_chunks(n):
        bound = min(len(cast_chunks), _cast_i[0] + n)
        while _cast_i[0] < bound:
            _emit_chunk(cast_chunks[_cast_i[0]])
            _cast_i[0] += 1

    def emit_tail_chunks(n):
        bound = min(len(cast_tail), _tail_i[0] + n)
        while _tail_i[0] < bound:
            _emit_chunk(cast_tail[_tail_i[0]])
            _tail_i[0] += 1

    last_pv_mm = 0
    pair_free = {"S0": [], "S1": [], "PV": []}
    pt_free = {0: 0, 1: 0}
    p_free = {0: 0, 1: 0}
    xk_war = {0: 0, 1: 0}
    xt_war = {0: 0, 1: 0}
    xk_done = {}
    xt_done = {}

    vS, vMadd, vNegm, vExp, vAlpha, vPT, vPTe, vPV, vO, vL = ({} for _ in range(10))
    _vctr = [0]
    _vmap = {}

    def vid(kt, qi):
        key = (kt, qi)
        if key not in _vmap:
            _vmap[key] = _vctr[0]
            _vctr[0] += 1
        return _vmap[key]

    def xk_load_cast(kt, qp_gate=0):
        buf = "xk%d" % (kt % 2)
        wts = []
        if xk_war[kt % 2]:
            wts.append(("ts", xk_war[kt % 2]))
        if qp_gate:
            wts.append(("ts", qp_gate))
        xk_done[kt] = dma("gpsimd", (lambda e, kt=kt, buf=buf: e.dma_start(
            out=sb[buf][:].rearrange("p (ic c) -> p ic c", c=1024),
            in_=dt["xfm"][:, kt * 1024:(kt + 1) * 1024].rearrange(
                "(ic p) c -> p ic c", p=128))),
            "g_xk%d" % (kt % 2), waits=wts)

    def xt_load_cast(kt, qp_gate=0):
        buf = "XT%d" % (kt % 2)
        wts = []
        if xt_war[kt % 2]:
            wts.append(("ts", xt_war[kt % 2]))
        if qp_gate:
            wts.append(("ts", qp_gate))
        xt_done[kt] = dma("gpsimd", (lambda e, kt=kt, buf=buf: e.dma_start(
            out=sb[buf][:].rearrange("p (kc c) -> p kc c", c=1024),
            in_=dt["xraw"][kt * 1024:(kt + 1) * 1024, :].rearrange(
                "(kc p) c -> p kc c", p=128))),
            "g_xt%d" % (kt % 2), waits=wts)

    def emit_qprime():
        halves = [("S0", 0), ("S0", 512), ("S1", 0), ("S1", 512)]
        hfree = [[], [], [], []]
        bank_last = {"S0": 0, "S1": 0}
        q_last_mm = {}
        ev = 0
        for q in range(QPC // 512):
            qb = "Qq%d" % (q % 2)
            if q >= 2:
                dma("gpsimd", (lambda e, q=q, qb=qb: e.dma_start(
                    out=sb[qb][:].rearrange("p (ic c) -> p ic c", c=512),
                    in_=dt["xq"][:, q * 512:(q + 1) * 512].rearrange(
                        "(ic p) c -> p ic c", p=128))),
                    "g_q%d" % (q % 2), waits=[("ts", q_last_mm[q - 2])])
            for ib in range(DC):
                hi = (q * DC + ib) % 4
                pr, off = halves[hi]
                wts0 = list(hfree[hi])
                last = 0
                for dc in range(DC):
                    wts = wts0
                    wts0 = []
                    if ib == 0 and dc == 0:
                        wts = wts + [("g_q%d" % (q % 2), 16 * (q // 2 + 1))]
                        if q == 0:
                            wts = wts + [("g_wk", G_WK)]
                    fn = (lambda e, pr=pr, off=off, dc=dc, ib=ib, qb=qb: e.matmul(
                        ps[pr][:, off:off + 512],
                        sb["WK"][:, dc * D + ib * 128: dc * D + ib * 128 + 128],
                        sb[qb][:, dc * 512:(dc + 1) * 512],
                        start=(dc == 0), stop=(dc == DC - 1)))
                    last = comp("tensor", fn, wts)
                ev = comp("scalar", (lambda e, pr=pr, off=off, ib=ib, q=q: e.copy(
                    out=sb["QP"][:, ib * QPC + q * 512: ib * QPC + (q + 1) * 512],
                    in_=ps[pr][:, off:off + 512])), [("ts", last)])
                hfree[hi] = [("ss", ev)]
                bank_last[pr] = ev
            q_last_mm[q] = cnt["ts"]
        pair_free["S0"] = [("ss", bank_last["S0"])]
        pair_free["S1"] = [("ss", bank_last["S1"])]
        return ev

    def emit_S(kt, qi, first_of_kt):
        v = vid(kt, qi)
        buf = "xk%d" % (kt % 2)
        pr = "S0" if qi % 2 == 0 else "S1"
        wts = list(pair_free[pr])
        if first_of_kt:
            wts.append(("g_xk%d" % (kt % 2), xk_done[kt]))
        if kt == 0 and qi == SLOTS[0][0]:
            wts += [("ss", QP_DONE), ("d_const", C_CONST)]
        lastmm = 0
        for ns in range(2):
            for ic in range(DC):
                fn = (lambda e, pr=pr, ns=ns, ic=ic, qi=qi, buf=buf: e.matmul(
                    ps[pr][:, ns * 512:(ns + 1) * 512],
                    sb["QP"][:, ic * QPC + qi * 128: ic * QPC + qi * 128 + 128],
                    sb[buf][:, ic * 1024 + ns * 512: ic * 1024 + (ns + 1) * 512],
                    start=(ic == 0), stop=(ic == DC - 1)))
                lastmm = comp("tensor", fn, wts)
                wts = ()
        vS[v] = lastmm
        xk_war[kt % 2] = lastmm
        pair_free[pr] = []

    def emit_mask(kt, qi):
        v = vid(kt, qi)
        pr = "S0" if qi % 2 == 0 else "S1"
        c1 = comp("vector", (lambda e, v=v: e.tensor_scalar(
            out=sb["msk"][:], in0=sb["uio"][:],
            scalar1=sb["theta"][:, v:v + 1], scalar2=NEG_BIG,
            op0=ALU.is_gt, op1=ALU.mult)),
            [("vs", cnt["vs"]), ("d_const", C_CONST if v == 0 else 0)])
        c2 = comp("vector", (lambda e, pr=pr: e.tensor_tensor(
            out=ps[pr][:, :], in0=ps[pr][:, :], in1=sb["msk"][:], op=ALU.add)),
            [("ts", vS[v]), ("vs", c1)])
        vMadd[v] = c2

    def emit_negm(kt, qi):
        v = vid(kt, qi)
        pr = "S0" if qi % 2 == 0 else "S1"
        c1 = comp("vector", (lambda e, pr=pr: e.reduce_max(
            sb["nmp"][:, 0:1], ps[pr][:, :], axis=AX.X, negate=True)),
            [("vs", vMadd[v])])
        if kt == 0:
            c2 = comp("vector", (lambda e, qi=qi: e.tensor_copy(
                out=sb["negm"][:, qi:qi + 1], in_=sb["nmp"][:, 0:1])), [("vs", c1)])
        else:
            c2 = comp("vector", (lambda e, qi=qi: e.tensor_tensor(
                out=sb["negm"][:, qi:qi + 1], in0=sb["negm"][:, qi:qi + 1],
                in1=sb["nmp"][:, 0:1], op=ALU.min)), [("vs", c1)])
        vNegm[v] = c2

    def emit_exp(kt, qi):
        v = vid(kt, qi)
        pr = "S0" if qi % 2 == 0 else "S1"
        par = qi % 2
        sl = v % 4
        wts = [("vs", vNegm[v]), ("vs", vL.get(v - 4, 0))]
        if kt > 0:
            ca = comp("scalar", (lambda e, qi=qi, sl=sl: e.activation(
                out=sb["alpha"][:, sl:sl + 1], in_=sb["mpos"][:, qi:qi + 1],
                func=AF.Exp, bias=sb["negm"][:, qi:qi + 1])), wts)
            vAlpha[v] = ca
            wts = [("ss", ca)]
        else:
            vAlpha[v] = 0
        ce = comp("scalar", (lambda e, pr=pr, par=par, qi=qi, sl=sl: e.activation(
            out=sb["P%d" % par][:], in_=ps[pr][:, :], func=AF.Exp,
            bias=sb["negm"][:, qi:qi + 1], accum_out=sb["lpart"][:, sl:sl + 1])),
            [("ts", p_free[par])] + wts)
        cm = comp("scalar", (lambda e, qi=qi: e.activation(
            out=sb["mpos"][:, qi:qi + 1], in_=sb["negm"][:, qi:qi + 1],
            func=AF.Copy, scale=-1.0)), [("ss", ce)])
        vExp[v] = ce
        pair_free[pr] = [("ss", ce)]

    def emit_PT(kt, qi):
        v = vid(kt, qi)
        par = qi % 2
        wts = [("ss", vExp[v]), ("ss", pt_free[par])]
        last = 0
        for kc in range(DC):
            fn = (lambda e, par=par, kc=kc: e.transpose(
                ps["PT%d" % par][:, kc * 128:(kc + 1) * 128],
                sb["P%d" % par][:, kc * 128:(kc + 1) * 128],
                sb["identb"][:]))
            last = comp("tensor", fn, wts)
            wts = ()
        vPT[v] = last
        p_free[par] = last

    def emit_PTevac(kt, qi):
        v = vid(kt, qi)
        par = qi % 2
        c1 = comp("scalar", (lambda e, par=par: e.copy(
            out=sb["PT%d" % par][:], in_=ps["PT%d" % par][:, :])),
            [("ts", vPT[v])])
        vPTe[v] = c1
        pt_free[par] = c1

    def emit_PV(kt, qi, first_of_kt):
        nonlocal last_pv_mm
        v = vid(kt, qi)
        par = qi % 2
        buf = "XT%d" % (kt % 2)
        wts = [("ss", vPTe[v])] + pair_free["PV"]
        if first_of_kt:
            wts.append(("g_xt%d" % (kt % 2), xt_done[kt]))
        last = 0
        for ns in range(2):
            for kc in range(DC):
                fn = (lambda e, par=par, ns=ns, kc=kc, buf=buf: e.matmul(
                    ps["PV"][:, ns * 512:(ns + 1) * 512],
                    sb["PT%d" % par][:, kc * 128:(kc + 1) * 128],
                    sb[buf][:, kc * 1024 + ns * 512: kc * 1024 + (ns + 1) * 512],
                    start=(kc == 0), stop=(kc == DC - 1)))
                last = comp("tensor", fn, wts)
                wts = ()
        vPV[v] = last
        last_pv_mm = last
        xt_war[kt % 2] = last
        pair_free["PV"] = []

    def emit_O(kt, qi):
        v = vid(kt, qi)
        sl = v % 4
        if kt == 0:
            c1 = comp("vector", (lambda e, qi=qi: e.tensor_copy(
                out=sb["O"][:, qi * 1024:(qi + 1) * 1024], in_=ps["PV"][:, :])),
                [("ts", vPV[v])])
            cl = comp("vector", (lambda e, qi=qi, sl=sl: e.tensor_copy(
                out=sb["lrun"][:, qi:qi + 1], in_=sb["lpart"][:, sl:sl + 1])),
                [("ss", vExp[v])])
        else:
            c1 = comp("vector", (lambda e, qi=qi, sl=sl: e.scalar_tensor_tensor(
                out=sb["O"][:, qi * 1024:(qi + 1) * 1024],
                in0=sb["O"][:, qi * 1024:(qi + 1) * 1024],
                scalar=sb["alpha"][:, sl:sl + 1], in1=ps["PV"][:, :],
                op0=ALU.mult, op1=ALU.add)),
                [("ts", vPV[v]), ("ss", vAlpha[v])])
            cl = comp("vector", (lambda e, qi=qi, sl=sl: e.scalar_tensor_tensor(
                out=sb["lrun"][:, qi:qi + 1], in0=sb["lrun"][:, qi:qi + 1],
                scalar=sb["alpha"][:, sl:sl + 1], in1=sb["lpart"][:, sl:sl + 1],
                op0=ALU.mult, op1=ALU.add)), [("ss", vExp[v])])
        vO[v] = c1
        vL[v] = cl
        pair_free["PV"] = [("vs", c1)]

    n_chunks = len(cast_chunks)
    xk_load_cast(0)
    dma("gpsimd", (lambda e: e.dma_start(
        out=sb["Qq1"][:].rearrange("p (ic c) -> p ic c", c=512),
        in_=dt["xq"][:, 512:1024].rearrange("(ic p) c -> p ic c", p=128))), "g_q1")
    xt_load_cast(0)
    QP_DONE = emit_qprime()
    qp_last_mm = cnt["ts"]
    dma("gpsimd", (lambda e: e.dma_start(
        out=sb["WTv"][:].rearrange("p (ic c) -> p ic c", c=1024),
        in_=dt["wattT"][:, D:2 * D].rearrange("(ic p) c -> p ic c", p=128))), "g_wt")
    G_WT = dcnt["g_wt"]
    es_q.close()
    sbuf("xk1", [128, DC * 1024], BF16, es_a)
    sbuf("XT1", [128, DC * 1024], BF16, es_a)
    vis_i = 0
    want_prev = 0
    for kt in range(NKT):
        if kt + 1 < NKT:
            xk_load_cast(kt + 1, qp_gate=qp_last_mm if kt == 0 else 0)
            xt_load_cast(kt + 1, qp_gate=qp_last_mm if kt == 0 else 0)
        sl = SLOTS[kt]
        nv = len(sl)
        for k in range(nv + 2):
            if k < nv:
                want = min(n_chunks, int(n_chunks * (vis_i + 1) / max(1, 0.7 * NVIS)))
                emit_cast_chunks(want - want_prev)
                want_prev = want
                vis_i += 1
                emit_S(kt, sl[k], first_of_kt=(k == 0))
                emit_mask(kt, sl[k])
                emit_negm(kt, sl[k])
            if 1 <= k <= nv:
                emit_exp(kt, sl[k - 1])
                emit_PT(kt, sl[k - 1])
            if 2 <= k <= nv + 1:
                emit_PTevac(kt, sl[k - 2])
                emit_PV(kt, sl[k - 2], first_of_kt=(k == 2))
                emit_O(kt, sl[k - 2])

    emit_cast_chunks(len(cast_chunks))

    INV_D = 1.0 / D
    xtm_slot_free = [0, 0]
    x1_ready_v = {}
    ep_s = 0
    ep_v = 0

    def emit_ep(qi):
        nonlocal ep_s, ep_v
        slot = qi % 2
        ld = dma("sync", (lambda e, qi=qi, slot=slot: e.dma_start(
            out=sb["late8"][:, slot * 1024:(slot + 1) * 1024],
            in_=dt["xtm"][qi * 128:(qi + 1) * 128, :])),
            "d_xtm%d" % slot, waits=[("vs", xtm_slot_free[slot])])
        lastkt = qi // cfg.G
        par = 0
        cr0 = comp("vector", (lambda e, qi=qi: e.tensor_scalar_mul(
            out=sb["rh"][:, 0:1], in0=sb["lrun"][:, qi:qi + 1], scalar1=2.0)),
            [("vs", vL[vid(lastkt, qi)]), ("ss", ep_s), ("vs", cnt["vs"])])
        cr = comp("vector", (lambda e: e.reciprocal(
            out=sb["rh"][:, 0:1], in_=sb["rh"][:, 0:1])), [("vs", cr0)])
        cnorm = comp("vector", (lambda e, qi=qi, par=par: e.tensor_scalar_mul(
            out=sb["Ob%d" % par][:], in0=sb["O"][:, qi * 1024:(qi + 1) * 1024],
            scalar1=sb["rh"][:, 0:1])),
            [("vs", cr), ("vs", vO[vid(lastkt, qi)]),
             ("ts", ep_state["ob_free"][par]),
             ("ts", T_ts if qi == 0 else 0)])
        wts = [("vs", cnorm), ("ss", ep_state["pte_evac"])]
        last = 0
        for ic in range(DC):
            fn = (lambda e, par=par, ic=ic: e.transpose(
                ps["PTE"][:, ic * 128:(ic + 1) * 128],
                sb["Ob%d" % par][:, ic * 128:(ic + 1) * 128],
                sb["identb"][:]))
            last = comp("tensor", fn, wts)
            wts = ()
        ep_state["ob_free"][par] = last
        pev = comp("scalar", (lambda e, par=par: e.copy(
            out=sb["Ofm%d" % par][:], in_=ps["PTE"][:, :])),
            [("ts", last), ("ts", ep_state["ofm_readers"][par])])
        ep_state["pte_evac"] = pev
        pr = ["F0", "F1", "F2"][rot["i"] % 3]
        rot["i"] += 1
        wts = [("ss", pev)] + pfb[pr + "a"] + pfb[pr + "b"]
        if qi == 0:
            wts.append(("g_wt", G_WT))
        lastp = 0
        for ns in range(2):
            for ic in range(DC):
                fn = (lambda e, pr=pr, par=par, ns=ns, ic=ic: e.matmul(
                    ps[pr][:, ns * 512:(ns + 1) * 512],
                    sb["Ofm%d" % par][:, ic * 128:(ic + 1) * 128],
                    sb["WTv"][:, ic * 1024 + ns * 512: ic * 1024 + (ns + 1) * 512],
                    start=(ic == 0), stop=(ic == DC - 1)))
                lastp = comp("tensor", fn, wts)
                wts = ()
        ep_state["ofm_readers"][par] = lastp
        cy = comp("vector", (lambda e, qi=qi, slot=slot, pr=pr: e.tensor_tensor(
            out=sb["O"][:, qi * 1024:(qi + 1) * 1024],
            in0=ps[pr][:, 0:1024],
            in1=sb["late8"][:, slot * 1024:(slot + 1) * 1024], op=ALU.add)),
            [("ts", lastp), ("d_xtm%d" % slot, ld), ("vs", cnorm)])
        pfb[pr + "a"] = [("vs", cy)]
        pfb[pr + "b"] = [("vs", cy)]
        xtm_slot_free[slot] = cy
        cb1 = comp("vector", (lambda e, qi=qi: e.bn_stats(
            sb["bst"][:, 0:6], sb["O"][:, qi * 1024: qi * 1024 + 512])),
            [("vs", cy), ("ss", ep_s), ("vs", cnt["vs"])])
        cb2 = comp("vector", (lambda e, qi=qi: e.bn_stats(
            sb["bst"][:, 6:12], sb["O"][:, qi * 1024 + 512: qi * 1024 + 1024])),
            [("vs", cb1)])
        cag = comp("vector", (lambda e: e.bn_aggr(
            sb["bmv"][:, 0:2], sb["bst"][:, 0:12])), [("vs", cb2)])
        crs0 = comp("scalar", (lambda e: e.activation(
            out=sb["mm2"][:, 0:1], in_=sb["bmv"][:, 1:2], func=AF.Sqrt,
            bias=sb["epsv"][:, 0:1])), [("vs", cag), ("ss", ep_s)])
        crs = comp("vector", (lambda e: e.reciprocal(
            out=sb["tv"][:, 0:1], in_=sb["mm2"][:, 0:1])),
            [("ss", crs0), ("vs", cnt["vs"])])
        cx1 = comp("vector", (lambda e, qi=qi: e.tensor_scalar(
            out=sb["O"][:, qi * 1024:(qi + 1) * 1024],
            in0=sb["O"][:, qi * 1024:(qi + 1) * 1024],
            scalar1=sb["bmv"][:, 0:1], scalar2=sb["tv"][:, 0:1],
            op0=ALU.subtract, op1=ALU.mult)), [("vs", crs)])
        ep_s = crs0
        ep_v = cx1
        x1_ready_v[qi] = cx1

    T_ts, T_vs, T_ss = cnt["ts"], cnt["vs"], cnt["ss"]
    SS_ATTN = cnt["ss"]
    es_a.close()

    es_b = ExitStack()
    sbuf("A", [128, HC * TO], BF16, es_b)
    sbuf("Bb", [128, HC * TO], BF16, es_b)
    sbuf("x1f", [128, DC * TO], BF16, es_b)
    NWB = 3
    for i in range(NWB):
        sbuf("wb%d" % i, [128, HC * 128], BF16, es_b)
    sbuf("wc0", [128, 1024], BF16, es_b)
    sbuf("wc1", [128, 1024], BF16, es_b)
    sbuf("wc2", [128, 1024], BF16, es_b)
    sbuf("xc0", [128, 1024], BF16, es_b)
    sbuf("xc1", [128, 1024], BF16, es_b)
    sbuf("ea", [128, TO], F32, es_b)
    sbuf("Ob0", [128, 1024], BF16, es_b)
    sbuf("Ofm0", [128, 1024], BF16, es_b)
    ps["F0"] = es_b.enter_context(nc.psum_tensor("F0", [128, 1024], F32))
    ps["F1"] = es_b.enter_context(nc.psum_tensor("F1", [128, 1024], F32))
    ps["F2"] = es_b.enter_context(nc.psum_tensor("F2", [128, 1024], F32))
    ps["PTF"] = es_b.enter_context(nc.psum_tensor("PTF", [128, 1024], BF16))
    ps["PTE"] = es_b.enter_context(nc.psum_tensor("PTE", [128, 1024], BF16))
    rot = {"i": 0}
    ep_state = {"ob_free": {0: 0, 1: 0}, "ofm_readers": {0: 0, 1: 0},
                "pte_evac": T_ss}

    wslot_free = [0] * NWB
    ring = 0

    def wload(src_dram, oc, ncols, waits_extra=(), queue="sync"):
        nonlocal ring
        slot = ring % NWB
        ring += 1
        sem = "d_w%d" % slot
        w = [("ts", wslot_free[slot])] + list(waits_extra)
        if src_dram in CAST_GATE:
            w.append(CAST_GATE[src_dram])
        my = dma(queue, (lambda e, src_dram=src_dram, oc=oc, ncols=ncols,
                         slot=slot: e.dma_start(
            out=sb["wb%d" % slot][:, 0:ncols],
            in_=dt[src_dram][oc * 128:(oc + 1) * 128, 0:ncols])), sem, waits=w)
        return slot, my

    wc_free = [0, 0, 0]
    wc_ring = [0]

    def w3load(j):
        slot = wc_ring[0] % 3
        wc_ring[0] += 1
        sem = "d_wc%d" % slot
        w = [("ts", wc_free[slot]), ("g_cw3", G_CW3)]
        my = dma("sync", (lambda e, j=j, slot=slot: e.dma_start(
            out=sb["wc%d" % slot][:, 0:512],
            in_=dt["w3s"][j * 128:(j + 1) * 128, :])), sem, waits=w)
        return slot, my

    pfb = {k: [("vs", T_vs), ("ss", T_ss)]
           for k in ("F0a", "F0b", "F1a", "F1b", "F2a", "F2b", "PTF")}
    BANK = {
        "F0a": lambda: ps["F0"][:, 0:512], "F0b": lambda: ps["F0"][:, 512:1024],
        "F1a": lambda: ps["F1"][:, 0:512], "F1b": lambda: ps["F1"][:, 512:1024],
        "F2a": lambda: ps["F2"][:, 0:512], "F2b": lambda: ps["F2"][:, 512:1024],
    }

    x1f_readers = 0
    A_readers = 0
    B_readers = 0
    ea_free = 0
    za_free = 0
    out_dmas = 0

    def emit_x1cast(tau):
        ev = 0
        for tb in range(4):
            xcn = "xc0" if tb % 2 == 0 else "xc1"
            qt = 4 * tau + tb
            cv = comp("vector", (lambda e, qt=qt, xcn=xcn: e.tensor_copy(
                out=sb[xcn][:], in_=sb["O"][:, qt * 1024:(qt + 1) * 1024])),
                [("ts", p_free.get("xc" + str(tb % 2), 0)),
                 ("vs", x1_ready_v[qt]),
                 ("ts", T_ts if tau == 0 else 0)])
            wts = [("vs", cv), ("ss", ev if (tau > 0 or tb > 0) else SS_ATTN)]
            last = 0
            for dc in range(DC):
                fn = (lambda e, dc=dc, xcn=xcn: e.transpose(
                    ps["PTF"][:, dc * 128:(dc + 1) * 128],
                    sb[xcn][:, dc * 128:(dc + 1) * 128],
                    sb["identb"][:]))
                last = comp("tensor", fn, wts)
                wts = ()
            p_free["xc" + str(tb % 2)] = last
            ev = comp("scalar", (lambda e, tb=tb: e.copy(
                out=sb["x1f"][:].rearrange("p (n t) -> p n t", t=TO)[:, :, tb * 128:(tb + 1) * 128],
                in_=ps["PTF"][:].rearrange("p (n t) -> p n t", t=128))),
                [("ts", last), ("ts", x1f_readers)])
        return ev

    for qi in range(4):
        emit_ep(qi)
    wait("sync", [("ts", T_ts), ("vs", T_vs), ("ss", T_ss)])
    next_x1f_evac = emit_x1cast(0)
    _ep_next = [4]
    for tau in range(NFT):
        x1f_evac = next_x1f_evac
        uA_done = 0
        for oc in range(HC):
            pr = ["F0", "F1", "F2"][rot["i"] % 3]
            rot["i"] += 1
            slot1, l1 = wload("w1s", oc, DC * 128)
            slot2, l2 = wload("w2s", oc, DC * 128, queue="gpsimd")
            if tau == 0:
                emit_tail_chunks(1)
                want_ep = 4 + ((oc + 1) * (NQT - 4)) // HC
                while _ep_next[0] < min(NQT, want_ep):
                    emit_tail_chunks(1)
                    emit_ep(_ep_next[0])
                    _ep_next[0] += 1
            wts = [("d_w%d" % slot1, l1), ("ss", x1f_evac)] + pfb[pr + "a"] + pfb[pr + "b"]
            lasta = lastb = 0
            for ic in range(DC):
                fn = (lambda e, pr=pr, ic=ic, slot=slot1: e.matmul(
                    ps[pr][:, 0:512],
                    sb["wb%d" % slot][:, ic * 128:(ic + 1) * 128],
                    sb["x1f"][:, ic * TO:(ic + 1) * TO],
                    start=(ic == 0), stop=(ic == DC - 1)))
                lasta = comp("tensor", fn, wts)
                wts = ()
            wts = [("d_w%d" % slot2, l2)]
            for ic in range(DC):
                fn = (lambda e, pr=pr, ic=ic, slot=slot2: e.matmul(
                    ps[pr][:, 512:1024],
                    sb["wb%d" % slot][:, ic * 128:(ic + 1) * 128],
                    sb["x1f"][:, ic * TO:(ic + 1) * TO],
                    start=(ic == 0), stop=(ic == DC - 1)))
                lastb = comp("tensor", fn, wts)
                wts = ()
            wslot_free[slot1] = lasta
            wslot_free[slot2] = lastb
            x1f_readers = lastb
            ca = comp("scalar", (lambda e, pr=pr, oc=oc: e.activation(
                out=sb["ea"][:], in_=ps[pr][:, 0:512], func=AF.Silu,
                scale=sb["s1"][:, oc:oc + 1])), [("ts", lasta), ("vs", ea_free)])
            cu = comp("vector", (lambda e, pr=pr, oc=oc: e.scalar_tensor_tensor(
                out=sb["A"][:, oc * TO:(oc + 1) * TO], in0=ps[pr][:, 512:1024],
                scalar=sb["s2"][:, oc:oc + 1], in1=sb["ea"][:],
                op0=ALU.mult, op1=ALU.mult)),
                [("ts", lastb), ("ss", ca), ("ts", A_readers if oc == 0 else 0)])
            ea_free = cu
            pfb[pr + "a"] = [("ss", ca)]
            pfb[pr + "b"] = [("vs", cu)]
            uA_done = cu

        def h_stage(src_sb, dst_sb, wname, srt, src_ready, dst_readers):
            halves = ["F2a", "F2b", "F0a", "F0b"]
            u_done = 0
            u_evs = []
            for oc in range(HC):
                bk = halves[oc % 4]
                slot, ld = wload(wname, oc, HC * 128, queue="gpsimd")
                wts = [("d_w%d" % slot, ld)] + ([src_ready] if oc == 0 else []) + pfb[bk]
                last = 0
                for ic in range(HC):
                    fn = (lambda e, bk=bk, ic=ic, slot=slot, src_sb=src_sb: e.matmul(
                        BANK[bk](),
                        sb["wb%d" % slot][:, ic * 128:(ic + 1) * 128],
                        sb[src_sb][:, ic * TO:(ic + 1) * TO],
                        start=(ic == 0), stop=(ic == HC - 1)))
                    last = comp("tensor", fn, wts)
                    wts = ()
                wslot_free[slot] = last
                cu = comp("scalar", (lambda e, bk=bk, oc=oc, srt=srt, dst_sb=dst_sb: e.activation(
                    out=sb[dst_sb][:, oc * TO:(oc + 1) * TO], in_=BANK[bk](),
                    func=AF.Gelu, scale=sb[srt][:, oc:oc + 1])),
                    [("ts", last), ("ts", dst_readers if oc == 0 else 0)])
                pfb[bk] = [("ss", cu)]
                u_done = cu
                u_evs.append(cu)
            return u_done, u_evs

        if tau == 0:
            while _ep_next[0] < NQT:
                emit_tail_chunks(1)
                emit_ep(_ep_next[0])
                _ep_next[0] += 1
        uB_done, _ = h_stage("A", "Bb", "h1tp", "s1h", ("vs", uA_done), B_readers)
        A_readers_save = cnt["ts"]
        if tau == 0:
            emit_tail_chunks(len(cast_tail))
        uA2_done, uA2_evs = h_stage("Bb", "A", "h2tp", "s2h", ("ss", uB_done), A_readers_save)
        B_readers = cnt["ts"]
        A_readers = 0
        if tau + 1 < NFT:
            next_x1f_evac = emit_x1cast(tau + 1)

        z_done = {}
        for oh in range(2):
            FT = ["F0a", "F0b", "F1a", "F1b"] if oh == 0 else ["F0a", "F0b", "F2a", "F2b"]
            wts_f = []
            for tb in range(4):
                wts_f += pfb[FT[tb]]
            lastmm = {tb: 0 for tb in range(4)}
            for ic in range(HC):
                slot, ld = w3load(oh * HC + ic)
                wts = [("d_wc%d" % slot, ld)]
                if oh == 0:
                    wts.append(("ss", uA2_evs[ic]))
                if ic == 0:
                    wts = wts + wts_f
                for tb in range(4):
                    fn = (lambda e, bk=FT[tb], ic=ic, slot=slot, tb=tb: e.matmul(
                        BANK[bk](),
                        sb["A"][:, ic * TO + tb * 128: ic * TO + tb * 128 + 128],
                        sb["wc%d" % slot][:, 0:512],
                        start=(ic == 0), stop=(ic == HC - 1)))
                    lastmm[tb] = comp("tensor", fn, wts)
                    wts = ()
                wc_free[slot] = lastmm[3]
            A_readers = lastmm[3]
            for tb in range(4):
                qt = 4 * tau + tb
                czm = comp("vector", (lambda e, bk=FT[tb], oh=oh: e.tensor_tensor(
                    out=BANK[bk](), in0=BANK[bk](),
                    in1=sb["s3b"][:, oh * 512:(oh + 1) * 512],
                    op=ALU.mult)), [("ts", lastmm[tb])])
                cz = comp("vector", (lambda e, bk=FT[tb], qt=qt, oh=oh: e.tensor_tensor(
                    out=sb["O"][:, qt * 1024 + oh * 512: qt * 1024 + (oh + 1) * 512],
                    in0=BANK[bk](),
                    in1=sb["O"][:, qt * 1024 + oh * 512: qt * 1024 + (oh + 1) * 512],
                    op=ALU.add)), [("vs", czm)])
                pfb[FT[tb]] = [("vs", cz)]
                z_done[(tb, oh)] = cz

        for tb in range(4):
            qt = 4 * tau + tb
            cb1 = comp("vector", (lambda e, qt=qt: e.bn_stats(
                sb["bst"][:, 0:6], sb["O"][:, qt * 1024: qt * 1024 + 512])),
                [("vs", z_done[(tb, 0)]), ("vs", z_done[(tb, 1)]),
                 ("ss", ep_s), ("vs", cnt["vs"])])
            cb2 = comp("vector", (lambda e, qt=qt: e.bn_stats(
                sb["bst"][:, 6:12], sb["O"][:, qt * 1024 + 512: qt * 1024 + 1024])),
                [("vs", cb1)])
            cag = comp("vector", (lambda e: e.bn_aggr(
                sb["bmv"][:, 0:2], sb["bst"][:, 0:12])), [("vs", cb2)])
            crs0 = comp("scalar", (lambda e: e.activation(
                out=sb["mm2"][:, 0:1], in_=sb["bmv"][:, 1:2], func=AF.Sqrt,
                bias=sb["epsv"][:, 0:1])), [("vs", cag), ("ss", ep_s)])
            crs = comp("vector", (lambda e: e.reciprocal(
                out=sb["tv"][:, 0:1], in_=sb["mm2"][:, 0:1])),
                [("ss", crs0), ("vs", cnt["vs"])])
            cx1 = comp("vector", (lambda e, qt=qt: e.tensor_scalar(
                out=sb["O"][:, qt * 1024:(qt + 1) * 1024],
                in0=sb["O"][:, qt * 1024:(qt + 1) * 1024],
                scalar1=sb["bmv"][:, 0:1], scalar2=sb["tv"][:, 0:1],
                op0=ALU.subtract, op1=ALU.mult)), [("vs", crs)])
            ep_s = crs0
            out_dmas = dma("gpsimd", (lambda e, qt=qt: e.dma_start(
                out=dt["out"].rearrange("(t p) d -> p t d", p=128)[:, qt:qt + 1, :],
                in_=sb["O"][:, qt * 1024:(qt + 1) * 1024].rearrange(
                    "p (t d) -> p t d", d=1024))),
                "d_out", waits=[("vs", cx1)])

    wait("sync", [("d_out", dcnt["d_out"])])

    with es_top:
      with es_wt:
        with es_b:
            with nc.Block() as block:
                @block.sync
                def _(e):
                    for f in streams["sync"]:
                        f(e)

                @block.tensor
                def _(e):
                    for f in streams["tensor"]:
                        f(e)

                @block.vector
                def _(e):
                    for f in streams["vector"]:
                        f(e)

                @block.scalar
                def _(e):
                    for f in streams["scalar"]:
                        f(e)

                @block.gpsimd
                def _(e):
                    for f in streams["gpsimd"]:
                        f(e)

    n_inst = sum(len(v) for v in streams.values())
    return nc, n_inst



def _pack_oc(wt, n_ic, n_oc, ocw=128):
    return np.ascontiguousarray(
        wt.reshape(n_ic, 128, n_oc, ocw).transpose(2, 0, 1, 3).reshape(n_oc * n_ic * 128, ocw))


def _pack_pc(wt, n_ic, n_oc):
    return np.ascontiguousarray(
        wt.reshape(n_ic, 128, n_oc, 128).transpose(2, 1, 0, 3).reshape(n_oc * 128, n_ic * 128))


def _stage(cfg: Cfg, inputs):
    D, DC, HC = cfg.D, cfg.DC, cfg.HC
    x = np.asarray(inputs["L0_attn_x"], dtype=np.float32)
    W = np.asarray(inputs["L0_attn_W"], dtype=np.float32)
    B = x.shape[0]
    shared = {}
    shared["wattT"] = np.ascontiguousarray(W.T)
    shared["wattK"] = np.ascontiguousarray(W[:1024, :])
    w1t = np.ascontiguousarray(np.asarray(inputs["L0_ffn_w1_w"], dtype=np.float32).T)
    w2t = np.ascontiguousarray(np.asarray(inputs["L0_ffn_w2_w"], dtype=np.float32).T)
    h1t = np.ascontiguousarray(np.asarray(inputs["L0_ffn_h1_w"], dtype=np.float32).T)
    h2t = np.ascontiguousarray(np.asarray(inputs["L0_ffn_h2_w"], dtype=np.float32).T)
    w3t = np.ascontiguousarray(np.asarray(inputs["L0_ffn_w3_w"], dtype=np.float32).T)
    shared["w1tp"] = _pack_pc(w1t, DC, HC)
    shared["w2tp"] = _pack_pc(w2t, DC, HC)
    shared["h1tp"] = _pack_pc(h1t, HC, HC)
    shared["h2tp"] = _pack_pc(h2t, HC, HC)
    shared["w3tp"] = _pack_oc(w3t, HC, 2, 512)
    for nm, key in (("s1", "L0_ffn_w1_s"), ("s2", "L0_ffn_w2_s"),
                    ("s1h", "L0_ffn_h1_s"), ("s2h", "L0_ffn_h2_s")):
        s = np.asarray(inputs[key], dtype=np.float32)
        shared[nm] = np.ascontiguousarray(s.reshape(HC, 128).T)
    shared["s3b"] = np.ascontiguousarray(
        np.tile(np.asarray(inputs["L0_ffn_w3_s"], dtype=np.float32)[None, :], (128, 1)))
    shared["uio"] = np.ascontiguousarray(
        (np.arange(1024, dtype=np.float32)[None, :]
         - np.arange(128, dtype=np.float32)[:, None]).astype(np.float16))
    shared["identb"] = np.eye(128, dtype=ml_dtypes.bfloat16)

    xfm_all = [np.ascontiguousarray(x[b].T) for b in range(B)]
    per_seq = cfg.n_cores // B
    in_maps = []
    for c in range(cfg.n_cores):
        seq = c // per_seq
        r = c % per_seq
        rows = [(cfg.cps * p + r) * 128 for p in range(cfg.NQT)]
        m = dict(shared)
        m["xfm"] = xfm_all[seq]
        m["xraw"] = x[seq]
        m["xq"] = np.ascontiguousarray(np.concatenate(
            [xfm_all[seq][:, q0:q0 + 128] for q0 in rows], axis=1))
        m["xtm"] = np.ascontiguousarray(np.concatenate(
            [x[seq][q0:q0 + 128, :] for q0 in rows], axis=0))
        th = np.zeros((128, cfg.NVIS), dtype=np.float32)
        v = 0
        for kt in range(cfg.NKT):
            for s in cfg.slots[kt]:
                th[:, v] = float(rows[s] - kt * 1024)
                v += 1
        m["theta"] = th
        in_maps.append(m)
    return in_maps


def run_cfg(cfg: Cfg, inputs, runner=None):
    in_maps = _stage(cfg, inputs)
    nc, n_inst = build_program(cfg)
    if runner is None:
        res = run_bass_kernel_spmd(nc, in_maps, core_ids=list(range(cfg.n_cores)))
        results = res.results
    else:
        results = runner(nc, in_maps)
    x = np.asarray(inputs["L0_attn_x"])
    B, S, D = x.shape
    out = np.empty((B, S, D), dtype=np.float32)
    per_seq = cfg.n_cores // B
    for c in range(cfg.n_cores):
        seq = c // per_seq
        r = c % per_seq
        res = results[c]["out"]
        for p in range(cfg.NQT):
            q0 = (cfg.cps * p + r) * 128
            out[seq, q0:q0 + 128, :] = res[p * 128:(p + 1) * 128]
    return out


def kernel(**inputs):
    return run_cfg(CFG_FULL, inputs)


# version 25
# speedup vs baseline: 1.0558x; 1.0147x over previous
import numpy as np
from contextlib import ExitStack
import ml_dtypes

import concourse.bass as bass
import concourse.mybir as mybir
from concourse.bass_utils import run_bass_kernel_spmd

F32 = mybir.dt.float32
BF16 = mybir.dt.bfloat16
AF = mybir.ActivationFunctionType
ALU = mybir.AluOpType
AX = mybir.AxisListType

NEG_BIG = -30000.0


class Cfg:
    def __init__(self, seq=8192, qpc=2048, n_cores=8):
        self.D = 1024
        self.DH = 4096
        self.DC = 8
        self.HC = 32
        self.seq = seq
        self.qpc = qpc
        self.NKT = seq // 1024
        self.NQT = qpc // 128
        self.TO = 512
        self.NFT = qpc // 512
        self.n_cores = n_cores
        self.cps = n_cores // 2
        self.G = 8 // self.cps
        self.slots = [[p for p in range(self.NQT) if p // self.G >= kt]
                      for kt in range(self.NKT)]
        self.NVIS = sum(len(s) for s in self.slots)


CFG_FULL = Cfg(8192, 2048, 8)
CFG_MINI = Cfg(4096, 2048, 4)


def build_program(cfg: Cfg):
    D, DH, DC, HC = cfg.D, cfg.DH, cfg.DC, cfg.HC
    NKT, NQT, TO, NFT = cfg.NKT, cfg.NQT, cfg.TO, cfg.NFT
    QPC, SEQ = cfg.qpc, cfg.seq
    NVIS = cfg.NVIS
    SLOTS = cfg.slots

    nc = bass.Bass("TRN2", target_bir_lowering=False, debug=False)

    dt = {}
    dt["xfm"] = nc.dram_tensor("xfm", [D, SEQ], F32, kind="ExternalInput").ap()
    dt["xq"] = nc.dram_tensor("xq", [D, QPC], F32, kind="ExternalInput").ap()
    dt["xtm"] = nc.dram_tensor("xtm", [QPC, D], F32, kind="ExternalInput").ap()
    dt["wattT"] = nc.dram_tensor("wattT", [D, 2 * D], F32, kind="ExternalInput").ap()
    dt["wattK"] = nc.dram_tensor("wattK", [D, D], F32, kind="ExternalInput").ap()
    dt["xraw"] = nc.dram_tensor("xraw", [SEQ, D], F32, kind="ExternalInput").ap()
    dt["w1tp"] = nc.dram_tensor("w1tp", [HC * 128, DC * 128], F32, kind="ExternalInput").ap()
    dt["w2tp"] = nc.dram_tensor("w2tp", [HC * 128, DC * 128], F32, kind="ExternalInput").ap()
    dt["h1tp"] = nc.dram_tensor("h1tp", [HC * 128, HC * 128], F32, kind="ExternalInput").ap()
    dt["h2tp"] = nc.dram_tensor("h2tp", [HC * 128, HC * 128], F32, kind="ExternalInput").ap()
    dt["w3tp"] = nc.dram_tensor("w3tp", [2 * HC * 128, 512], F32, kind="ExternalInput").ap()
    for s in ("s1", "s2", "s1h", "s2h"):
        dt[s] = nc.dram_tensor(s, [128, HC], F32, kind="ExternalInput").ap()
    dt["s3b"] = nc.dram_tensor("s3b", [128, D], F32, kind="ExternalInput").ap()
    dt["theta"] = nc.dram_tensor("theta", [128, NVIS], F32, kind="ExternalInput").ap()
    dt["uio"] = nc.dram_tensor("uio", [128, 1024], mybir.dt.float16, kind="ExternalInput").ap()
    dt["identb"] = nc.dram_tensor("identb", [128, 128], BF16, kind="ExternalInput").ap()
    dt["out"] = nc.dram_tensor("out", [QPC, D], F32, kind="ExternalOutput").ap()
    dt["w1s"] = nc.dram_tensor("w1s", [HC * 128, DC * 128], BF16, kind="Internal").ap()
    dt["w2s"] = nc.dram_tensor("w2s", [HC * 128, DC * 128], BF16, kind="Internal").ap()
    dt["w3s"] = nc.dram_tensor("w3s", [2 * HC * 128, 512], BF16, kind="Internal").ap()

    sem_names = ["ts", "vs", "ss", "gs",
                 "d_const", "g_xk0", "g_xk1", "g_q0", "g_q1", "d_xtm0", "d_xtm1", "d_out",
                 "g_wt", "g_wk", "g_xt0", "g_xt1", "g_cw1", "g_cw2", "g_cw3", "g_ch1", "g_ch2",
                 "d_w0", "d_w1", "d_w2", "d_w3",
                 "d_wc0", "d_wc1", "d_wc2"]
    es_top = ExitStack()
    SEM = {n: es_top.enter_context(nc.semaphore(n)) for n in sem_names}

    sb = {}

    def sbuf(name, shape, dtp, stack):
        sb[name] = stack.enter_context(nc.sbuf_tensor("sb_" + name, shape, dtp))
        return sb[name]

    sbuf("O", [128, NQT * 1024], F32, es_top)
    for s in ("s1", "s2", "s1h", "s2h"):
        sbuf(s, [128, HC], F32, es_top)
    sbuf("s3b", [128, D], F32, es_top)
    sbuf("identb", [128, 128], BF16, es_top)
    sbuf("lrun", [128, NQT], F32, es_top)
    for s in ("rh", "s1v", "s2v", "nmean", "mm2", "tv", "epsv"):
        sbuf(s, [128, 1], F32, es_top)
    sbuf("bst", [128, 12], F32, es_top)
    sbuf("bmv", [128, 2], F32, es_top)
    sbuf("late8", [128, 2048], F32, es_top)

    streams = {"sync": [], "tensor": [], "vector": [], "scalar": [], "gpsimd": []}
    cnt = {"ts": 0, "vs": 0, "ss": 0, "gs": 0}
    dcnt = {n: 0 for n in sem_names if n not in ("ts", "vs", "ss", "gs")}

    SELF = {"tensor": "ts", "vector": "vs", "scalar": "ss", "gpsimd": "gs"}

    def comp(eng, build_fn, waits=()):
        s = SELF[eng]
        cnt[s] += 1
        my = cnt[s]
        w = tuple((a, b) for (a, b) in waits if b > 0)

        def f(e, build_fn=build_fn, w=w, s=s):
            for sname, v in w:
                e.wait_ge(SEM[sname], v)
            build_fn(e).then_inc(SEM[s], 1)
        streams[eng].append(f)
        return my

    def dma(eng, build_fn, sem_name, waits=()):
        dcnt[sem_name] += 16
        my = dcnt[sem_name]
        w = tuple((a, b) for (a, b) in waits if b > 0)

        def f(e, build_fn=build_fn, w=w, sem_name=sem_name):
            for sname, v in w:
                e.wait_ge(SEM[sname], v)
            build_fn(e).then_inc(SEM[sem_name], 16)
        streams[eng].append(f)
        return my

    def wait(eng, waits):
        w = tuple((a, b) for (a, b) in waits if b > 0)

        def f(e, w=w):
            for sname, v in w:
                e.wait_ge(SEM[sname], v)
        streams[eng].append(f)

    es_wt = ExitStack()
    sbuf("WTv", [128, DC * 1024], BF16, es_wt)
    es_a = ExitStack()
    sbuf("QP", [128, DC * QPC], BF16, es_a)
    sbuf("xk0", [128, DC * 1024], BF16, es_a)
    sbuf("XT0", [128, DC * 1024], BF16, es_a)
    sbuf("P0", [128, 1024], BF16, es_a)
    sbuf("P1", [128, 1024], BF16, es_a)
    sbuf("PT0", [128, 1024], BF16, es_a)
    sbuf("PT1", [128, 1024], BF16, es_a)
    sbuf("uio", [128, 1024], mybir.dt.float16, es_a)
    sbuf("theta", [128, NVIS], F32, es_a)
    sbuf("msk", [128, 1024], BF16, es_a)
    sbuf("negm", [128, NQT], F32, es_a)
    sbuf("mpos", [128, NQT], F32, es_a)
    sbuf("alpha", [128, 4], F32, es_a)
    sbuf("lpart", [128, 4], F32, es_a)
    sbuf("nmp", [128, 1], F32, es_a)

    for name in ("uio", "theta", "s1", "s2", "s1h", "s2h", "s3b", "identb"):
        dma("sync", (lambda e, n=name: e.dma_start(out=sb[n][:], in_=dt[n])), "d_const")
    C_CONST = dcnt["d_const"]

    comp("vector", lambda e: e.memset(sb["epsv"][:], 1e-5))
    es_q = ExitStack()
    sbuf("Qq0", [128, DC * 512], BF16, es_q)
    sbuf("Qq1", [128, DC * 512], BF16, es_q)
    sbuf("WK", [128, DC * D], BF16, es_q)
    ps = {}
    ps["S0"] = es_a.enter_context(nc.psum_tensor("S0", [128, 1024], F32))
    ps["S1"] = es_a.enter_context(nc.psum_tensor("S1", [128, 1024], F32))
    ps["PV"] = es_a.enter_context(nc.psum_tensor("PV", [128, 1024], F32))
    ps["PT0"] = es_a.enter_context(nc.psum_tensor("PT0p", [128, 1024], BF16))
    ps["PT1"] = es_a.enter_context(nc.psum_tensor("PT1p", [128, 1024], BF16))

    dma("gpsimd", (lambda e: e.dma_start(
        out=sb["WK"][:].rearrange("p (dc c) -> p dc c", c=D),
        in_=dt["wattK"].rearrange("(dc p) c -> p dc c", p=128))), "g_wk")
    dma("gpsimd", (lambda e: e.dma_start(
        out=sb["Qq0"][:].rearrange("p (ic c) -> p ic c", c=512),
        in_=dt["xq"][:, 0:512].rearrange("(ic p) c -> p ic c", p=128))), "g_q0")
    G_WK = dcnt["g_wk"]

    cast_chunks = []

    def cast_rows(src, dst, rows, cols, sem):
        step = max(128, (2 * 1024 * 1024) // (cols * 4))
        n = 0
        for r0 in range(0, rows, step):
            r1 = min(rows, r0 + step)
            cast_chunks.append((src, dst, r0, r1, sem))
            n += 16
        return n

    G_CW1 = cast_rows("w1tp", "w1s", HC * 128, DC * 128, "g_cw1")
    G_CW2 = cast_rows("w2tp", "w2s", HC * 128, DC * 128, "g_cw2")
    G_CW3 = cast_rows("w3tp", "w3s", 2 * HC * 128, 512, "g_cw3")
    cast_tail = []
    CAST_GATE = {"w1s": ("g_cw1", G_CW1), "w2s": ("g_cw2", G_CW2)}
    _cast_i = [0]
    _tail_i = [0]

    def _emit_chunk(tup):
        s_, d_, r0, r1, sem = tup
        dma("gpsimd", (lambda e, r0=r0, r1=r1, s_=s_, d_=d_: e.dma_start(
            out=dt[d_][r0:r1, :], in_=dt[s_][r0:r1, :])), sem)

    def emit_cast_chunks(n):
        bound = min(len(cast_chunks), _cast_i[0] + n)
        while _cast_i[0] < bound:
            _emit_chunk(cast_chunks[_cast_i[0]])
            _cast_i[0] += 1

    def emit_tail_chunks(n):
        bound = min(len(cast_tail), _tail_i[0] + n)
        while _tail_i[0] < bound:
            _emit_chunk(cast_tail[_tail_i[0]])
            _tail_i[0] += 1

    last_pv_mm = 0
    pair_free = {"S0": [], "S1": [], "PV": []}
    pt_free = {0: 0, 1: 0}
    p_free = {0: 0, 1: 0}
    xk_war = {0: 0, 1: 0}
    xt_war = {0: 0, 1: 0}
    xk_done = {}
    xt_done = {}

    vS, vMadd, vNegm, vExp, vAlpha, vPT, vPTe, vPV, vO, vL = ({} for _ in range(10))
    _vctr = [0]
    _vmap = {}

    def vid(kt, qi):
        key = (kt, qi)
        if key not in _vmap:
            _vmap[key] = _vctr[0]
            _vctr[0] += 1
        return _vmap[key]

    def xk_load_cast(kt, qp_gate=0):
        buf = "xk%d" % (kt % 2)
        wts = []
        if xk_war[kt % 2]:
            wts.append(("ts", xk_war[kt % 2]))
        if qp_gate:
            wts.append(("ts", qp_gate))
        xk_done[kt] = dma("gpsimd", (lambda e, kt=kt, buf=buf: e.dma_start(
            out=sb[buf][:].rearrange("p (ic c) -> p ic c", c=1024),
            in_=dt["xfm"][:, kt * 1024:(kt + 1) * 1024].rearrange(
                "(ic p) c -> p ic c", p=128))),
            "g_xk%d" % (kt % 2), waits=wts)

    def xt_load_cast(kt, qp_gate=0):
        buf = "XT%d" % (kt % 2)
        wts = []
        if xt_war[kt % 2]:
            wts.append(("ts", xt_war[kt % 2]))
        if qp_gate:
            wts.append(("ts", qp_gate))
        xt_done[kt] = dma("gpsimd", (lambda e, kt=kt, buf=buf: e.dma_start(
            out=sb[buf][:].rearrange("p (kc c) -> p kc c", c=1024),
            in_=dt["xraw"][kt * 1024:(kt + 1) * 1024, :].rearrange(
                "(kc p) c -> p kc c", p=128))),
            "g_xt%d" % (kt % 2), waits=wts)

    def emit_qprime():
        halves = [("S0", 0), ("S0", 512), ("S1", 0), ("S1", 512)]
        hfree = [[], [], [], []]
        bank_last = {"S0": 0, "S1": 0}
        q_last_mm = {}
        ev = 0
        for q in range(QPC // 512):
            qb = "Qq%d" % (q % 2)
            if q >= 2:
                dma("gpsimd", (lambda e, q=q, qb=qb: e.dma_start(
                    out=sb[qb][:].rearrange("p (ic c) -> p ic c", c=512),
                    in_=dt["xq"][:, q * 512:(q + 1) * 512].rearrange(
                        "(ic p) c -> p ic c", p=128))),
                    "g_q%d" % (q % 2), waits=[("ts", q_last_mm[q - 2])])
            for ib in range(DC):
                hi = (q * DC + ib) % 4
                pr, off = halves[hi]
                wts0 = list(hfree[hi])
                last = 0
                for dc in range(DC):
                    wts = wts0
                    wts0 = []
                    if ib == 0 and dc == 0:
                        wts = wts + [("g_q%d" % (q % 2), 16 * (q // 2 + 1))]
                        if q == 0:
                            wts = wts + [("g_wk", G_WK)]
                    fn = (lambda e, pr=pr, off=off, dc=dc, ib=ib, qb=qb: e.matmul(
                        ps[pr][:, off:off + 512],
                        sb["WK"][:, dc * D + ib * 128: dc * D + ib * 128 + 128],
                        sb[qb][:, dc * 512:(dc + 1) * 512],
                        start=(dc == 0), stop=(dc == DC - 1)))
                    last = comp("tensor", fn, wts)
                ev = comp("scalar", (lambda e, pr=pr, off=off, ib=ib, q=q: e.copy(
                    out=sb["QP"][:, ib * QPC + q * 512: ib * QPC + (q + 1) * 512],
                    in_=ps[pr][:, off:off + 512])), [("ts", last)])
                hfree[hi] = [("ss", ev)]
                bank_last[pr] = ev
            q_last_mm[q] = cnt["ts"]
        pair_free["S0"] = [("ss", bank_last["S0"])]
        pair_free["S1"] = [("ss", bank_last["S1"])]
        return ev

    def emit_S(kt, qi, first_of_kt):
        v = vid(kt, qi)
        buf = "xk%d" % (kt % 2)
        pr = "S0" if qi % 2 == 0 else "S1"
        wts = list(pair_free[pr])
        if first_of_kt:
            wts.append(("g_xk%d" % (kt % 2), xk_done[kt]))
        if kt == 0 and qi == SLOTS[0][0]:
            wts += [("ss", QP_DONE), ("d_const", C_CONST)]
        lastmm = 0
        for ns in range(2):
            for ic in range(DC):
                fn = (lambda e, pr=pr, ns=ns, ic=ic, qi=qi, buf=buf: e.matmul(
                    ps[pr][:, ns * 512:(ns + 1) * 512],
                    sb["QP"][:, ic * QPC + qi * 128: ic * QPC + qi * 128 + 128],
                    sb[buf][:, ic * 1024 + ns * 512: ic * 1024 + (ns + 1) * 512],
                    start=(ic == 0), stop=(ic == DC - 1)))
                lastmm = comp("tensor", fn, wts)
                wts = ()
        vS[v] = lastmm
        xk_war[kt % 2] = lastmm
        pair_free[pr] = []

    def emit_mask(kt, qi):
        v = vid(kt, qi)
        pr = "S0" if qi % 2 == 0 else "S1"
        c1 = comp("vector", (lambda e, v=v: e.tensor_scalar(
            out=sb["msk"][:], in0=sb["uio"][:],
            scalar1=sb["theta"][:, v:v + 1], scalar2=NEG_BIG,
            op0=ALU.is_gt, op1=ALU.mult)),
            [("vs", cnt["vs"]), ("d_const", C_CONST if v == 0 else 0)])
        c2 = comp("vector", (lambda e, pr=pr: e.tensor_tensor(
            out=ps[pr][:, :], in0=ps[pr][:, :], in1=sb["msk"][:], op=ALU.add)),
            [("ts", vS[v]), ("vs", c1)])
        vMadd[v] = c2

    def emit_negm(kt, qi):
        v = vid(kt, qi)
        pr = "S0" if qi % 2 == 0 else "S1"
        c1 = comp("vector", (lambda e, pr=pr: e.reduce_max(
            sb["nmp"][:, 0:1], ps[pr][:, :], axis=AX.X, negate=True)),
            [("vs", vMadd[v])])
        if kt == 0:
            c2 = comp("vector", (lambda e, qi=qi: e.tensor_copy(
                out=sb["negm"][:, qi:qi + 1], in_=sb["nmp"][:, 0:1])), [("vs", c1)])
        else:
            c2 = comp("vector", (lambda e, qi=qi: e.tensor_tensor(
                out=sb["negm"][:, qi:qi + 1], in0=sb["negm"][:, qi:qi + 1],
                in1=sb["nmp"][:, 0:1], op=ALU.min)), [("vs", c1)])
        vNegm[v] = c2

    def emit_exp(kt, qi):
        v = vid(kt, qi)
        pr = "S0" if qi % 2 == 0 else "S1"
        par = qi % 2
        sl = v % 4
        wts = [("vs", vNegm[v]), ("vs", vL.get(v - 4, 0))]
        if kt > 0:
            ca = comp("scalar", (lambda e, qi=qi, sl=sl: e.activation(
                out=sb["alpha"][:, sl:sl + 1], in_=sb["mpos"][:, qi:qi + 1],
                func=AF.Exp, bias=sb["negm"][:, qi:qi + 1])), wts)
            vAlpha[v] = ca
            wts = [("ss", ca)]
        else:
            vAlpha[v] = 0
        ce = comp("scalar", (lambda e, pr=pr, par=par, qi=qi, sl=sl: e.activation(
            out=sb["P%d" % par][:], in_=ps[pr][:, :], func=AF.Exp,
            bias=sb["negm"][:, qi:qi + 1], accum_out=sb["lpart"][:, sl:sl + 1])),
            [("ts", p_free[par])] + wts)
        cm = comp("scalar", (lambda e, qi=qi: e.activation(
            out=sb["mpos"][:, qi:qi + 1], in_=sb["negm"][:, qi:qi + 1],
            func=AF.Copy, scale=-1.0)), [("ss", ce)])
        vExp[v] = ce
        pair_free[pr] = [("ss", ce)]

    def emit_PT(kt, qi):
        v = vid(kt, qi)
        par = qi % 2
        wts = [("ss", vExp[v]), ("ss", pt_free[par])]
        last = 0
        for kc in range(DC):
            fn = (lambda e, par=par, kc=kc: e.transpose(
                ps["PT%d" % par][:, kc * 128:(kc + 1) * 128],
                sb["P%d" % par][:, kc * 128:(kc + 1) * 128],
                sb["identb"][:]))
            last = comp("tensor", fn, wts)
            wts = ()
        vPT[v] = last
        p_free[par] = last

    def emit_PTevac(kt, qi):
        v = vid(kt, qi)
        par = qi % 2
        c1 = comp("scalar", (lambda e, par=par: e.copy(
            out=sb["PT%d" % par][:], in_=ps["PT%d" % par][:, :])),
            [("ts", vPT[v])])
        vPTe[v] = c1
        pt_free[par] = c1

    def emit_PV(kt, qi, first_of_kt):
        nonlocal last_pv_mm
        v = vid(kt, qi)
        par = qi % 2
        buf = "XT%d" % (kt % 2)
        wts = [("ss", vPTe[v])] + pair_free["PV"]
        if first_of_kt:
            wts.append(("g_xt%d" % (kt % 2), xt_done[kt]))
        last = 0
        for ns in range(2):
            for kc in range(DC):
                fn = (lambda e, par=par, ns=ns, kc=kc, buf=buf: e.matmul(
                    ps["PV"][:, ns * 512:(ns + 1) * 512],
                    sb["PT%d" % par][:, kc * 128:(kc + 1) * 128],
                    sb[buf][:, kc * 1024 + ns * 512: kc * 1024 + (ns + 1) * 512],
                    start=(kc == 0), stop=(kc == DC - 1)))
                last = comp("tensor", fn, wts)
                wts = ()
        vPV[v] = last
        last_pv_mm = last
        xt_war[kt % 2] = last
        pair_free["PV"] = []

    def emit_O(kt, qi):
        v = vid(kt, qi)
        sl = v % 4
        if kt == 0:
            c1 = comp("vector", (lambda e, qi=qi: e.tensor_copy(
                out=sb["O"][:, qi * 1024:(qi + 1) * 1024], in_=ps["PV"][:, :])),
                [("ts", vPV[v])])
            cl = comp("vector", (lambda e, qi=qi, sl=sl: e.tensor_copy(
                out=sb["lrun"][:, qi:qi + 1], in_=sb["lpart"][:, sl:sl + 1])),
                [("ss", vExp[v])])
        else:
            c1 = comp("vector", (lambda e, qi=qi, sl=sl: e.scalar_tensor_tensor(
                out=sb["O"][:, qi * 1024:(qi + 1) * 1024],
                in0=sb["O"][:, qi * 1024:(qi + 1) * 1024],
                scalar=sb["alpha"][:, sl:sl + 1], in1=ps["PV"][:, :],
                op0=ALU.mult, op1=ALU.add)),
                [("ts", vPV[v]), ("ss", vAlpha[v])])
            cl = comp("vector", (lambda e, qi=qi, sl=sl: e.scalar_tensor_tensor(
                out=sb["lrun"][:, qi:qi + 1], in0=sb["lrun"][:, qi:qi + 1],
                scalar=sb["alpha"][:, sl:sl + 1], in1=sb["lpart"][:, sl:sl + 1],
                op0=ALU.mult, op1=ALU.add)), [("ss", vExp[v])])
        vO[v] = c1
        vL[v] = cl
        pair_free["PV"] = [("vs", c1)]

    n_chunks = len(cast_chunks)
    xk_load_cast(0)
    dma("gpsimd", (lambda e: e.dma_start(
        out=sb["Qq1"][:].rearrange("p (ic c) -> p ic c", c=512),
        in_=dt["xq"][:, 512:1024].rearrange("(ic p) c -> p ic c", p=128))), "g_q1")
    xt_load_cast(0)
    QP_DONE = emit_qprime()
    qp_last_mm = cnt["ts"]
    dma("gpsimd", (lambda e: e.dma_start(
        out=sb["WTv"][:].rearrange("p (ic c) -> p ic c", c=1024),
        in_=dt["wattT"][:, D:2 * D].rearrange("(ic p) c -> p ic c", p=128))), "g_wt")
    G_WT = dcnt["g_wt"]
    es_q.close()
    sbuf("xk1", [128, DC * 1024], BF16, es_a)
    sbuf("XT1", [128, DC * 1024], BF16, es_a)
    VLIST = [(kt, qi) for kt in range(NKT) for qi in SLOTS[kt]]
    NVV = len(VLIST)
    first_idx = {}
    for i, (kt_, qi_) in enumerate(VLIST):
        if kt_ not in first_idx:
            first_idx[kt_] = i
    vis_i = 0
    want_prev = 0
    for i in range(NVV + 2):
        if i < NVV:
            kt, qi = VLIST[i]
            if i == first_idx[kt] and kt + 1 < NKT:
                xk_load_cast(kt + 1, qp_gate=qp_last_mm if kt == 0 else 0)
            want = min(n_chunks, int(n_chunks * (vis_i + 1) / max(1, 0.7 * NVIS)))
            emit_cast_chunks(want - want_prev)
            want_prev = want
            vis_i += 1
            emit_S(kt, qi, first_of_kt=(i == first_idx[kt]))
            emit_mask(kt, qi)
            emit_negm(kt, qi)
        if 1 <= i <= NVV:
            emit_exp(*VLIST[i - 1])
            emit_PT(*VLIST[i - 1])
        if 2 <= i <= NVV + 1:
            ktp, qip = VLIST[i - 2]
            fokp = (i - 2 == first_idx[ktp])
            if fokp and ktp + 1 < NKT:
                xt_load_cast(ktp + 1, qp_gate=qp_last_mm if ktp == 0 else 0)
            emit_PTevac(ktp, qip)
            emit_PV(ktp, qip, first_of_kt=fokp)
            emit_O(ktp, qip)

    emit_cast_chunks(len(cast_chunks))

    INV_D = 1.0 / D
    xtm_slot_free = [0, 0]
    x1_ready_v = {}
    ep_s = 0
    ep_v = 0

    def emit_ep(qi):
        nonlocal ep_s, ep_v
        slot = qi % 2
        ld = dma("gpsimd", (lambda e, qi=qi, slot=slot: e.dma_start(
            out=sb["late8"][:, slot * 1024:(slot + 1) * 1024],
            in_=dt["xtm"][qi * 128:(qi + 1) * 128, :])),
            "d_xtm%d" % slot, waits=[("vs", xtm_slot_free[slot])])
        lastkt = qi // cfg.G
        par = 0
        cr0 = comp("vector", (lambda e, qi=qi: e.tensor_scalar_mul(
            out=sb["rh"][:, 0:1], in0=sb["lrun"][:, qi:qi + 1], scalar1=2.0)),
            [("vs", vL[vid(lastkt, qi)]), ("ss", ep_s), ("vs", cnt["vs"])])
        cr = comp("vector", (lambda e: e.reciprocal(
            out=sb["rh"][:, 0:1], in_=sb["rh"][:, 0:1])), [("vs", cr0)])
        cnorm = comp("vector", (lambda e, qi=qi, par=par: e.tensor_scalar_mul(
            out=sb["Ob%d" % par][:], in0=sb["O"][:, qi * 1024:(qi + 1) * 1024],
            scalar1=sb["rh"][:, 0:1])),
            [("vs", cr), ("vs", vO[vid(lastkt, qi)]),
             ("ts", ep_state["ob_free"][par]),
             ("ts", T_ts if qi == 0 else 0)])
        wts = [("vs", cnorm), ("ss", ep_state["pte_evac"])]
        last = 0
        for ic in range(DC):
            fn = (lambda e, par=par, ic=ic: e.transpose(
                ps["PTE"][:, ic * 128:(ic + 1) * 128],
                sb["Ob%d" % par][:, ic * 128:(ic + 1) * 128],
                sb["identb"][:]))
            last = comp("tensor", fn, wts)
            wts = ()
        ep_state["ob_free"][par] = last
        pev = comp("scalar", (lambda e, par=par: e.copy(
            out=sb["Ofm%d" % par][:], in_=ps["PTE"][:, :])),
            [("ts", last), ("ts", ep_state["ofm_readers"][par])])
        ep_state["pte_evac"] = pev
        pr = ["F0", "F1", "F2"][rot["i"] % 3]
        rot["i"] += 1
        wts = [("ss", pev)] + pfb[pr + "a"] + pfb[pr + "b"]
        if qi == 0:
            wts.append(("g_wt", G_WT))
        lastp = 0
        for ns in range(2):
            for ic in range(DC):
                fn = (lambda e, pr=pr, par=par, ns=ns, ic=ic: e.matmul(
                    ps[pr][:, ns * 512:(ns + 1) * 512],
                    sb["Ofm%d" % par][:, ic * 128:(ic + 1) * 128],
                    sb["WTv"][:, ic * 1024 + ns * 512: ic * 1024 + (ns + 1) * 512],
                    start=(ic == 0), stop=(ic == DC - 1)))
                lastp = comp("tensor", fn, wts)
                wts = ()
        ep_state["ofm_readers"][par] = lastp
        cy = comp("vector", (lambda e, qi=qi, slot=slot, pr=pr: e.tensor_tensor(
            out=sb["O"][:, qi * 1024:(qi + 1) * 1024],
            in0=ps[pr][:, 0:1024],
            in1=sb["late8"][:, slot * 1024:(slot + 1) * 1024], op=ALU.add)),
            [("ts", lastp), ("d_xtm%d" % slot, ld), ("vs", cnorm)])
        pfb[pr + "a"] = [("vs", cy)]
        pfb[pr + "b"] = [("vs", cy)]
        xtm_slot_free[slot] = cy
        cb1 = comp("vector", (lambda e, qi=qi: e.bn_stats(
            sb["bst"][:, 0:6], sb["O"][:, qi * 1024: qi * 1024 + 512])),
            [("vs", cy), ("ss", ep_s), ("vs", cnt["vs"])])
        cb2 = comp("vector", (lambda e, qi=qi: e.bn_stats(
            sb["bst"][:, 6:12], sb["O"][:, qi * 1024 + 512: qi * 1024 + 1024])),
            [("vs", cb1)])
        cag = comp("vector", (lambda e: e.bn_aggr(
            sb["bmv"][:, 0:2], sb["bst"][:, 0:12])), [("vs", cb2)])
        crs0 = comp("scalar", (lambda e: e.activation(
            out=sb["mm2"][:, 0:1], in_=sb["bmv"][:, 1:2], func=AF.Sqrt,
            bias=sb["epsv"][:, 0:1])), [("vs", cag), ("ss", ep_s)])
        crs = comp("vector", (lambda e: e.reciprocal(
            out=sb["tv"][:, 0:1], in_=sb["mm2"][:, 0:1])),
            [("ss", crs0), ("vs", cnt["vs"])])
        cx1 = comp("vector", (lambda e, qi=qi: e.tensor_scalar(
            out=sb["O"][:, qi * 1024:(qi + 1) * 1024],
            in0=sb["O"][:, qi * 1024:(qi + 1) * 1024],
            scalar1=sb["bmv"][:, 0:1], scalar2=sb["tv"][:, 0:1],
            op0=ALU.subtract, op1=ALU.mult)), [("vs", crs)])
        ep_s = crs0
        ep_v = cx1
        x1_ready_v[qi] = cx1

    T_ts, T_vs, T_ss = cnt["ts"], cnt["vs"], cnt["ss"]
    SS_ATTN = cnt["ss"]
    es_a.close()

    es_b = ExitStack()
    sbuf("A", [128, HC * TO], BF16, es_b)
    sbuf("Bb", [128, HC * TO], BF16, es_b)
    sbuf("x1f", [128, DC * TO], BF16, es_b)
    NWB = 3
    for i in range(NWB):
        sbuf("wb%d" % i, [128, HC * 128], BF16, es_b)
    sbuf("wc0", [128, 1024], BF16, es_b)
    sbuf("wc1", [128, 1024], BF16, es_b)
    sbuf("wc2", [128, 1024], BF16, es_b)
    sbuf("xc0", [128, 1024], BF16, es_b)
    sbuf("xc1", [128, 1024], BF16, es_b)
    sbuf("ea", [128, TO], F32, es_b)
    sbuf("Ob0", [128, 1024], BF16, es_b)
    sbuf("Ofm0", [128, 1024], BF16, es_b)
    ps["F0"] = es_b.enter_context(nc.psum_tensor("F0", [128, 1024], F32))
    ps["F1"] = es_b.enter_context(nc.psum_tensor("F1", [128, 1024], F32))
    ps["F2"] = es_b.enter_context(nc.psum_tensor("F2", [128, 1024], F32))
    ps["PTF"] = es_b.enter_context(nc.psum_tensor("PTF", [128, 1024], BF16))
    ps["PTE"] = es_b.enter_context(nc.psum_tensor("PTE", [128, 1024], BF16))
    rot = {"i": 0}
    ep_state = {"ob_free": {0: 0, 1: 0}, "ofm_readers": {0: 0, 1: 0},
                "pte_evac": T_ss}

    wslot_free = [0] * NWB
    ring = 0

    def wload(src_dram, oc, ncols, waits_extra=(), queue="sync"):
        nonlocal ring
        slot = ring % NWB
        ring += 1
        sem = "d_w%d" % slot
        w = [("ts", wslot_free[slot])] + list(waits_extra)
        if src_dram in CAST_GATE:
            w.append(CAST_GATE[src_dram])
        my = dma(queue, (lambda e, src_dram=src_dram, oc=oc, ncols=ncols,
                         slot=slot: e.dma_start(
            out=sb["wb%d" % slot][:, 0:ncols],
            in_=dt[src_dram][oc * 128:(oc + 1) * 128, 0:ncols])), sem, waits=w)
        return slot, my

    wc_free = [0, 0, 0]
    wc_ring = [0]

    def w3load(j):
        slot = wc_ring[0] % 3
        wc_ring[0] += 1
        sem = "d_wc%d" % slot
        w = [("ts", wc_free[slot]), ("g_cw3", G_CW3)]
        my = dma("sync", (lambda e, j=j, slot=slot: e.dma_start(
            out=sb["wc%d" % slot][:, 0:512],
            in_=dt["w3s"][j * 128:(j + 1) * 128, :])), sem, waits=w)
        return slot, my

    pfb = {k: [("vs", T_vs), ("ss", T_ss)]
           for k in ("F0a", "F0b", "F1a", "F1b", "F2a", "F2b", "PTF")}
    BANK = {
        "F0a": lambda: ps["F0"][:, 0:512], "F0b": lambda: ps["F0"][:, 512:1024],
        "F1a": lambda: ps["F1"][:, 0:512], "F1b": lambda: ps["F1"][:, 512:1024],
        "F2a": lambda: ps["F2"][:, 0:512], "F2b": lambda: ps["F2"][:, 512:1024],
    }

    x1f_readers = 0
    A_readers = 0
    B_readers = 0
    ea_free = 0
    za_free = 0
    out_dmas = 0

    def emit_x1cast(tau):
        ev = 0
        for tb in range(4):
            xcn = "xc0" if tb % 2 == 0 else "xc1"
            qt = 4 * tau + tb
            cv = comp("vector", (lambda e, qt=qt, xcn=xcn: e.tensor_copy(
                out=sb[xcn][:], in_=sb["O"][:, qt * 1024:(qt + 1) * 1024])),
                [("ts", p_free.get("xc" + str(tb % 2), 0)),
                 ("vs", x1_ready_v[qt]),
                 ("ts", T_ts if tau == 0 else 0)])
            wts = [("vs", cv), ("ss", ev if (tau > 0 or tb > 0) else SS_ATTN)]
            last = 0
            for dc in range(DC):
                fn = (lambda e, dc=dc, xcn=xcn: e.transpose(
                    ps["PTF"][:, dc * 128:(dc + 1) * 128],
                    sb[xcn][:, dc * 128:(dc + 1) * 128],
                    sb["identb"][:]))
                last = comp("tensor", fn, wts)
                wts = ()
            p_free["xc" + str(tb % 2)] = last
            ev = comp("scalar", (lambda e, tb=tb: e.copy(
                out=sb["x1f"][:].rearrange("p (n t) -> p n t", t=TO)[:, :, tb * 128:(tb + 1) * 128],
                in_=ps["PTF"][:].rearrange("p (n t) -> p n t", t=128))),
                [("ts", last), ("ts", x1f_readers)])
        return ev

    for qi in range(4):
        emit_ep(qi)
    wait("sync", [("ts", T_ts), ("vs", T_vs), ("ss", T_ss)])
    next_x1f_evac = emit_x1cast(0)
    _ep_next = [4]
    for tau in range(NFT):
        x1f_evac = next_x1f_evac
        uA_done = 0
        for oc in range(HC):
            pr = ["F0", "F1", "F2"][rot["i"] % 3]
            rot["i"] += 1
            slot1, l1 = wload("w1s", oc, DC * 128)
            slot2, l2 = wload("w2s", oc, DC * 128, queue="gpsimd")
            if tau == 0:
                emit_tail_chunks(1)
                want_ep = 4 + ((oc + 1) * (NQT - 4)) // HC
                while _ep_next[0] < min(NQT, want_ep):
                    emit_tail_chunks(1)
                    emit_ep(_ep_next[0])
                    _ep_next[0] += 1
            wts = [("d_w%d" % slot1, l1), ("ss", x1f_evac)] + pfb[pr + "a"] + pfb[pr + "b"]
            lasta = lastb = 0
            for ic in range(DC):
                fn = (lambda e, pr=pr, ic=ic, slot=slot1: e.matmul(
                    ps[pr][:, 0:512],
                    sb["wb%d" % slot][:, ic * 128:(ic + 1) * 128],
                    sb["x1f"][:, ic * TO:(ic + 1) * TO],
                    start=(ic == 0), stop=(ic == DC - 1)))
                lasta = comp("tensor", fn, wts)
                wts = ()
            wts = [("d_w%d" % slot2, l2)]
            for ic in range(DC):
                fn = (lambda e, pr=pr, ic=ic, slot=slot2: e.matmul(
                    ps[pr][:, 512:1024],
                    sb["wb%d" % slot][:, ic * 128:(ic + 1) * 128],
                    sb["x1f"][:, ic * TO:(ic + 1) * TO],
                    start=(ic == 0), stop=(ic == DC - 1)))
                lastb = comp("tensor", fn, wts)
                wts = ()
            wslot_free[slot1] = lasta
            wslot_free[slot2] = lastb
            x1f_readers = lastb
            ca = comp("scalar", (lambda e, pr=pr, oc=oc: e.activation(
                out=sb["ea"][:], in_=ps[pr][:, 0:512], func=AF.Silu,
                scale=sb["s1"][:, oc:oc + 1])), [("ts", lasta), ("vs", ea_free)])
            cu = comp("vector", (lambda e, pr=pr, oc=oc: e.scalar_tensor_tensor(
                out=sb["A"][:, oc * TO:(oc + 1) * TO], in0=ps[pr][:, 512:1024],
                scalar=sb["s2"][:, oc:oc + 1], in1=sb["ea"][:],
                op0=ALU.mult, op1=ALU.mult)),
                [("ts", lastb), ("ss", ca), ("ts", A_readers if oc == 0 else 0)])
            ea_free = cu
            pfb[pr + "a"] = [("ss", ca)]
            pfb[pr + "b"] = [("vs", cu)]
            uA_done = cu

        def h_stage(src_sb, dst_sb, wname, srt, src_ready, dst_readers):
            halves = ["F2a", "F2b", "F0a", "F0b"]
            u_done = 0
            u_evs = []
            for oc in range(HC):
                bk = halves[oc % 4]
                slot, ld = wload(wname, oc, HC * 128, queue="gpsimd")
                wts = [("d_w%d" % slot, ld)] + ([src_ready] if oc == 0 else []) + pfb[bk]
                last = 0
                for ic in range(HC):
                    fn = (lambda e, bk=bk, ic=ic, slot=slot, src_sb=src_sb: e.matmul(
                        BANK[bk](),
                        sb["wb%d" % slot][:, ic * 128:(ic + 1) * 128],
                        sb[src_sb][:, ic * TO:(ic + 1) * TO],
                        start=(ic == 0), stop=(ic == HC - 1)))
                    last = comp("tensor", fn, wts)
                    wts = ()
                wslot_free[slot] = last
                cu = comp("scalar", (lambda e, bk=bk, oc=oc, srt=srt, dst_sb=dst_sb: e.activation(
                    out=sb[dst_sb][:, oc * TO:(oc + 1) * TO], in_=BANK[bk](),
                    func=AF.Gelu, scale=sb[srt][:, oc:oc + 1])),
                    [("ts", last), ("ts", dst_readers if oc == 0 else 0)])
                pfb[bk] = [("ss", cu)]
                u_done = cu
                u_evs.append(cu)
            return u_done, u_evs

        if tau == 0:
            while _ep_next[0] < NQT:
                emit_tail_chunks(1)
                emit_ep(_ep_next[0])
                _ep_next[0] += 1
        uB_done, _ = h_stage("A", "Bb", "h1tp", "s1h", ("vs", uA_done), B_readers)
        A_readers_save = cnt["ts"]
        if tau == 0:
            emit_tail_chunks(len(cast_tail))
        uA2_done, uA2_evs = h_stage("Bb", "A", "h2tp", "s2h", ("ss", uB_done), A_readers_save)
        B_readers = cnt["ts"]
        A_readers = 0
        if tau + 1 < NFT:
            next_x1f_evac = emit_x1cast(tau + 1)

        z_done = {}
        for oh in range(2):
            FT = ["F0a", "F0b", "F1a", "F1b"] if oh == 0 else ["F0a", "F0b", "F2a", "F2b"]
            wts_f = []
            for tb in range(4):
                wts_f += pfb[FT[tb]]
            lastmm = {tb: 0 for tb in range(4)}
            for ic in range(HC):
                slot, ld = w3load(oh * HC + ic)
                wts = [("d_wc%d" % slot, ld)]
                if oh == 0:
                    wts.append(("ss", uA2_evs[ic]))
                if ic == 0:
                    wts = wts + wts_f
                for tb in range(4):
                    fn = (lambda e, bk=FT[tb], ic=ic, slot=slot, tb=tb: e.matmul(
                        BANK[bk](),
                        sb["A"][:, ic * TO + tb * 128: ic * TO + tb * 128 + 128],
                        sb["wc%d" % slot][:, 0:512],
                        start=(ic == 0), stop=(ic == HC - 1)))
                    lastmm[tb] = comp("tensor", fn, wts)
                    wts = ()
                wc_free[slot] = lastmm[3]
            A_readers = lastmm[3]
            for tb in range(4):
                qt = 4 * tau + tb
                czm = comp("vector", (lambda e, bk=FT[tb], oh=oh: e.tensor_tensor(
                    out=BANK[bk](), in0=BANK[bk](),
                    in1=sb["s3b"][:, oh * 512:(oh + 1) * 512],
                    op=ALU.mult)), [("ts", lastmm[tb])])
                cz = comp("vector", (lambda e, bk=FT[tb], qt=qt, oh=oh: e.tensor_tensor(
                    out=sb["O"][:, qt * 1024 + oh * 512: qt * 1024 + (oh + 1) * 512],
                    in0=BANK[bk](),
                    in1=sb["O"][:, qt * 1024 + oh * 512: qt * 1024 + (oh + 1) * 512],
                    op=ALU.add)), [("vs", czm)])
                pfb[FT[tb]] = [("vs", cz)]
                z_done[(tb, oh)] = cz

        for tb in range(4):
            qt = 4 * tau + tb
            cb1 = comp("vector", (lambda e, qt=qt: e.bn_stats(
                sb["bst"][:, 0:6], sb["O"][:, qt * 1024: qt * 1024 + 512])),
                [("vs", z_done[(tb, 0)]), ("vs", z_done[(tb, 1)]),
                 ("ss", ep_s), ("vs", cnt["vs"])])
            cb2 = comp("vector", (lambda e, qt=qt: e.bn_stats(
                sb["bst"][:, 6:12], sb["O"][:, qt * 1024 + 512: qt * 1024 + 1024])),
                [("vs", cb1)])
            cag = comp("vector", (lambda e: e.bn_aggr(
                sb["bmv"][:, 0:2], sb["bst"][:, 0:12])), [("vs", cb2)])
            crs0 = comp("scalar", (lambda e: e.activation(
                out=sb["mm2"][:, 0:1], in_=sb["bmv"][:, 1:2], func=AF.Sqrt,
                bias=sb["epsv"][:, 0:1])), [("vs", cag), ("ss", ep_s)])
            crs = comp("vector", (lambda e: e.reciprocal(
                out=sb["tv"][:, 0:1], in_=sb["mm2"][:, 0:1])),
                [("ss", crs0), ("vs", cnt["vs"])])
            cx1 = comp("vector", (lambda e, qt=qt: e.tensor_scalar(
                out=sb["O"][:, qt * 1024:(qt + 1) * 1024],
                in0=sb["O"][:, qt * 1024:(qt + 1) * 1024],
                scalar1=sb["bmv"][:, 0:1], scalar2=sb["tv"][:, 0:1],
                op0=ALU.subtract, op1=ALU.mult)), [("vs", crs)])
            ep_s = crs0
            out_dmas = dma("gpsimd", (lambda e, qt=qt: e.dma_start(
                out=dt["out"].rearrange("(t p) d -> p t d", p=128)[:, qt:qt + 1, :],
                in_=sb["O"][:, qt * 1024:(qt + 1) * 1024].rearrange(
                    "p (t d) -> p t d", d=1024))),
                "d_out", waits=[("vs", cx1)])

    wait("sync", [("d_out", dcnt["d_out"])])

    with es_top:
      with es_wt:
        with es_b:
            with nc.Block() as block:
                @block.sync
                def _(e):
                    for f in streams["sync"]:
                        f(e)

                @block.tensor
                def _(e):
                    for f in streams["tensor"]:
                        f(e)

                @block.vector
                def _(e):
                    for f in streams["vector"]:
                        f(e)

                @block.scalar
                def _(e):
                    for f in streams["scalar"]:
                        f(e)

                @block.gpsimd
                def _(e):
                    for f in streams["gpsimd"]:
                        f(e)

    n_inst = sum(len(v) for v in streams.values())
    return nc, n_inst



def _pack_oc(wt, n_ic, n_oc, ocw=128):
    return np.ascontiguousarray(
        wt.reshape(n_ic, 128, n_oc, ocw).transpose(2, 0, 1, 3).reshape(n_oc * n_ic * 128, ocw))


def _pack_pc(wt, n_ic, n_oc):
    return np.ascontiguousarray(
        wt.reshape(n_ic, 128, n_oc, 128).transpose(2, 1, 0, 3).reshape(n_oc * 128, n_ic * 128))


def _stage(cfg: Cfg, inputs):
    D, DC, HC = cfg.D, cfg.DC, cfg.HC
    x = np.asarray(inputs["L0_attn_x"], dtype=np.float32)
    W = np.asarray(inputs["L0_attn_W"], dtype=np.float32)
    B = x.shape[0]
    shared = {}
    shared["wattT"] = np.ascontiguousarray(W.T)
    shared["wattK"] = np.ascontiguousarray(W[:1024, :])
    w1t = np.ascontiguousarray(np.asarray(inputs["L0_ffn_w1_w"], dtype=np.float32).T)
    w2t = np.ascontiguousarray(np.asarray(inputs["L0_ffn_w2_w"], dtype=np.float32).T)
    h1t = np.ascontiguousarray(np.asarray(inputs["L0_ffn_h1_w"], dtype=np.float32).T)
    h2t = np.ascontiguousarray(np.asarray(inputs["L0_ffn_h2_w"], dtype=np.float32).T)
    w3t = np.ascontiguousarray(np.asarray(inputs["L0_ffn_w3_w"], dtype=np.float32).T)
    shared["w1tp"] = _pack_pc(w1t, DC, HC)
    shared["w2tp"] = _pack_pc(w2t, DC, HC)
    shared["h1tp"] = _pack_pc(h1t, HC, HC)
    shared["h2tp"] = _pack_pc(h2t, HC, HC)
    shared["w3tp"] = _pack_oc(w3t, HC, 2, 512)
    for nm, key in (("s1", "L0_ffn_w1_s"), ("s2", "L0_ffn_w2_s"),
                    ("s1h", "L0_ffn_h1_s"), ("s2h", "L0_ffn_h2_s")):
        s = np.asarray(inputs[key], dtype=np.float32)
        shared[nm] = np.ascontiguousarray(s.reshape(HC, 128).T)
    shared["s3b"] = np.ascontiguousarray(
        np.tile(np.asarray(inputs["L0_ffn_w3_s"], dtype=np.float32)[None, :], (128, 1)))
    shared["uio"] = np.ascontiguousarray(
        (np.arange(1024, dtype=np.float32)[None, :]
         - np.arange(128, dtype=np.float32)[:, None]).astype(np.float16))
    shared["identb"] = np.eye(128, dtype=ml_dtypes.bfloat16)

    xfm_all = [np.ascontiguousarray(x[b].T) for b in range(B)]
    per_seq = cfg.n_cores // B
    in_maps = []
    for c in range(cfg.n_cores):
        seq = c // per_seq
        r = c % per_seq
        rows = [(cfg.cps * p + r) * 128 for p in range(cfg.NQT)]
        m = dict(shared)
        m["xfm"] = xfm_all[seq]
        m["xraw"] = x[seq]
        m["xq"] = np.ascontiguousarray(np.concatenate(
            [xfm_all[seq][:, q0:q0 + 128] for q0 in rows], axis=1))
        m["xtm"] = np.ascontiguousarray(np.concatenate(
            [x[seq][q0:q0 + 128, :] for q0 in rows], axis=0))
        th = np.zeros((128, cfg.NVIS), dtype=np.float32)
        v = 0
        for kt in range(cfg.NKT):
            for s in cfg.slots[kt]:
                th[:, v] = float(rows[s] - kt * 1024)
                v += 1
        m["theta"] = th
        in_maps.append(m)
    return in_maps


def run_cfg(cfg: Cfg, inputs, runner=None):
    in_maps = _stage(cfg, inputs)
    nc, n_inst = build_program(cfg)
    if runner is None:
        res = run_bass_kernel_spmd(nc, in_maps, core_ids=list(range(cfg.n_cores)))
        results = res.results
    else:
        results = runner(nc, in_maps)
    x = np.asarray(inputs["L0_attn_x"])
    B, S, D = x.shape
    out = np.empty((B, S, D), dtype=np.float32)
    per_seq = cfg.n_cores // B
    for c in range(cfg.n_cores):
        seq = c // per_seq
        r = c % per_seq
        res = results[c]["out"]
        for p in range(cfg.NQT):
            q0 = (cfg.cps * p + r) * 128
            out[seq, q0:q0 + 128, :] = res[p * 128:(p + 1) * 128]
    return out


def kernel(**inputs):
    return run_cfg(CFG_FULL, inputs)
